# version 06
# speedup vs baseline: 1.6573x; 1.0693x over previous
import contextlib
import math
import numpy as np
import concourse.bass as bass
import concourse.mybir as mybir
from concourse.bass_utils import run_bass_kernel_spmd
from concourse.bass_core_rust import IndirectOffsetOnAxis

F32 = mybir.dt.float32
BF16 = mybir.dt.bfloat16
F16 = mybir.dt.float16
I32 = mybir.dt.int32
AF = mybir.ActivationFunctionType
ALU = mybir.AluOpType
AX = mybir.AxisListType

P = 128
T = 2048
D = 1024
NT = T // P
KC = D // P
H = 16
DH = 64
M = 256
NCORE = 8
LN16 = math.log(16.0)
TQ = 512


class Sem:

    def __init__(self, nc, name, stack):
        self.h = stack.enter_context(nc.semaphore(name))
        self.v = 0

    def inc(self, instr, n=1):
        instr.then_inc(self.h, n)
        self.v += n
        return self.v


class Prog:

    def __init__(self, nc, block, stack):
        self.nc = nc
        self.block = block
        self.stack = stack
        self._sems = {}
        self.psum_guard = {}
        self.buf_guard = {}

    def sem(self, name):
        if name not in self._sems:
            self._sems[name] = Sem(self.nc, name, self.stack)
        return self._sems[name]


def pipe(pr, name, jobs, psums, producer, consumer, t_entry=(), c_entry=(),
         consumer_engine="scalar", dma_emit=None, nstage=0, stage_buf=None):
    mm = pr.sem("mm")
    ev = pr.sem("ev_" + consumer_engine)
    dms = [pr.sem(f"dm{s}") for s in range(nstage)] if dma_emit else []
    nbuf = len(psums)
    mm_base, ev_base = mm.v, ev.v
    dm_base = [d.v for d in dms]

    @pr.block.tensor
    def _(t):
        for s, v in t_entry:
            t.wait_ge(s.h, v)
        for pm in psums:
            g = pr.psum_guard.get(id(pm))
            if g is not None:
                t.wait_ge(g[0].h, g[1])
        for j, job in enumerate(jobs):
            if j >= nbuf:
                t.wait_ge(ev.h, ev_base + j - nbuf + 1)
            mm.inc(producer(t, job, psums[j % nbuf]))

    sec = getattr(pr.block, consumer_engine)

    bg = pr.buf_guard.get(id(stage_buf)) if stage_buf is not None else None

    @sec
    def _(e):
        for s, v in c_entry:
            e.wait_ge(s.h, v)
        if bg is not None:
            for sm, vv in bg:
                e.wait_ge(sm.h, vv)
        for j, job in enumerate(jobs):
            e.wait_ge(mm.h, mm_base + j + 1)
            if dms and j >= nstage:
                sl = j % nstage
                e.wait_ge(dms[sl].h, dm_base[sl] + 16 * ((j - nstage) // nstage + 1))
            ev.inc(consumer(e, job, psums[j % nbuf], j % nstage if nstage else 0))

    if dma_emit:
        @pr.block.sync
        def _(sy):
            for j, job in enumerate(jobs):
                sl = j % nstage
                sy.wait_ge(ev.h, ev_base + j + 1)
                sy.wait_ge(dms[sl].h, dms[sl].v)
                dms[sl].inc(dma_emit(sy, job, sl), 16)

    for pm in psums:
        pr.psum_guard[id(pm)] = (ev, ev.v)
    if stage_buf is not None and dms:
        pr.buf_guard[id(stage_buf)] = [(d, d.v) for d in dms]
    return ev


def emit_ln(pr, x_view, tiles, cols, eps, out_emit=None, n=D, sd_mode=False,
            entry=(), apply=True):
    block = pr.block
    lnst = pr.sem("lnst")
    nt = len(tiles)

    @block.vector
    def _(v):
        for s_, vv in entry:
            v.wait_ge(s_.h, vv)
        last = None
        for j, i in enumerate(tiles):
            last = v.reduce_sum(cols["sum"][:, j:j + 1], x_view(i), AX.X)
        lnst.inc(last)

    @block.scalar
    def _(s):
        for s_, vv in entry:
            s.wait_ge(s_.h, vv)
        for j, i in enumerate(tiles):
            if j:
                s.wait_ge(lnst.h, lnst.v)
            lnst.inc(s.activation(cols["trash"][:], x_view(i), AF.Square,
                                  accum_out=cols["sq"][:, j:j + 1]))

    @block.vector
    def _(v):
        v.wait_ge(lnst.h, lnst.v)
        lnst.inc(v.tensor_scalar_mul(cols["mu"][:, 0:nt], cols["sum"][:, 0:nt],
                                     1.0 / n))
        v.wait_ge(lnst.h, lnst.v)
        v.tensor_tensor(cols["var"][:, 0:nt], cols["mu"][:, 0:nt],
                        cols["mu"][:, 0:nt], ALU.mult)
        lnst.inc(v.tensor_scalar(cols["sq"][:, 0:nt], cols["sq"][:, 0:nt],
                                 1.0 / n, None, ALU.mult))
        v.wait_ge(lnst.h, lnst.v)
        lnst.inc(v.tensor_sub(cols["var"][:, 0:nt], cols["sq"][:, 0:nt],
                              cols["var"][:, 0:nt]))

    if sd_mode:
        @block.scalar
        def _(s):
            s.wait_ge(lnst.h, lnst.v)
            lnst.inc(s.activation(cols["var"][:, 0:nt], cols["var"][:, 0:nt],
                                  AF.Sqrt, scale=float(n) / (n - 1)))

        @block.vector
        def _(v):
            v.wait_ge(lnst.h, lnst.v)
            lnst.inc(v.tensor_scalar(cols["var"][:, 0:nt], cols["var"][:, 0:nt],
                                     float(eps), None, ALU.add))

        @block.scalar
        def _(s):
            s.wait_ge(lnst.h, lnst.v)
            lnst.inc(s.activation(cols["rstd"][:, 0:nt], cols["var"][:, 0:nt],
                                  AF.Sqrt))
    else:
        @block.vector
        def _(v):
            v.wait_ge(lnst.h, lnst.v)
            lnst.inc(v.tensor_scalar(cols["var"][:, 0:nt], cols["var"][:, 0:nt],
                                     float(eps), None, ALU.add))

        @block.scalar
        def _(s):
            s.wait_ge(lnst.h, lnst.v)
            lnst.inc(s.activation(cols["rstd"][:, 0:nt], cols["var"][:, 0:nt],
                                  AF.Sqrt))

    @block.vector
    def _(v):
        v.wait_ge(lnst.h, lnst.v)
        lnst.inc(v.reciprocal(cols["rstd"][:, 0:nt], cols["rstd"][:, 0:nt]))
        v.wait_ge(lnst.h, lnst.v)
        lnst.inc(v.scalar_tensor_tensor(cols["nmr"][:, 0:nt], cols["mu"][:, 0:nt],
                                        -1.0, cols["rstd"][:, 0:nt],
                                        ALU.mult, ALU.mult))

    if apply:
        @block.scalar
        def _(s):
            s.wait_ge(lnst.h, lnst.v)
            last = None
            for j, i in enumerate(tiles):
                if out_emit is None:
                    last = s.activation(x_view(i), x_view(i), AF.Identity,
                                        bias=cols["nmr"][:, j:j + 1],
                                        scale=cols["rstd"][:, j:j + 1])
                else:
                    last = out_emit(s, i, cols["rstd"][:, j:j + 1],
                                    cols["nmr"][:, j:j + 1])
            lnst.inc(last)
    return lnst


def alloc_ln_cols(sb, nt=NT, n=D, tag=""):
    return dict(sum=sb(f"c_sum{tag}", [P, nt], F32),
                sq=sb(f"c_sq{tag}", [P, nt], F32),
                mu=sb(f"c_mu{tag}", [P, nt], F32),
                var=sb(f"c_var{tag}", [P, nt], F32),
                rstd=sb(f"c_rstd{tag}", [P, nt], F32),
                nmr=sb(f"c_nmr{tag}", [P, nt], F32),
                trash=sb(f"c_trash{tag}", [P, n], BF16))



def build_l0_pre():
    nc = bass.Bass("TRN2", target_bir_lowering=False, debug=False)
    x_d = nc.dram_tensor("x", [T, D], F32, kind="ExternalInput").ap()
    w_d = {nm: nc.dram_tensor(nm, [D, D], F32, kind="ExternalInput").ap()
           for nm in ("q_w", "k_w", "v_w")}
    b_d = {nm: nc.dram_tensor(nm, [1, D], F32, kind="ExternalInput").ap()
           for nm in ("q_b", "k_b", "v_b")}
    proj_d = nc.dram_tensor("proj", [M, DH], F32, kind="ExternalInput").ap()
    ident_d = nc.dram_tensor("ident", [P, P], F32, kind="ExternalInput").ap()

    ln1_d = nc.dram_tensor("ln1", [T, D], F32, kind="ExternalOutput").ap()
    qpt_d = nc.dram_tensor("qpt", [H * M, T], F32, kind="ExternalOutput").ap()
    kpt_d = nc.dram_tensor("kpt", [H * M, T], F32, kind="ExternalOutput").ap()
    kptok_d = nc.dram_tensor("kptok", [T, H * M], F32, kind="ExternalOutput").ap()
    vtok_d = nc.dram_tensor("vtok", [T, D], F32, kind="ExternalOutput").ap()

    with contextlib.ExitStack() as st_:
        sb = lambda *a: st_.enter_context(nc.sbuf_tensor(*a))
        ps = lambda *a: st_.enter_context(nc.psum_tensor(*a))
        qw_s = sb("qw_s", [P, KC, D], BF16)
        kw_s = sb("kw_s", [P, KC, D], BF16)
        vw_s = sb("vw_s", [P, KC, D], BF16)
        qb_row = sb("qb_row", [1, D], BF16)
        kb_row = sb("kb_row", [1, D], BF16)
        vb_row = sb("vb_row", [1, D], BF16)
        ident_s = sb("ident_s", [P, P], BF16)
        identf_s = sb("identf_s", [P, P], F32)
        proj_lo = sb("proj_lo", [P, DH], BF16)
        proj_hi = sb("proj_hi", [P, DH], BF16)
        projT_aug = sb("projT_aug", [DH + 1, M], BF16)
        ones_red = sb("ones_red", [P, 1], BF16)
        ones_bc = sb("ones_bc", [1, TQ], BF16)
        nln16 = sb("nln16", [P, 1], F32)
        x_s = sb("x_s", [P, NT, D], F32)
        lnT_s = sb("lnT_s", [P, KC, T], BF16)
        qt_f = sb("qt_f", [P, T], F32)
        kt_f = sb("kt_f", [P, T], F32)
        xs_aug = sb("xs_aug", [DH + 1, T], BF16)
        sq_bf = sb("sq_bf", [DH, T], BF16)
        map_st = sb("map_st", [P, 4, TQ], F32)
        ktok_st = sb("ktok_st", [P, 4, M], F32)
        v_st = sb("v_st", [P, 4, TQ], F32)
        cols = alloc_ln_cols(sb)

        p_tr0 = ps("p_tr0", [P, P], F32)
        p_tr1 = ps("p_tr1", [P, P], F32)
        p_trb = ps("p_trb", [P, P], BF16)
        p_mm0 = ps("p_mm0", [P, TQ], F32)
        p_mm1 = ps("p_mm1", [P, TQ], F32)
        p_corr = ps("p_corr", [1, TQ], F32)
        p_tok0 = ps("p_tok0", [P, M], F32)
        p_tok1 = ps("p_tok1", [P, M], F32)

        with nc.Block() as block:
            pr = Prog(nc, block, st_)
            ld = pr.sem("ld")
            st = pr.sem("st")
            feat = pr.sem("feat")
            fmm = pr.sem("fmm")

            @block.gpsimd
            def _(g):
                ld.inc(g.dma_start(out=x_s[:], in_=x_d.rearrange("(n p) d -> p n d", p=P)), 16)
                ld.inc(g.dma_start(out=qw_s[:], in_=w_d["q_w"].rearrange("(k p) d -> p k d", p=P)), 16)
                ld.inc(g.dma_start(out=kw_s[:], in_=w_d["k_w"].rearrange("(k p) d -> p k d", p=P)), 16)
                ld.inc(g.dma_start(out=vw_s[:], in_=w_d["v_w"].rearrange("(k p) d -> p k d", p=P)), 16)
                ld.inc(g.dma_start(out=qb_row[:], in_=b_d["q_b"]), 16)
                ld.inc(g.dma_start(out=kb_row[:], in_=b_d["k_b"]), 16)
                ld.inc(g.dma_start(out=vb_row[:], in_=b_d["v_b"]), 16)
                ld.inc(g.dma_start(out=ident_s[:], in_=ident_d), 16)
                ld.inc(g.dma_start(out=identf_s[:], in_=ident_d), 16)
                ld.inc(g.dma_start(out=proj_lo[:], in_=proj_d[0:P, :]), 16)
                ld.inc(g.dma_start(out=proj_hi[:], in_=proj_d[P:M, :]), 16)

            cst = pr.sem("cst")

            @block.vector
            def _(v):
                v.memset(ones_red[:], 1.0)
                v.memset(ones_bc[:], 1.0)
                v.memset(nln16[:], -LN16)
                cst.inc(v.memset(projT_aug[DH:DH + 1, :], 1.0))

            @block.vector
            def _(v):
                v.wait_ge(ld.h, ld.v)
            @block.scalar
            def _(s):
                s.wait_ge(ld.h, ld.v)
            lnd = emit_ln(pr, lambda i: x_s[:, i, :], list(range(NT)), cols, 1e-5)

            @block.sync
            def _(sy):
                sy.wait_ge(lnd.h, lnd.v)
                st.inc(sy.dma_start(out=ln1_d.rearrange("(n p) d -> p n d", p=P),
                                    in_=x_s[:]), 16)

            tr_ev = pipe(
                pr, "tr", [(i, k) for i in range(NT) for k in range(KC)],
                [p_tr0, p_tr1],
                lambda t, job, pm: t.transpose(
                    pm[:], x_s[:, job[0], job[1] * P:(job[1] + 1) * P], identf_s[:]),
                lambda e, job, pm, sl: e.tensor_copy(
                    lnT_s[:, job[1], job[0] * P:(job[0] + 1) * P], pm[:]),
                t_entry=[(lnd, lnd.v)], consumer_engine="vector")

            ptr_ev = pipe(
                pr, "ptr", [0, 1], [p_trb],
                lambda t, j, pm: t.transpose(pm[0:DH, :], (proj_lo if j == 0 else proj_hi)[:], ident_s[:]),
                lambda e, j, pm, sl: e.copy(projT_aug[0:DH, j * P:(j + 1) * P], pm[0:DH, :]),
                t_entry=[(ld, ld.v)])

            xs_readers = []
            for hp in range(KC):
                def qk_prod(t, job, pm, hp=hp):
                    side, tq = job
                    w_s = qw_s if side == 0 else kw_s
                    bb = qb_row if side == 0 else kb_row
                    for k in range(KC):
                        t.matmul(pm[:], w_s[:, k, hp * P:(hp + 1) * P],
                                 lnT_s[:, k, tq * TQ:(tq + 1) * TQ],
                                 start=(k == 0), stop=False)
                    return t.matmul(pm[:], bb[:, hp * P:(hp + 1) * P],
                                    ones_bc[:, 0:TQ], start=False, stop=True)

                def qk_cons(e, job, pm, sl, hp=hp):
                    side, tq = job
                    dst = qt_f if side == 0 else kt_f
                    return e.copy(dst[:, tq * TQ:(tq + 1) * TQ], pm[:])

                qk_entry = [(tr_ev, tr_ev.v), (feat, feat.v), (cst, cst.v)]
                qk_ev = pipe(pr, f"qk{hp}",
                             [(sd, tq) for sd in range(2) for tq in range(T // TQ)],
                             [p_mm0, p_mm1], qk_prod, qk_cons, t_entry=qk_entry)

                for hh in range(2):
                    head = 2 * hp + hh
                    base = DH * hh
                    for side in range(2):
                        src = qt_f if side == 0 else kt_f
                        outT = qpt_d if side == 0 else kpt_d
                        xs_guards = list(xs_readers)

                        @block.scalar
                        def _(s, src=src, base=base, v0=qk_ev.v,
                              xg=xs_guards):
                            s.wait_ge(qk_ev.h, v0)
                            for sm, vv in xg:
                                s.wait_ge(sm.h, vv)
                            s.activation(xs_aug[0:DH, :], src[base:base + DH, :],
                                         AF.Identity, scale=0.125)
                            feat.inc(s.activation(sq_bf[:], src[base:base + DH, :],
                                                  AF.Square, scale=0.125))

                        corr_ev = pipe(
                            pr, f"corr{head}_{side}", list(range(T // TQ)),
                            [p_corr],
                            lambda t, tq, pm: t.matmul(
                                pm[:], ones_red[0:DH, :],
                                sq_bf[:, tq * TQ:(tq + 1) * TQ],
                                start=True, stop=True),
                            lambda e, tq, pm, sl: e.activation(
                                xs_aug[DH:DH + 1, tq * TQ:(tq + 1) * TQ], pm[:],
                                AF.Identity, scale=-0.5),
                            t_entry=[(feat, feat.v), (cst, cst.v)])

                        def map_prod(t, job, pm):
                            mf, tq = job
                            return t.matmul(pm[:], projT_aug[:, mf * P:(mf + 1) * P],
                                            xs_aug[:, tq * TQ:(tq + 1) * TQ],
                                            start=True, stop=True)

                        def map_cons(e, job, pm, sl):
                            return e.activation(map_st[:, sl, :], pm[:], AF.Exp,
                                                bias=nln16[:])

                        def map_dma(sy, job, sl, head=head, outT=outT):
                            mf, tq = job
                            return sy.dma_start(
                                out=outT[head * M + mf * P:head * M + (mf + 1) * P,
                                         tq * TQ:(tq + 1) * TQ],
                                in_=map_st[:, sl, :])

                        pipe(pr, f"map{head}_{side}",
                             [(mf, tq) for mf in range(2) for tq in range(T // TQ)],
                             [p_mm0, p_mm1], map_prod, map_cons,
                             t_entry=[(feat, feat.v), (corr_ev, corr_ev.v),
                                      (cst, cst.v)],
                             dma_emit=map_dma, nstage=4, stage_buf=map_st)
                        mm_s = pr.sem("mm")
                        xs_readers = [(mm_s, mm_s.v)]

                        if side == 1:
                            def tok_prod(t, i, pm):
                                return t.matmul(pm[:], xs_aug[:, i * P:(i + 1) * P],
                                                projT_aug[:], start=True, stop=True)

                            def tok_cons(e, i, pm, sl):
                                return e.activation(ktok_st[:, sl, :], pm[:],
                                                    AF.Exp, bias=nln16[:])

                            def tok_dma(sy, i, sl, head=head):
                                return sy.dma_start(
                                    out=kptok_d[i * P:(i + 1) * P,
                                                head * M:(head + 1) * M],
                                    in_=ktok_st[:, sl, :])

                            pipe(pr, f"tok{head}", list(range(NT)),
                                 [p_tok0, p_tok1], tok_prod, tok_cons,
                                 t_entry=[(feat, feat.v), (corr_ev, corr_ev.v),
                                          (cst, cst.v)],
                                 dma_emit=tok_dma,
                                 nstage=4, stage_buf=ktok_st)
                            tk_s = pr.sem("mm")
                            xs_readers = [(tk_s, tk_s.v)]

            def v_prod(t, job, pm):
                i, dq = job
                for k in range(KC):
                    t.matmul(pm[:], lnT_s[:, k, i * P:(i + 1) * P],
                             vw_s[:, k, dq * TQ:(dq + 1) * TQ],
                             start=(k == 0), stop=False)
                return t.matmul(pm[:], ones_bc[:, 0:P],
                                vb_row[:, dq * TQ:(dq + 1) * TQ],
                                start=False, stop=True)

            pipe(pr, "v", [(i, dq) for i in range(NT) for dq in range(2)],
                 [p_mm0, p_mm1], v_prod,
                 lambda e, job, pm, sl: e.copy(v_st[:, sl, :], pm[:]),
                 t_entry=[(tr_ev, tr_ev.v), (feat, feat.v), (ld, ld.v),
                          (cst, cst.v)],
                 dma_emit=lambda sy, job, sl: sy.dma_start(
                     out=vtok_d[job[0] * P:(job[0] + 1) * P,
                                job[1] * TQ:(job[1] + 1) * TQ],
                     in_=v_st[:, sl, :]),
                 nstage=4, stage_buf=v_st)

            @block.sync
            def _(sy):
                for nm, s_ in list(pr._sems.items()):
                    if nm.startswith("dm") or nm == "st":
                        sy.wait_ge(s_.h, s_.v)
    return nc



def build_l0_scan(NU=4, S=8192):
    NCH = S // P
    nc = bass.Bass("TRN2", target_bir_lowering=False, debug=False)
    qpt_d = nc.dram_tensor("qpt_u", [NU, M, S], F32, kind="ExternalInput").ap()
    kpt_d = nc.dram_tensor("kpt_u", [NU, M, S], F32, kind="ExternalInput").ap()
    ktok_d = nc.dram_tensor("ktok_u", [NU, S, M], F32, kind="ExternalInput").ap()
    v_d = nc.dram_tensor("v_u", [NU, S, DH], F32, kind="ExternalInput").ap()
    mask_d = nc.dram_tensor("mask_ut", [P, P], F32, kind="ExternalInput").ap()
    out_d = nc.dram_tensor("attn_u", [NU, S, DH], F32, kind="ExternalOutput").ap()

    with contextlib.ExitStack() as st_:
        sb = lambda *a: st_.enter_context(nc.sbuf_tensor(*a))
        ps = lambda *a: st_.enter_context(nc.psum_tensor(*a))
        qpt_s = sb("qpt_s", [P, 2, S], BF16)
        kpt_s = sb("kpt_s", [P, 2, S], BF16)
        ktok_s = sb("ktok_s", [P, NCH, M], BF16)
        v_s = sb("v_s", [P, NCH, DH], BF16)
        mask_s = sb("mask_s", [P, P], BF16)
        ones_red = sb("ones_red", [P, 1], BF16)
        a_bf = sb("a_bf", [P, P], BF16)
        state_s = sb("state_s", [P, 2, DH + 1], F32)
        state_bf = sb("state_bf", [P, 2, DH + 1], BF16)
        rd = sb("rd", [P, 1], F32)
        stage = sb("stage", [P, 4, DH], F32)
        p_a = ps("p_a", [P, P], F32)
        p_o = ps("p_o", [P, DH + 1], F32)
        p_u0 = ps("p_u0", [P, DH + 1], F32)
        p_u1 = ps("p_u1", [P, DH + 1], F32)

        with nc.Block() as block:
            pr = Prog(nc, block, st_)
            ld = pr.sem("ld")
            mm = pr.sem("mm")
            evV = pr.sem("evV")
            evS = pr.sem("evS")
            cst = pr.sem("cst")
            dms = [pr.sem(f"dm{i}") for i in range(4)]

            @block.gpsimd
            def _(g):
                ld.inc(g.dma_start(out=mask_s[:], in_=mask_d), 16)

            @block.vector
            def _(v):
                cst.inc(v.memset(ones_red[:], 1.0))

            for u in range(NU):
                @block.gpsimd
                def _(g, u=u):
                    g.wait_ge(mm.h, mm.v)
                    ld.inc(g.dma_start(out=qpt_s[:],
                                       in_=qpt_d[u].rearrange("(c p) t -> p c t", p=P)), 16)
                    ld.inc(g.dma_start(out=kpt_s[:],
                                       in_=kpt_d[u].rearrange("(c p) t -> p c t", p=P)), 16)
                    ld.inc(g.dma_start(out=ktok_s[:],
                                       in_=ktok_d[u].rearrange("(n p) m -> p n m", p=P)), 16)
                    ld.inc(g.dma_start(out=v_s[:],
                                       in_=v_d[u].rearrange("(n p) d -> p n d", p=P)), 16)

                @block.vector
                def _(v):
                    v.wait_ge(evS.h, evS.v)
                    v.memset(state_s[:], 0.0)
                    evV.inc(v.memset(state_bf[:], 0.0))

                for c in range(NCH):
                    tok = slice(c * P, (c + 1) * P)

                    @block.tensor
                    def _(t, u=u, c=c, tok=tok):
                        t.wait_ge(ld.h, ld.v)
                        t.wait_ge(evV.h, evV.v)
                        t.wait_ge(evS.h, evS.v)
                        t.wait_ge(cst.h, cst.v)
                        t.matmul(p_a[:], kpt_s[:, 0, tok], qpt_s[:, 0, tok],
                                 start=True, stop=False)
                        mm.inc(t.matmul(p_a[:], kpt_s[:, 1, tok],
                                        qpt_s[:, 1, tok], start=False, stop=True))

                    @block.vector
                    def _(v, c=c):
                        v.wait_ge(mm.h, mm.v)
                        evV.inc(v.tensor_tensor(a_bf[:], p_a[:], mask_s[:],
                                                ALU.mult))

                    @block.tensor
                    def _(t, c=c, tok=tok):
                        t.wait_ge(evV.h, evV.v)
                        t.matmul(p_o[:, 0:DH], a_bf[:], v_s[:, c, :],
                                 start=True, stop=False)
                        t.matmul(p_o[:, DH:DH + 1], a_bf[:], ones_red[:],
                                 start=False, stop=False)
                        t.matmul(p_o[:], qpt_s[:, 0, tok], state_bf[:, 0, :],
                                 start=False, stop=False)
                        mm.inc(t.matmul(p_o[:], qpt_s[:, 1, tok], state_bf[:, 1, :],
                                        start=False, stop=True))
                        t.matmul(p_u0[:, 0:DH], ktok_s[:, c, 0:P], v_s[:, c, :],
                                 start=True, stop=False)
                        t.matmul(p_u0[:, DH:DH + 1], ktok_s[:, c, 0:P], ones_red[:],
                                 start=False, stop=True)
                        t.matmul(p_u1[:, 0:DH], ktok_s[:, c, P:M], v_s[:, c, :],
                                 start=True, stop=False)
                        mm.inc(t.matmul(p_u1[:, DH:DH + 1], ktok_s[:, c, P:M],
                                        ones_red[:], start=False, stop=True))

                    @block.vector
                    def _(v, c=c):
                        v.wait_ge(mm.h, mm.v - 1)
                        v.wait_ge(evS.h, evS.v)
                        evV.inc(v.reciprocal(rd[:], p_o[:, DH:DH + 1]))
                        v.wait_ge(mm.h, mm.v)
                        v.tensor_add(state_s[:, 0, :], state_s[:, 0, :], p_u0[:])
                        evV.inc(v.tensor_add(state_s[:, 1, :], state_s[:, 1, :],
                                             p_u1[:]))

                    @block.scalar
                    def _(s, u=u, c=c):
                        s.wait_ge(evV.h, evV.v - 1)
                        j = c % 4
                        sy_t = dms[j]
                        s.wait_ge(sy_t.h, sy_t.v)
                        evS.inc(s.activation(stage[:, j, :], p_o[:, 0:DH],
                                             AF.Identity, scale=rd[:]))
                        s.wait_ge(evV.h, evV.v)
                        s.activation(state_bf[:, 0, :], state_s[:, 0, :], AF.Identity)
                        evS.inc(s.activation(state_bf[:, 1, :], state_s[:, 1, :],
                                             AF.Identity))

                    @block.sync
                    def _(sy, u=u, c=c):
                        j = c % 4
                        sy.wait_ge(evS.h, evS.v - 1)
                        dms[j].inc(sy.dma_start(
                            out=out_d[u, c * P:(c + 1) * P, :],
                            in_=stage[:, j, :]), 16)

            @block.sync
            def _(sy):
                for d in dms:
                    sy.wait_ge(d.h, d.v)
    return nc



def build_l0_post():
    nc = bass.Bass("TRN2", target_bir_lowering=False, debug=False)
    attnT_d = nc.dram_tensor("attnT", [D, T], F32, kind="ExternalInput").ap()
    ln1_d = nc.dram_tensor("ln1", [T, D], F32, kind="ExternalInput").ap()
    x_d = nc.dram_tensor("x", [T, D], F32, kind="ExternalInput").ap()
    ow_d = nc.dram_tensor("out_w", [D, D], F32, kind="ExternalInput").ap()
    ob_d = nc.dram_tensor("out_b", [1, D], F32, kind="ExternalInput").ap()
    ga_d = nc.dram_tensor("gamma", [1, D], F32, kind="ExternalInput").ap()
    be_d = nc.dram_tensor("beta", [1, D], F32, kind="ExternalInput").ap()
    x1_d = nc.dram_tensor("x1", [T, D], F32, kind="ExternalOutput").ap()

    with contextlib.ExitStack() as st_:
        sb = lambda *a: st_.enter_context(nc.sbuf_tensor(*a))
        ps = lambda *a: st_.enter_context(nc.psum_tensor(*a))
        at_s = sb("at_s", [P, KC, T], BF16)
        ow_s = sb("ow_s", [P, KC, D], BF16)
        ob_row = sb("ob_row", [1, D], BF16)
        ga_row = sb("ga_row", [1, D], F32)
        be_row = sb("be_row", [1, D], F32)
        ones_bc = sb("ones_bc", [1, TQ], BF16)
        onesf_bc = sb("onesf_bc", [1, P], F32)
        Bg = sb("Bg", [P, D], F32)
        Bb = sb("Bb", [P, D], F32)
        res_s = sb("res_s", [P, NT, D], F32)
        ln1_t = sb("ln1_t", [P, 2, D], F32)
        x_t = sb("x_t", [P, 2, D], F32)
        xh_t = sb("xh_t", [P, 2, D], F32)
        stage = sb("stage", [P, 4, D], F32)
        cols = alloc_ln_cols(sb)
        p_mm0 = ps("p_mm0", [P, TQ], F32)
        p_mm1 = ps("p_mm1", [P, TQ], F32)
        p_g = ps("p_g", [P, TQ], F32)

        with nc.Block() as block:
            pr = Prog(nc, block, st_)
            ld = pr.sem("ld")
            cst = pr.sem("cst")
            evV = pr.sem("ev_vector")
            evS = pr.sem("ev_scalar")
            mmlds = [pr.sem("mmld0"), pr.sem("mmld1")]
            xlds = [pr.sem("xld0"), pr.sem("xld1")]
            dms = [pr.sem(f"dm{i}") for i in range(4)]

            @block.gpsimd
            def _(g):
                ld.inc(g.dma_start(out=at_s[:], in_=attnT_d.rearrange("(k p) t -> p k t", p=P)), 16)
                ld.inc(g.dma_start(out=ow_s[:], in_=ow_d.rearrange("(k p) d -> p k d", p=P)), 16)
                ld.inc(g.dma_start(out=ob_row[:], in_=ob_d), 16)
                ld.inc(g.dma_start(out=ga_row[:], in_=ga_d), 16)
                ld.inc(g.dma_start(out=be_row[:], in_=be_d), 16)

            @block.vector
            def _(v):
                v.memset(ones_bc[:], 1.0)
                cst.inc(v.memset(onesf_bc[:], 1.0))

            g_ev = pipe(pr, "g", [(w, dq) for w in range(2) for dq in range(2)],
                        [p_g],
                        lambda t, j, pm: t.matmul(
                            pm[:], onesf_bc[:, 0:P],
                            (ga_row if j[0] == 0 else be_row)[:, j[1] * TQ:(j[1] + 1) * TQ],
                            start=True, stop=True),
                        lambda e, j, pm, sl: e.tensor_copy(
                            (Bg if j[0] == 0 else Bb)[:, j[1] * TQ:(j[1] + 1) * TQ],
                            pm[:]),
                        t_entry=[(ld, ld.v), (cst, cst.v)],
                        consumer_engine="vector")

            rbase = evV.v

            @block.gpsimd
            def _(g):
                for i in range(NT):
                    sl = i % 2
                    if i >= 2:
                        g.wait_ge(evV.h, rbase + 2 * (i - 1))
                    g.wait_ge(mmlds[sl].h, mmlds[sl].v)
                    mmlds[sl].inc(g.dma_start(out=ln1_t[:, sl, :],
                                              in_=ln1_d[i * P:(i + 1) * P, :]), 16)

            def r_prod(t, job, pm):
                i, dq = job
                for k in range(KC):
                    t.matmul(pm[:], at_s[:, k, i * P:(i + 1) * P],
                             ow_s[:, k, dq * TQ:(dq + 1) * TQ],
                             start=(k == 0), stop=False)
                return t.matmul(pm[:], ones_bc[:, 0:P],
                                ob_row[:, dq * TQ:(dq + 1) * TQ],
                                start=False, stop=True)

            def r_cons(e, job, pm, sl):
                i, dq = job
                e.wait_ge(mmlds[i % 2].h, 16 * (i // 2 + 1))
                return e.scalar_tensor_tensor(
                    res_s[:, i, dq * TQ:(dq + 1) * TQ], pm[:], 1.0,
                    ln1_t[:, i % 2, dq * TQ:(dq + 1) * TQ], ALU.mult, ALU.add)

            r_ev = pipe(pr, "r", [(i, dq) for i in range(NT) for dq in range(2)],
                        [p_mm0, p_mm1], r_prod, r_cons,
                        t_entry=[(ld, ld.v), (cst, cst.v)],
                        consumer_engine="vector")

            ln_i = emit_ln(pr, lambda i: res_s[:, i, :], list(range(NT)), cols,
                           1e-6, entry=[(r_ev, r_ev.v), (g_ev, g_ev.v)],
                           apply=False)

            for i in range(NT):
                @block.scalar
                def _(s, i=i):
                    s.wait_ge(ln_i.h, ln_i.v)
                    if i >= 2:
                        s.wait_ge(evV.h, evV.v)
                    evS.inc(s.activation(xh_t[:, i % 2, :], res_s[:, i, :],
                                         AF.Identity, bias=cols["nmr"][:, i:i + 1],
                                         scale=cols["rstd"][:, i:i + 1]))

                @block.vector
                def _(v, i=i):
                    v.wait_ge(evS.h, evS.v)
                    evV.inc(v.tensor_mul(res_s[:, i, :], xh_t[:, i % 2, :], Bg[:]))
                    v.wait_ge(evV.h, evV.v)
                    evV.inc(v.tensor_add(res_s[:, i, :], res_s[:, i, :], Bb[:]))

            ln_o = emit_ln(pr, lambda i: res_s[:, i, :], list(range(NT)), cols,
                           1e-5, entry=[(evV, evV.v), (evS, evS.v)], apply=False)

            obase_S = evS.v

            xbaseV = evV.v

            @block.gpsimd
            def _(g):
                for i in range(NT):
                    sl = i % 2
                    if i >= 2:
                        g.wait_ge(evV.h, xbaseV + (i - 1))
                    g.wait_ge(xlds[sl].h, xlds[sl].v)
                    xlds[sl].inc(g.dma_start(out=x_t[:, sl, :],
                                             in_=x_d[i * P:(i + 1) * P, :]), 16)

            for i in range(NT):
                @block.scalar
                def _(s, i=i):
                    s.wait_ge(ln_o.h, ln_o.v)
                    if i >= 2:
                        s.wait_ge(evV.h, evV.v)
                    evS.inc(s.activation(xh_t[:, i % 2, :], res_s[:, i, :],
                                         AF.Identity, bias=cols["nmr"][:, i:i + 1],
                                         scale=cols["rstd"][:, i:i + 1]))

                @block.vector
                def _(v, i=i):
                    v.wait_ge(evS.h, evS.v)
                    v.wait_ge(xlds[i % 2].h, 16 * (i // 2 + 1))
                    sl = i % 4
                    v.wait_ge(dms[sl].h, dms[sl].v)
                    evV.inc(v.tensor_add(stage[:, sl, :], xh_t[:, i % 2, :],
                                         x_t[:, i % 2, :]))

                @block.sync
                def _(sy, i=i):
                    sy.wait_ge(evV.h, evV.v)
                    sl = i % 4
                    sy.wait_ge(dms[sl].h, dms[sl].v)
                    dms[sl].inc(sy.dma_start(out=x1_d[i * P:(i + 1) * P, :],
                                             in_=stage[:, sl, :]), 16)

            @block.sync
            def _(sy):
                for d_ in dms:
                    sy.wait_ge(d_.h, d_.v)
    return nc



NSLOT = T + 256 * 15
NTS = NSLOT // P
G = NSLOT // 16
GPAD = 384


def fff_host_weights(nw, nb, w1, b1, w2, b2):
    nwT_aug = np.concatenate([np.ascontiguousarray(nw.T),
                              nb[None, :]], axis=0)
    w12aug = np.concatenate([b1.reshape(4096, 1),
                             w2.reshape(4096, D),
                             np.ascontiguousarray(w1.transpose(0, 2, 1)
                                                  ).reshape(4096, D)],
                            axis=1)
    return nwT_aug, w12aug, np.ascontiguousarray(b2)


def fff_consts():
    return dict(
        triu_strict=np.triu(np.ones((P, P), np.float32), 1),
        blockdiag=np.kron(np.eye(8, dtype=np.float32),
                          np.ones((16, 16), np.float32)),
        g16col=np.stack([(np.arange(P) + gc * P) * 16.0
                         for gc in range(GPAD // P)], 1).astype(np.float32),
        iota256col=np.stack([np.arange(P, dtype=np.float32),
                             np.arange(P, 2 * P, dtype=np.float32)], 1),
        posmap=(np.arange(P) // 16).astype(np.int32).reshape(P, 1),
        jcol=(np.arange(P) % 16).astype(np.float32).reshape(P, 1),
        identf=np.eye(P, dtype=np.float32),
    )


def alloc_fff_persist(sb, nff):
    return dict(
        UT=sb("f_UT", [P, nff, 2, T], BF16),
        dest=sb("f_dest", [P, nff, NT], F32),
        identb=sb("f_identb", [P, P], BF16),
        identh=sb("f_identh", [P, P], F16),
    )


def alloc_fff_route_bufs(sb, ps):
    return dict(
        lg=sb("f_lg", [P, NT, 255], F32),
        U=sb("f_U", [P, NT, M], BF16),
        ua=sb("f_ua", [P, P], BF16),
        ub=sb("f_ub", [P, M], BF16),
        sg=sb("f_sg", [P, P], BF16),
        t1=sb("f_t1", [P, P], BF16),
        ctrow=sb("f_ctrow", [1, M], F32),
        scst=sb("f_scst", [P, 2, D], BF16),
        zt=sb("f_zt", [P, 512], BF16),
        c16i=sb("f_c16i", [1, M], I32),
        roff=sb("f_roff", [1, M], F32),
        crow=sb("f_crow", [1, M], F32),
        c16=sb("f_c16", [1, M], F32),
        offrow=sb("f_off", [1, M], F32),
        ocrow=sb("f_oc", [1, M], F32),
        glrow=sb("f_gl", [1, GPAD], F32),

        glf=sb("f_glf", [P, 1], F32),
        rhs2=sb("f_rhs2", [P, 2, 2], F32),
        UTf=sb("f_UTf", [P, 2, P], F32),
        rankc=sb("f_rank", [P, 1], F32),
        destc=sb("f_destc", [P, 1], F32),
        desti=sb("f_desti", [P, 1], I32),
        tr256=sb("f_tr256", [P, M], F32),
        g16=sb("f_g16", [P, GPAD // P], F32),
        iota2=sb("f_iota2", [P, 2], F32),
        triu=sb("f_triu", [P, P], BF16),
        onesf=sb("f_onesf", [P, 1], BF16),
        onesfbc=sb("f_onesfbc", [1, P], F32),
        onesbc1=sb("f_onesbc1", [1, P], BF16),
        identf=sb("f_identf", [P, P], F32),
        p_l=ps("fp_l", [P, 255], F32),
        p_r=ps("fp_r", [P, M], F32),
        p_cnt=ps("fp_cnt", [1, M], F32),
        p_d=ps("fp_d", [P, 2], F32),
        p_t=ps("fp_t", [P, P], F32),
        p_tb=ps("fp_tb", [P, P], BF16),
    )


def emit_fff_route(pr, bf, fi, xn_tok, xnT, xnT_lo, nwT_s, nwT_lo, nb_row,
                   nb_lo, Xs_d, gl_d,
                   entry=()):
    block = pr.block
    mm = pr.sem("mm")
    evV = pr.sem("ev_vector")
    evS = pr.sem("ev_scalar")
    sc = pr.sem("scat")
    ld = pr.sem("ld")

    dmz = pr.sem("dmz")
    Xs_flat = Xs_d.rearrange("(p a) b -> p (a b)", p=P)
    nrow = (NSLOT // P) * D

    @block.vector
    def _(v):
        if not bf.get("_zt0"):
            bf["_zt0"] = True
            pr.sem("cstz").inc(v.memset(bf["zt"][:], 0.0))

    @block.sync
    def _(sy):
        sy.wait_ge(pr.sem("cstz").h, pr.sem("cstz").v)
        sy.wait_ge(pr.sem("scat").h, pr.sem("scat").v)
        for z0 in range(0, nrow, 512):
            zw = min(512, nrow - z0)
            sy.wait_ge(dmz.h, dmz.v)
            dmz.inc(sy.dma_start(out=Xs_flat[:, z0:z0 + zw],
                                 in_=bf["zt"][:, 0:zw]), 16)

    def lg_prod(t, i, pm):
        for k in range(KC):
            t.matmul(pm[:], xnT[:, k, i * P:(i + 1) * P], nwT_s[:, k, :],
                     start=(k == 0), stop=False)
        for k in range(KC):
            t.matmul(pm[:], xnT[:, k, i * P:(i + 1) * P], nwT_lo[:, k, :],
                     start=False, stop=False)
        for k in range(KC):
            t.matmul(pm[:], xnT_lo[:, k, i * P:(i + 1) * P], nwT_s[:, k, :],
                     start=False, stop=False)
        t.matmul(pm[:], bf["onesbc1"][:, 0:P], nb_row[:],
                 start=False, stop=False)
        return t.matmul(pm[:], bf["onesbc1"][:, 0:P], nb_lo[:],
                        start=False, stop=True)

    lg_ev = pipe(pr, "lg", list(range(NT)), [bf["p_l"]], lg_prod,
                 lambda e, i, pm, sl: e.tensor_copy(bf["lg"][:, i, :], pm[:]),
                 t_entry=list(entry), consumer_engine="vector")

    for i in range(NT):
        @block.vector
        def _(v, i=i):
            v.wait_ge(lg_ev.h, lg_ev.v)
            v.wait_ge(mm.h, mm.v)
            evV.inc(v.memset(bf["ua"][:, 0:1], 1.0))
            bufs = [bf["ua"], bf["ub"]]
            for l in range(8):
                w = 1 << l
                lo = w - 1
                u = bufs[l % 2]
                u2 = bf["U"][:, i, :] if l == 7 else bufs[(l + 1) % 2]
                evV.inc(v.tensor_scalar(bf["sg"][:, 0:w],
                                        bf["lg"][:, i, lo:lo + w], 0.0, None,
                                        ALU.is_gt))
                v.wait_ge(evV.h, evV.v)
                evV.inc(v.tensor_mul(bf["t1"][:, 0:w], u[:, 0:w],
                                     bf["sg"][:, 0:w]))
                v.wait_ge(evV.h, evV.v)
                evV.inc(v.tensor_sub(u2[:, 0:2 * w:2], u[:, 0:w],
                                     bf["t1"][:, 0:w]))
                evV.inc(v.tensor_copy(u2[:, 1:2 * w:2], bf["t1"][:, 0:w]))

    def cnt_prod(t, i, pm):
        return t.matmul(pm[:], bf["onesf"][:], bf["U"][:, i, :],
                        start=(i == 0), stop=(i == NT - 1))

    cnt_ev = pipe(pr, "cnt", [list(range(NT))], [bf["p_cnt"]],
                  lambda t, jobs, pm: [cnt_prod(t, i, pm) for i in jobs][-1],
                  lambda e, jobs, pm, sl: e.tensor_copy(bf["crow"][:], pm[:]),
                  t_entry=[(evV, evV.v)], consumer_engine="vector")

    def ut_prod(t, job, pm):
        i, c2 = job
        return t.transpose(pm[:], bf["U"][:, i, c2 * P:(c2 + 1) * P],
                           bf["identb"][:])

    ut_ev = pipe(pr, "ut", [(i, c2) for i in range(NT) for c2 in range(2)],
                 [bf["p_tb"]], ut_prod,
                 lambda e, job, pm, sl: e.tensor_copy(
                     bf["UT"][:, fi, job[1], job[0] * P:(job[0] + 1) * P], pm[:]),
                 t_entry=[(evV, evV.v)], consumer_engine="vector")

    @block.vector
    def _(v):
        v.wait_ge(cnt_ev.h, cnt_ev.v)
        evV.inc(v.memset(bf["roff"][:], 0.0))
        v.wait_ge(evV.h, evV.v)
        evV.inc(v.tensor_scalar(bf["c16"][:], bf["crow"][:], 15.0, 0.0625,
                                ALU.add, ALU.mult))
        v.wait_ge(evV.h, evV.v)
        evV.inc(v.tensor_copy(bf["c16i"][:], bf["c16"][:]))
        v.wait_ge(evV.h, evV.v)
        evV.inc(v.tensor_copy(bf["c16"][:], bf["c16i"][:]))
        v.wait_ge(evV.h, evV.v)
        evV.inc(v.tensor_scalar_mul(bf["c16"][:], bf["c16"][:], 16.0))
        v.wait_ge(evV.h, evV.v)
        evV.inc(v.tensor_tensor_scan(bf["offrow"][:], bf["c16"][:], bf["c16"][:],
                                     0.0, ALU.add, ALU.bypass))
        v.wait_ge(evV.h, evV.v)
        evV.inc(v.tensor_sub(bf["offrow"][:], bf["offrow"][:], bf["c16"][:]))

    boff_ev = pipe(pr, "boff", [0], [bf["p_r"]],
                   lambda t, j, pm: t.matmul(pm[:], bf["onesfbc"][:],
                                             bf["offrow"][:], start=True,
                                             stop=True),
                   lambda e, j, pm, sl: e.tensor_copy(bf["lg"][:].rearrange("p n w -> p (n w)")[:, KC * 255:KC * 255 + M][:], pm[:]),
                   t_entry=[(evV, evV.v)], consumer_engine="vector")

    for gc in range(GPAD // P):
        @block.vector
        def _(v, gc=gc):
            v.wait_ge(boff_ev.h, boff_ev.v)
            if gc:
                v.wait_ge(mm.h, mm.v)
            evV.inc(v.tensor_scalar(bf["tr256"][:], bf["lg"][:].rearrange("p n w -> p (n w)")[:, KC * 255:KC * 255 + M][:],
                                    bf["g16"][:, gc:gc + 1], None, ALU.is_le))
            v.wait_ge(evV.h, evV.v)
            evV.inc(v.tensor_reduce(bf["glf"][:], bf["tr256"][:], AX.X, ALU.add))
            v.wait_ge(evV.h, evV.v)
            evV.inc(v.tensor_scalar(bf["glf"][:], bf["glf"][:], -1.0, None,
                                    ALU.add))

        gl_ev = pipe(pr, f"glt{gc}", [0], [bf["p_t"]],
                     lambda t, j, pm, gc=gc: t.transpose(pm[0:1, :], bf["glf"][:],
                                                         bf["identf"][:]),
                     lambda e, j, pm, sl, gc=gc: e.tensor_copy(
                         bf["glrow"][:, gc * P:(gc + 1) * P], pm[0:1, :]),
                     t_entry=[(evV, evV.v)],
                     c_entry=[(pr.sem("dmgl"), pr.sem("dmgl").v)],
                     consumer_engine="vector")

    @block.sync
    def _(sy):
        sy.wait_ge(pr.sem("ev_vector").h, pr.sem("ev_vector").v)
        pr.sem("dmgl").inc(sy.dma_start(out=gl_d, in_=bf["glrow"][:]), 16)

    for i in range(NT):
        ct_ev = pipe(pr, f"ct{fi}_{i}", [0], [bf["p_cnt"]],
                     lambda t, j, pm, i=i: t.matmul(
                         pm[:], bf["onesf"][:], bf["U"][:, i, :],
                         start=True, stop=True),
                     lambda e, j, pm, sl: e.tensor_copy(bf["ctrow"][:], pm[:]),
                     t_entry=[(evV, evV.v)], consumer_engine="vector")

        @block.vector
        def _(v, i=i):
            v.wait_ge(evV.h, evV.v)
            v.wait_ge(mm.h, mm.v)
            evV.inc(v.tensor_add(bf["ocrow"][:], bf["offrow"][:], bf["roff"][:]))
            v.wait_ge(evV.h, evV.v)
            evV.inc(v.tensor_add(bf["roff"][:], bf["roff"][:], bf["ctrow"][:]))

        oc_ev = pipe(pr, f"oct{fi}_{i}", [0, 1], [bf["p_t"]],
                     lambda t, c2, pm: t.transpose(
                         pm[:, 0:1], bf["ocrow"][:, c2 * P:(c2 + 1) * P],
                         bf["identf"][0:1, 0:1]),
                     lambda e, c2, pm, sl: e.tensor_copy(
                         bf["rhs2"][:, c2, 0:1], pm[:, 0:1]),
                     t_entry=[(evV, evV.v)], consumer_engine="vector")

        @block.vector
        def _(v, i=i):
            v.wait_ge(mm.h, mm.v)
            v.tensor_copy(bf["UTf"][:, 0, :], bf["UT"][:, fi, 0, i * P:(i + 1) * P])
            evV.inc(v.tensor_copy(bf["UTf"][:, 1, :],
                                  bf["UT"][:, fi, 1, i * P:(i + 1) * P]))

        @block.tensor
        def _(t, i=i):
            t.wait_ge(oc_ev.h, oc_ev.v)
            t.wait_ge(evV.h, evV.v)
            for c2 in range(2):
                t.matmul(bf["p_d"][:], bf["UTf"][:, c2, :],
                         bf["rhs2"][:, c2, :], start=(c2 == 0), stop=(c2 == 1))
            mm.inc(t.matmul(bf["p_r"][:], bf["triu"][:], bf["U"][:, i, :],
                            start=True, stop=True))

        @block.vector
        def _(v, i=i):
            v.wait_ge(mm.h, mm.v)
            evV.inc(v.scalar_tensor_tensor(bf["tr256"][:], bf["p_r"][:], 1.0,
                                           bf["U"][:, i, :], ALU.mult, ALU.mult,
                                           accum_out=bf["rankc"][:]))
            v.wait_ge(evV.h, evV.v)
            evV.inc(v.tensor_add(bf["destc"][:], bf["p_d"][:, 0:1], bf["rankc"][:]))
            v.wait_ge(evV.h, evV.v)
            evV.inc(v.tensor_copy(bf["dest"][:, fi, i:i + 1], bf["destc"][:]))
            v.wait_ge(sc.h, sc.v)
            evV.inc(v.tensor_copy(bf["desti"][:], bf["destc"][:]))

        @block.scalar
        def _(s, i=i, entry=tuple(entry)):
            for s_, vv in entry:
                s.wait_ge(s_.h, vv)
            s.wait_ge(sc.h, max(0, sc.v - 16))
            evS.inc(s.activation(bf["scst"][:, i % 2, :], xn_tok(i),
                                 AF.Identity))

        @block.gpsimd
        def _(g, i=i):
            g.wait_ge(evV.h, evV.v)
            g.wait_ge(evS.h, evS.v)
            g.wait_ge(dmz.h, dmz.v)
            g.wait_ge(sc.h, sc.v)
            sc.inc(g.indirect_dma_start(
                Xs_d, IndirectOffsetOnAxis(ap=bf["desti"][:], axis=0),
                bf["scst"][:, i % 2, :], None), 16)


def alloc_fff_comp_bufs(sb, ps, half):
    HS = half * P
    return dict(
        XsT=sb("fc_XsT", [P, KC, HS], BF16),
        b2=sb("fc_b2", [P, 2, D], BF16),
        glcol=sb("fc_glc", [P, 2], F32),
        idxf=sb("fc_idxf", [P, 2], F32),
        idxi=sb("fc_idxi", [P, 2], I32),
        posmap=sb("fc_pos", [P, 1], I32),
        jcol=sb("fc_jcol", [P, 1], F32),
        w2f=sb("fc_w2f", [P, 2, 2049], F32),
        w2b=sb("fc_w2b", [P, 2, 1025], BF16),
        w1tb=sb("fc_w1tb", [P, 2, D], BF16),
        asm=sb("fc_asm", [P, 2, KC, P], BF16),
        hmask=sb("fc_hm", [P, P], BF16),
        hp=sb("fc_hp", [P, P], BF16),
        bd=sb("fc_bd", [P, P], BF16),
        outst=sb("fc_out", [P, 4, D], F32),
        gst=sb("fc_gst", [P, 2, D], F32),
        oidx=sb("fc_oidx", [P, 1], I32),
        p_h=ps("fcp_h", [P, P], F32),
        p_h2=ps("fcp_h2", [P, P], F32),
        p_o0=ps("fcp_o0", [P, TQ], F32),
        p_o1=ps("fcp_o1", [P, TQ], F32),
        p_b0=ps("fcp_b0", [P, TQ], F32),
        p_b1=ps("fcp_b1", [P, TQ], F32),
        p_tb=ps("fcp_tb", [P, P], BF16),
    )


def emit_fff_compute(pr, bf, cb, fi, w12aug_d, b2_d, Xs_d, gl_d, out_d,
                     half=23, entry=()):
    block = pr.block
    mm = pr.sem("mm")
    evV = pr.sem("ev_vector")
    evS = pr.sem("ev_scalar")
    ld = pr.sem("ld")
    scg = pr.sem("scat")
    wgs = pr.sem("wgs")
    dmt = pr.sem("dmt")
    act_fn = bf["act_fn"]
    idxv, castv, actv, evacv, wgsv = {}, {}, {}, {}, {}

    @block.gpsimd
    def _(g):
        g.wait_ge(ld.h, ld.v)
        g.wait_ge(mm.h, mm.v)
        ld.inc(g.dma_start(out=cb["b2"][:],
                           in_=b2_d.rearrange("(c p) d -> p c d", p=P)), 16)

    nhalf = NTS // half + (1 if NTS % half else 0)
    for hh in range(nhalf):
        t0 = hh * half
        t1 = min(NTS, (hh + 1) * half)
        HS = (t1 - t0) * P

        @block.sync
        def _(sy, t0=t0, HS=HS):
            sy.wait_ge(pr.sem("scat").h, pr.sem("scat").v)
            if t0 == 0:
                pass
            else:
                sy.wait_ge(mm.h, mm.v)
            for k in range(KC):
                sy.wait_ge(dmt.h, dmt.v)
                dmt.inc(sy.dma_start(
                    out=cb["XsT"][:, k, 0:HS],
                    in_=Xs_d[t0 * P:t0 * P + HS, k * P:(k + 1) * P],
                    transpose=True), 16)

        def gather_stage(ti):
            sl2 = ti % 2

            @block.gpsimd
            def _(g, ti=ti, sl2=sl2):
                g.wait_ge(ld.h, ld.v)
                if ti - 2 in idxv:
                    g.wait_ge(evV.h, idxv[ti - 2])
                g.wait_ge(pr.sem("dmgl").h, pr.sem("dmgl").v)
                g.wait_ge(scg.h, scg.v)
                scg.inc(g.indirect_dma_start(
                    cb["glcol"][:, sl2:sl2 + 1], None, gl_d,
                    IndirectOffsetOnAxis(ap=cb["posmap"][:], axis=1),
                    element_offset=8 * ti), 16)

            @block.vector
            def _(v, ti=ti, sl2=sl2):
                v.wait_ge(scg.h, scg.v)
                evV.inc(v.scalar_tensor_tensor(
                    cb["idxf"][:, sl2:sl2 + 1], cb["glcol"][:, sl2:sl2 + 1],
                    16.0, cb["jcol"][:], ALU.mult, ALU.add))
                v.wait_ge(evV.h, evV.v)
                evV.inc(v.tensor_copy(cb["idxi"][:, sl2:sl2 + 1],
                                      cb["idxf"][:, sl2:sl2 + 1]))
            idxv[ti] = evV.v

            @block.gpsimd
            def _(g, ti=ti, sl2=sl2):
                g.wait_ge(evV.h, evV.v if ti - 2 not in castv
                          else max(idxv[ti], castv[ti - 2]))
                if ti - 2 in actv:
                    g.wait_ge(evS.h, actv[ti - 2])
                g.wait_ge(wgs.h, wgs.v)
                wgs.inc(g.indirect_dma_start(
                    cb["w2f"][:, sl2, :], None, w12aug_d,
                    IndirectOffsetOnAxis(ap=cb["idxi"][:, sl2:sl2 + 1],
                                         axis=0)), 16)

        def compute_stage(ti, tl):
            sl2 = ti % 2
            p_h = cb["p_h"] if sl2 == 0 else cb["p_h2"]

            @block.vector
            def _(v, ti=ti, sl2=sl2, wv=None):
                v.wait_ge(wgs.h, wgsv[ti])
                v.wait_ge(evS.h, evacv.get(ti - 2, 0))
                evV.inc(v.tensor_copy(cb["w2b"][:, sl2, :],
                                      cb["w2f"][:, sl2, 0:1025]))
                evV.inc(v.tensor_copy(cb["w1tb"][:, sl2, :],
                                      cb["w2f"][:, sl2, 1025:2049]))
            castv[ti] = evV.v

            asm_ev = pipe(pr, f"asm{fi}_{ti}", list(range(KC)), [cb["p_tb"]],
                          lambda t, k, pm, sl2=sl2: t.transpose(
                              pm[:], cb["w1tb"][:, sl2, k * P:(k + 1) * P],
                              bf["identb"][:]),
                          lambda e, k, pm, sl, ti=ti: e.tensor_copy(
                              cb["asm"][:, ti % 2, k, :], pm[:]),
                          t_entry=[(evV, evV.v)], consumer_engine="vector")

            @block.tensor
            def _(t, ti=ti, tl=tl, p_h=p_h, av=asm_ev.v):
                t.wait_ge(asm_ev.h, av)
                t.wait_ge(dmt.h, dmt.v)
                if ti - 2 in actv:
                    t.wait_ge(evS.h, actv[ti - 2])
                for k in range(KC):
                    mmi = t.matmul(p_h[:], cb["asm"][:, ti % 2, k, :],
                                   cb["XsT"][:, k, tl * P:(tl + 1) * P],
                                   start=(k == 0), stop=(k == KC - 1))
                mm.inc(mmi)

            @block.scalar
            def _(s, ti=ti, sl2=sl2, p_h=p_h):
                s.wait_ge(mm.h, mm.v)
                evS.inc(s.activation(cb["hmask"][:], p_h[:], act_fn,
                                     bias=cb["w2f"][:, sl2, 0:1]))
            actv[ti] = evS.v

            @block.vector
            def _(v, ti=ti):
                v.wait_ge(evS.h, evS.v)
                evV.inc(v.tensor_mul(cb["hp"][:], cb["hmask"][:], cb["bd"][:]))

            @block.tensor
            def _(t, ti=ti):
                t.wait_ge(evV.h, evV.v)
                t.wait_ge(evS.h, evS.v)
                t.matmul(cb["p_o0"][:], cb["hp"][:],
                         cb["w2b"][:, ti % 2, 1:1 + TQ], start=True, stop=True)
                mm.inc(t.matmul(cb["p_o1"][:], cb["hp"][:],
                                cb["w2b"][:, ti % 2, 1 + TQ:1 + D],
                                start=True, stop=True))

            @block.scalar
            def _(s, ti=ti):
                s.wait_ge(mm.h, mm.v)
                sl = ti % 4
                d_ = pr.sem(f"dm{sl}")
                s.wait_ge(d_.h, d_.v)
                do_ = pr.sem(f"dmo{sl}")
                if do_.v:
                    s.wait_ge(do_.h, do_.v)
                s.activation(cb["outst"][:, sl, 0:TQ], cb["p_o0"][:], AF.Identity)
                evS.inc(s.activation(cb["outst"][:, sl, TQ:D], cb["p_o1"][:],
                                     AF.Identity))
            evacv[ti] = evS.v

            @block.sync
            def _(sy, ti=ti):
                sy.wait_ge(evS.h, evS.v)
                sl = ti % 4
                d_ = pr.sem(f"dm{sl}")
                sy.wait_ge(d_.h, d_.v)
                d_.inc(sy.dma_start(out=bf["OutS_d"][ti * P:(ti + 1) * P, :],
                                    in_=cb["outst"][:, sl, :]), 16)

        gather_stage(t0)
        wgsv[t0] = wgs.v
        for ti in range(t0, t1):
            if ti + 1 < t1:
                gather_stage(ti + 1)
                wgsv[ti + 1] = wgs.v
            compute_stage(ti, ti - t0)

    for i in range(NT):
        @block.vector
        def _(v, i=i):
            v.wait_ge(evV.h, evV.v)
            v.wait_ge(pr.sem("scat").h, pr.sem("scat").v)
            evV.inc(v.tensor_copy(cb["oidx"][:], bf["dest"][:, fi, i:i + 1]))

        @block.gpsimd
        def _(g, i=i):
            g.wait_ge(evV.h, evV.v)
            for sl in range(4):
                d_ = pr.sem(f"dm{sl}")
                g.wait_ge(d_.h, d_.v)
            g.wait_ge(scg.h, scg.v)
            scg.inc(g.indirect_dma_start(
                cb["gst"][:, i % 2, :], None, bf["OutS_d"],
                IndirectOffsetOnAxis(ap=cb["oidx"][:], axis=0)), 16)

        @block.tensor
        def _(t, i=i):
            t.wait_ge(ld.h, ld.v)
            t.wait_ge(evV.h, evV.v)
            t.matmul(cb["p_b0"][:], bf["UT"][:, fi, 0, i * P:(i + 1) * P],
                     cb["b2"][:, 0, 0:TQ], start=True, stop=False)
            t.matmul(cb["p_b0"][:], bf["UT"][:, fi, 1, i * P:(i + 1) * P],
                     cb["b2"][:, 1, 0:TQ], start=False, stop=True)
            t.matmul(cb["p_b1"][:], bf["UT"][:, fi, 0, i * P:(i + 1) * P],
                     cb["b2"][:, 0, TQ:D], start=True, stop=False)
            mm.inc(t.matmul(cb["p_b1"][:], bf["UT"][:, fi, 1, i * P:(i + 1) * P],
                            cb["b2"][:, 1, TQ:D], start=False, stop=True))

        @block.vector
        def _(v, i=i):
            v.wait_ge(mm.h, mm.v)
            v.wait_ge(scg.h, scg.v)
            sl = i % 4
            d_ = pr.sem(f"dmo{sl}")
            v.wait_ge(d_.h, d_.v)
            v.tensor_add(cb["outst"][:, sl, 0:TQ], cb["gst"][:, i % 2, 0:TQ],
                         cb["p_b0"][:])
            evV.inc(v.tensor_add(cb["outst"][:, sl, TQ:D],
                                 cb["gst"][:, i % 2, TQ:D], cb["p_b1"][:]))

        @block.sync
        def _(sy, i=i):
            sy.wait_ge(evV.h, evV.v)
            sl = i % 4
            d_ = pr.sem(f"dmo{sl}")
            sy.wait_ge(d_.h, d_.v)
            d_.inc(sy.dma_start(out=out_d[i * P:(i + 1) * P, :],
                                in_=cb["outst"][:, sl, :]), 16)


def emit_barrier(pr, engines=("tensor", "vector", "scalar", "gpsimd", "sync")):
    sems = list(pr._sems.values())
    for en in engines:
        sec = getattr(pr.block, en)

        @sec
        def _(e):
            for s_ in sems:
                if s_.v:
                    e.wait_ge(s_.h, s_.v)



def build_l0_ffn1():
    nc = bass.Bass("TRN2", target_bir_lowering=False, debug=False)
    x1_d = nc.dram_tensor("x1", [T, D], F32, kind="ExternalInput").ap()
    wd = {}
    for f in "qkv":
        wd[f] = dict(
            nwT=nc.dram_tensor(f"{f}_nwT", [D + 1, 255], F32, kind="ExternalInput").ap(),
            w12aug=nc.dram_tensor(f"{f}_w12aug", [4096, 2 * D + 1], F32, kind="ExternalInput").ap(),
            b2=nc.dram_tensor(f"{f}_b2", [M, D], F32, kind="ExternalInput").ap(),
        )
    cst_d = {nm: nc.dram_tensor(nm, list(arr.shape),
                                I32 if arr.dtype == np.int32 else F32,
                                kind="ExternalInput").ap()
             for nm, arr in fff_consts().items()}
    xn_d = nc.dram_tensor("xn", [T, D], F32, kind="ExternalOutput").ap()
    out_d = {f: nc.dram_tensor(f"{f}_out", [T, D], F32, kind="ExternalOutput").ap()
             for f in "qkv"}
    Xs_d = {f: nc.dram_tensor(f"Xs_{f}", [NSLOT, D], BF16).ap() for f in "qkv"}
    gl_d = {f: nc.dram_tensor(f"gl_{f}", [1, GPAD], F32).ap() for f in "qkv"}
    OutS_d = {f: nc.dram_tensor(f"OutS_{f}", [NSLOT, D], F32).ap() for f in "qkv"}

    with contextlib.ExitStack() as st_:
        sb = lambda *a: st_.enter_context(nc.sbuf_tensor(*a))
        ps = lambda *a: st_.enter_context(nc.psum_tensor(*a))
        bf_p = alloc_fff_persist(sb, 3)
        cols = alloc_ln_cols(sb)

        with nc.Block() as block:
            pr = Prog(nc, block, st_)
            ld = pr.sem("ld")
            cst = pr.sem("cst")
            evV = pr.sem("ev_vector")
            evS = pr.sem("ev_scalar")
            mm = pr.sem("mm")

            with contextlib.ExitStack() as ph1:
                sb1 = lambda *a: ph1.enter_context(nc.sbuf_tensor(*a))
                ps1 = lambda *a: ph1.enter_context(nc.psum_tensor(*a))
                bf = dict(alloc_fff_route_bufs(sb1, ps1), **bf_p)
                x_s = sb1("x_s", [P, NT, D], F32)
                xnT_s = sb1("xnT_s", [P, KC, T], BF16)
                xnT_lo = sb1("xnT_lo", [P, KC, T], BF16)
                nwT_s = sb1("nwT_s", [P, KC, 255], BF16)
                nwT_lo = sb1("nwT_lo", [P, KC, 255], BF16)
                nb_f = sb1("nb_f", [1, 255], F32)
                nb_row = sb1("nb_row", [1, 255], BF16)
                nb_lo = sb1("nb_lo", [1, 255], BF16)
                p_tr0 = ps1("p_tr0", [P, P], F32)
                p_tr1 = ps1("p_tr1", [P, P], F32)

                @block.gpsimd
                def _(g):
                    ld.inc(g.dma_start(out=x_s[:],
                                       in_=x1_d.rearrange("(n p) d -> p n d", p=P)), 16)
                    for nm in cst_d:
                        tgt = {"triu_strict": bf["triu"], "blockdiag": None,
                               "g16col": bf["g16"], "iota256col": bf["iota2"],
                               "posmap": None, "jcol": None,
                               "identf": bf["identf"]}[nm]
                        if tgt is not None:
                            ld.inc(g.dma_start(out=tgt[:], in_=cst_d[nm]), 16)
                    ld.inc(g.dma_start(out=bf["identb"][:], in_=cst_d["identf"]), 16)

                @block.vector
                def _(v):
                    v.memset(bf["onesf"][:], 1.0)
                    v.memset(bf["onesfbc"][:], 1.0)
                    cst.inc(v.memset(bf["onesbc1"][:], 1.0))
                    v.wait_ge(ld.h, ld.v)
                    for c2 in range(2):
                        cst.inc(v.tensor_copy(bf["rhs2"][:, c2, 1:2],
                                              bf["iota2"][:, c2:c2 + 1]))

                ln_o = emit_ln(pr, lambda i: x_s[:, i, :], list(range(NT)),
                               cols, 1e-5, entry=[(ld, ld.v)])
                ln_s = emit_ln(pr, lambda i: x_s[:, i, :], list(range(NT)),
                               cols, 1e-5, entry=[(ln_o, ln_o.v)], sd_mode=True)

                @block.sync
                def _(sy):
                    sy.wait_ge(ln_s.h, ln_s.v)
                    pr.sem("dmx").inc(sy.dma_start(
                        out=xn_d.rearrange("(n p) d -> p n d", p=P), in_=x_s[:]), 16)

                tr_mm = pr.sem("trmm")
                tr_ev = pr.sem("trev")
                trjobs = [(i, k) for i in range(NT) for k in range(KC)]

                @block.tensor
                def _(t):
                    t.wait_ge(ln_s.h, ln_s.v)
                    t.wait_ge(ld.h, ld.v)
                    for j, (i, k) in enumerate(trjobs):
                        if j >= 2:
                            t.wait_ge(tr_ev.h, 2 * (j - 1))
                        tr_mm.inc(t.transpose(
                            (p_tr0 if j % 2 == 0 else p_tr1)[:],
                            x_s[:, i, k * P:(k + 1) * P], bf["identf"][:]))

                @block.vector
                def _(v):
                    for j, (i, k) in enumerate(trjobs):
                        pm = (p_tr0 if j % 2 == 0 else p_tr1)
                        v.wait_ge(tr_mm.h, j + 1)
                        tr_ev.inc(v.tensor_copy(
                            xnT_s[:, k, i * P:(i + 1) * P], pm[:]))
                        v.wait_ge(tr_ev.h, tr_ev.v)
                        tr_ev.inc(v.scalar_tensor_tensor(
                            xnT_lo[:, k, i * P:(i + 1) * P], pm[:], 1.0,
                            xnT_s[:, k, i * P:(i + 1) * P],
                            ALU.mult, ALU.subtract))

                for fi, f in enumerate("qkv"):
                    nwT_f = bf["lg"][:, 0:KC, 0:255]

                    @block.gpsimd
                    def _(g, f=f, nwT_f=nwT_f):
                        g.wait_ge(ld.h, ld.v)
                        g.wait_ge(mm.h, mm.v)
                        g.wait_ge(evV.h, evV.v)
                        ld.inc(g.dma_start(
                            out=nwT_f,
                            in_=wd[f]["nwT"][0:D, :].rearrange("(k p) n -> p k n", p=P)), 16)
                        ld.inc(g.dma_start(out=nb_f[:],
                                           in_=wd[f]["nwT"][D:D + 1, :]), 16)

                    @block.vector
                    def _(v, nwT_f=nwT_f):
                        v.wait_ge(ld.h, ld.v)
                        evV.inc(v.tensor_copy(nwT_s[:], nwT_f))
                        evV.inc(v.tensor_copy(nb_row[:], nb_f[:]))
                        v.wait_ge(evV.h, evV.v)
                        evV.inc(v.scalar_tensor_tensor(
                            nwT_lo[:], nwT_f, 1.0, nwT_s[:],
                            ALU.mult, ALU.subtract))
                        evV.inc(v.scalar_tensor_tensor(
                            nb_lo[:], nb_f[:], 1.0, nb_row[:],
                            ALU.mult, ALU.subtract))

                    emit_fff_route(pr, bf, fi, lambda i: x_s[:, i, :],
                                   xnT_s, xnT_lo, nwT_s, nwT_lo, nb_row, nb_lo,
                                   Xs_d[f], gl_d[f],
                                   entry=[(tr_ev, tr_ev.v), (ld, ld.v),
                                          (cst, cst.v), (evS, evS.v),
                                          (evV, evV.v)])

                emit_barrier(pr)

            with contextlib.ExitStack() as ph2:
                sb2 = lambda *a: ph2.enter_context(nc.sbuf_tensor(*a))
                ps2 = lambda *a: ph2.enter_context(nc.psum_tensor(*a))
                cb = alloc_fff_comp_bufs(sb2, ps2, 23)
                bf = dict(bf_p)
                bf["act_fn"] = AF.Relu
                block = pr.block

                @block.gpsimd
                def _(g):
                    g.wait_ge(ld.h, ld.v)
                    ld.inc(g.dma_start(out=cb["posmap"][:], in_=cst_d["posmap"]), 16)
                    ld.inc(g.dma_start(out=cb["jcol"][:], in_=cst_d["jcol"]), 16)
                    ld.inc(g.dma_start(out=cb["bd"][:], in_=cst_d["blockdiag"]), 16)

                for fi, f in enumerate("qkv"):
                    bf["OutS_d"] = OutS_d[f]
                    emit_fff_compute(pr, bf, cb, fi, wd[f]["w12aug"],
                                     wd[f]["b2"], Xs_d[f],
                                     gl_d[f], out_d[f], half=23)

                @block.sync
                def _(sy):
                    for nm, s_ in list(pr._sems.items()):
                        if nm.startswith("dm"):
                            sy.wait_ge(s_.h, s_.v)
    return nc



def build_l0_ffn2(S=8192):
    nc = bass.Bass("TRN2", target_bir_lowering=False, debug=False)
    xn_d = nc.dram_tensor("xn", [T, D], F32, kind="ExternalInput").ap()
    x1_d = nc.dram_tensor("x1", [T, D], F32, kind="ExternalInput").ap()
    qT_d = nc.dram_tensor("qT", [D, T], F32, kind="ExternalInput").ap()
    kT_d = nc.dram_tensor("kT", [D, S], F32, kind="ExternalInput").ap()
    v_d = nc.dram_tensor("vfull", [S, D], F32, kind="ExternalInput").ap()
    fw = dict(
        nwT=nc.dram_tensor("f_nwT", [D + 1, 255], F32, kind="ExternalInput").ap(),
        w12aug=nc.dram_tensor("f_w12aug", [4096, 2 * D + 1], F32, kind="ExternalInput").ap(),
        b2=nc.dram_tensor("f_b2", [M, D], F32, kind="ExternalInput").ap(),
    )
    cst_d = {nm: nc.dram_tensor(nm, list(arr.shape),
                                I32 if arr.dtype == np.int32 else F32,
                                kind="ExternalInput").ap()
             for nm, arr in fff_consts().items()}
    x2_d = nc.dram_tensor("x2", [T, D], F32, kind="ExternalOutput").ap()
    y_dram = nc.dram_tensor("y_scr", [T, D], F32).ap()
    fout_d = nc.dram_tensor("fout_scr", [T, D], F32).ap()
    Xs_f = nc.dram_tensor("Xs_f", [NSLOT, D], BF16).ap()
    gl_f = nc.dram_tensor("gl_f", [1, GPAD], F32).ap()
    OutS_f = nc.dram_tensor("OutS_f", [NSLOT, D], F32).ap()

    NKC = S // TQ

    with contextlib.ExitStack() as st_:
        sb = lambda *a: st_.enter_context(nc.sbuf_tensor(*a))
        ps = lambda *a: st_.enter_context(nc.psum_tensor(*a))
        bf_p = alloc_fff_persist(sb, 1)
        cols = alloc_ln_cols(sb)

        with nc.Block() as block:
            pr = Prog(nc, block, st_)
            ld = pr.sem("ld")
            cst = pr.sem("cst")
            evV = pr.sem("ev_vector")
            evS = pr.sem("ev_scalar")
            mm = pr.sem("mm")
            dmy = [pr.sem(f"dmy{i}") for i in range(2)]

            with contextlib.ExitStack() as phA:
                sbA = lambda *a: phA.enter_context(nc.sbuf_tensor(*a))
                psA = lambda *a: phA.enter_context(nc.psum_tensor(*a))
                S4 = sbA("S4", [P, 4, S], F16)
                qt_s = sbA("qt_s", [P, KC, TQ], BF16)
                kt_s = sbA("kt_s", [P, 2, KC, TQ], BF16)
                vt_s = sbA("vt_s", [P, 2, 4, D], F16)
                xn_t = sbA("xn_t", [P, 2, D], F32)
                at_s = sbA("at_s", [P, P], F16)
                yst = sbA("yst", [P, 2, D], F32)
                dcol = sbA("dcol", [P, 4], F32)
                mcol = sbA("mcol", [P, 1], F32)
                rcol = sbA("rcol", [P, 1], F32)
                p_s0 = psA("p_s0", [P, TQ], F32)
                p_s1 = psA("p_s1", [P, TQ], F32)
                p_tb = psA("p_tb", [P, P], F16)
                p_av0 = psA("p_av0", [P, TQ], F32)
                p_av1 = psA("p_av1", [P, TQ], F32)

                kld = [pr.sem(f"kld{i}") for i in range(2)]
                vld = [pr.sem(f"vld{i}") for i in range(2)]

                @block.gpsimd
                def _(g):
                    ld.inc(g.dma_start(out=bf_p["identh"][:],
                                       in_=cst_d["identf"]), 16)

                for grp in range(4):
                    @block.gpsimd
                    def _(g, grp=grp):
                        g.wait_ge(ld.h, ld.v)
                        if grp:
                            g.wait_ge(mm.h, mm.v)
                        ld.inc(g.dma_start(
                            out=qt_s[:],
                            in_=qT_d[:, grp * TQ:(grp + 1) * TQ]
                            .rearrange("(k p) t -> p k t", p=P)), 16)

                    for ktc in range(NKC):
                        sl = ktc % 2

                        @block.gpsimd
                        def _(g, grp=grp, ktc=ktc, sl=sl):
                            g.wait_ge(kld[sl].h, kld[sl].v)
                            if ktc >= 2:
                                g.wait_ge(mm.h, mm.v)
                            kld[sl].inc(g.dma_start(
                                out=kt_s[:, sl, :, :],
                                in_=kT_d[:, ktc * TQ:(ktc + 1) * TQ]
                                .rearrange("(k p) t -> p k t", p=P)), 16)

                        for qt in range(4):
                            def s_prod(t, j, pm, grp=grp, ktc=ktc, sl=sl, qt=qt):
                                t.wait_ge(kld[sl].h, kld[sl].v)
                                t.wait_ge(ld.h, ld.v)
                                for k in range(KC):
                                    mmi = t.matmul(
                                        pm[:], qt_s[:, k, qt * P:(qt + 1) * P],
                                        kt_s[:, sl, k, :],
                                        start=(k == 0), stop=(k == KC - 1))
                                return mmi

                            pipe(pr, f"s{grp}_{ktc}_{qt}", [0], [p_s0, p_s1][sl:sl + 1],
                                 s_prod,
                                 lambda e, j, pm, s_, grp=grp, ktc=ktc, qt=qt:
                                 e.tensor_copy(
                                     S4[:, qt, ktc * TQ:(ktc + 1) * TQ], pm[:]),
                                 consumer_engine="vector")

                    for qt in range(4):
                        @block.vector
                        def _(v, qt=qt):
                            v.wait_ge(mm.h, mm.v)
                            v.wait_ge(evS.h, evS.v)
                            evV.inc(v.tensor_reduce(mcol[:], S4[:, qt, :], AX.X,
                                                    ALU.max))
                            v.wait_ge(evV.h, evV.v)
                            evV.inc(v.tensor_scalar_mul(mcol[:], mcol[:],
                                                        -1.0 / 32.0))

                        @block.scalar
                        def _(s, qt=qt):
                            s.wait_ge(evV.h, evV.v)
                            evS.inc(s.activation(S4[:, qt, :], S4[:, qt, :],
                                                 AF.Exp, bias=mcol[:],
                                                 scale=1.0 / 32.0,
                                                 accum_out=dcol[:, qt:qt + 1]))

                    for qt in range(4):
                        @block.vector
                        def _(v, qt=qt):
                            v.wait_ge(evS.h, evS.v)
                            evV.inc(v.reciprocal(rcol[:], dcol[:, qt:qt + 1]))

                        for ktc in range(NKC):
                            sl = ktc % 2
                            if qt == 0 or True:
                                pass

                            @block.gpsimd
                            def _(g, qt=qt, ktc=ktc, sl=sl):
                                g.wait_ge(vld[sl].h, vld[sl].v)
                                g.wait_ge(mm.h, mm.v)
                                vld[sl].inc(g.dma_start(
                                    out=vt_s[:, sl, :, :],
                                    in_=v_d[ktc * TQ:(ktc + 1) * TQ, :]
                                    .rearrange("(a p) d -> p a d", p=P)), 16)

                            for blk in range(4):
                                def av_prod(t, j, pm, qt=qt, ktc=ktc, sl=sl, blk=blk):
                                    t.wait_ge(evS.h, evS.v)
                                    t.wait_ge(vld[sl].h, vld[sl].v)
                                    return t.transpose(
                                        pm[:],
                                        S4[:, qt, ktc * TQ + blk * P:
                                           ktc * TQ + (blk + 1) * P],
                                        bf_p["identh"][:])

                                tr_ev = pipe(pr, f"avt{qt}_{ktc}_{blk}", [0],
                                             [p_tb], av_prod,
                                             lambda e, j, pm, s_: e.tensor_copy(
                                                 at_s[:], pm[:]),
                                             consumer_engine="vector")

                                @block.tensor
                                def _(t, qt=qt, ktc=ktc, sl=sl, blk=blk,
                                      first=(ktc == 0 and blk == 0),
                                      last=(ktc == NKC - 1 and blk == 3)):
                                    t.wait_ge(tr_ev.h, tr_ev.v)
                                    t.wait_ge(vld[sl].h, vld[sl].v)
                                    if first:
                                        t.wait_ge(evS.h, evS.v)
                                    t.matmul(p_av0[:], at_s[:],
                                             vt_s[:, sl, blk, 0:TQ],
                                             start=first, stop=last)
                                    mm.inc(t.matmul(p_av1[:], at_s[:],
                                                    vt_s[:, sl, blk, TQ:D],
                                                    start=first, stop=last))

                        @block.gpsimd
                        def _(g, grp=grp, qt=qt):
                            i = grp * 4 + qt
                            g.wait_ge(dmy[i % 2].h, dmy[i % 2].v)
                            g.wait_ge(evV.h, evV.v)
                            pr.sem("xnld").inc(g.dma_start(
                                out=xn_t[:, i % 2, :],
                                in_=xn_d[i * P:(i + 1) * P, :]), 16)

                        @block.scalar
                        def _(s, grp=grp, qt=qt):
                            i = grp * 4 + qt
                            s.wait_ge(mm.h, mm.v)
                            s.wait_ge(evV.h, evV.v)
                            s.wait_ge(dmy[i % 2].h, dmy[i % 2].v)
                            s.activation(yst[:, i % 2, 0:TQ], p_av0[:],
                                         AF.Identity, scale=rcol[:])
                            evS.inc(s.activation(yst[:, i % 2, TQ:D], p_av1[:],
                                                 AF.Identity, scale=rcol[:]))

                        @block.vector
                        def _(v, grp=grp, qt=qt):
                            i = grp * 4 + qt
                            v.wait_ge(evS.h, evS.v)
                            v.wait_ge(pr.sem("xnld").h, pr.sem("xnld").v)
                            evV.inc(v.tensor_add(yst[:, i % 2, :],
                                                 yst[:, i % 2, :],
                                                 xn_t[:, i % 2, :]))

                        @block.sync
                        def _(sy, grp=grp, qt=qt):
                            i = grp * 4 + qt
                            sy.wait_ge(evV.h, evV.v)
                            dmy[i % 2].inc(sy.dma_start(
                                out=y_dram[i * P:(i + 1) * P, :],
                                in_=yst[:, i % 2, :]), 16)

                emit_barrier(pr)

            with contextlib.ExitStack() as phB:
                sbB = lambda *a: phB.enter_context(nc.sbuf_tensor(*a))
                psB = lambda *a: phB.enter_context(nc.psum_tensor(*a))
                bf = dict(alloc_fff_route_bufs(sbB, psB), **bf_p)
                y_s = sbB("y_s", [P, NT, D], F32)
                yT_s = sbB("yT_s", [P, KC, T], BF16)
                yT_lo = sbB("yT_lo", [P, KC, T], BF16)
                nwT_s = sbB("nwT_s", [P, KC, 255], BF16)
                nwT_lo = sbB("nwT_lo", [P, KC, 255], BF16)
                nb_f = sbB("nb_f", [1, 255], F32)
                nb_row = sbB("nb_row", [1, 255], BF16)
                nb_lo = sbB("nb_lo", [1, 255], BF16)
                p_tr0 = psB("p_tr0", [P, P], F32)
                p_tr1 = psB("p_tr1", [P, P], F32)
                nwT_f = bf["lg"][:, 0:KC, 0:255]

                @block.gpsimd
                def _(g, nwT_f=nwT_f):
                    g.wait_ge(ld.h, ld.v)
                    ld.inc(g.dma_start(out=y_s[:],
                                       in_=y_dram.rearrange("(n p) d -> p n d", p=P)), 16)
                    for nm in ("triu_strict", "g16col", "iota256col", "identf"):
                        tgt = {"triu_strict": bf["triu"], "g16col": bf["g16"],
                               "iota256col": bf["iota2"], "identf": bf["identf"]}[nm]
                        ld.inc(g.dma_start(out=tgt[:], in_=cst_d[nm]), 16)
                    ld.inc(g.dma_start(out=bf["identb"][:], in_=cst_d["identf"]), 16)
                    ld.inc(g.dma_start(out=nwT_f,
                                       in_=fw["nwT"][0:D, :].rearrange("(k p) n -> p k n", p=P)), 16)
                    ld.inc(g.dma_start(out=nb_f[:], in_=fw["nwT"][D:D + 1, :]), 16)

                @block.vector
                def _(v, nwT_f=nwT_f):
                    v.memset(bf["onesf"][:], 1.0)
                    v.memset(bf["onesfbc"][:], 1.0)
                    cst.inc(v.memset(bf["onesbc1"][:], 1.0))
                    v.wait_ge(ld.h, ld.v)
                    for c2 in range(2):
                        cst.inc(v.tensor_copy(bf["rhs2"][:, c2, 1:2],
                                              bf["iota2"][:, c2:c2 + 1]))
                    evV.inc(v.tensor_copy(nwT_s[:], nwT_f))
                    evV.inc(v.tensor_copy(nb_row[:], nb_f[:]))
                    v.wait_ge(evV.h, evV.v)
                    evV.inc(v.scalar_tensor_tensor(nwT_lo[:], nwT_f, 1.0,
                                                   nwT_s[:], ALU.mult,
                                                   ALU.subtract))
                    evV.inc(v.scalar_tensor_tensor(nb_lo[:], nb_f[:], 1.0,
                                                   nb_row[:], ALU.mult,
                                                   ALU.subtract))

                tr_mm = pr.sem("trmm")
                tr_ev = pr.sem("trev")
                trjobs = [(i, k) for i in range(NT) for k in range(KC)]

                @block.tensor
                def _(t):
                    t.wait_ge(ld.h, ld.v)
                    for j, (i, k) in enumerate(trjobs):
                        if j >= 2:
                            t.wait_ge(tr_ev.h, 2 * (j - 1))
                        tr_mm.inc(t.transpose(
                            (p_tr0 if j % 2 == 0 else p_tr1)[:],
                            y_s[:, i, k * P:(k + 1) * P], bf["identf"][:]))

                @block.vector
                def _(v):
                    for j, (i, k) in enumerate(trjobs):
                        pm = (p_tr0 if j % 2 == 0 else p_tr1)
                        v.wait_ge(tr_mm.h, j + 1)
                        tr_ev.inc(v.tensor_copy(
                            yT_s[:, k, i * P:(i + 1) * P], pm[:]))
                        v.wait_ge(tr_ev.h, tr_ev.v)
                        tr_ev.inc(v.scalar_tensor_tensor(
                            yT_lo[:, k, i * P:(i + 1) * P], pm[:], 1.0,
                            yT_s[:, k, i * P:(i + 1) * P],
                            ALU.mult, ALU.subtract))

                emit_fff_route(pr, bf, 0, lambda i: y_s[:, i, :],
                               yT_s, yT_lo, nwT_s, nwT_lo, nb_row, nb_lo,
                               Xs_f, gl_f,
                               entry=[(tr_ev, tr_ev.v), (ld, ld.v),
                                      (cst, cst.v), (evV, evV.v)])
                emit_barrier(pr)

            with contextlib.ExitStack() as phC:
                sbC = lambda *a: phC.enter_context(nc.sbuf_tensor(*a))
                psC = lambda *a: phC.enter_context(nc.psum_tensor(*a))
                cb = alloc_fff_comp_bufs(sbC, psC, 23)
                bf = dict(bf_p)
                bf["act_fn"] = AF.Gelu
                bf["OutS_d"] = OutS_f

                @block.gpsimd
                def _(g):
                    g.wait_ge(ld.h, ld.v)
                    ld.inc(g.dma_start(out=cb["posmap"][:], in_=cst_d["posmap"]), 16)
                    ld.inc(g.dma_start(out=cb["jcol"][:], in_=cst_d["jcol"]), 16)
                    ld.inc(g.dma_start(out=cb["bd"][:], in_=cst_d["blockdiag"]), 16)

                emit_fff_compute(pr, bf, cb, 0, fw["w12aug"], fw["b2"], Xs_f,
                                 gl_f, fout_d, half=23)
                emit_barrier(pr)

            with contextlib.ExitStack() as phD:
                sbD = lambda *a: phD.enter_context(nc.sbuf_tensor(*a))
                sub_s = sbD("sub_s", [P, NT, D], F32)
                str_t = sbD("str_t", [P, 2, D], F32)
                x1_t = sbD("x1_t", [P, 2, D], F32)
                xh_t = sbD("xh_t", [P, 2, D], F32)
                stg = sbD("stg", [P, 4, D], F32)
                dms = [pr.sem(f"dmD{i}") for i in range(4)]
                yl = [pr.sem(f"yl{i}") for i in range(2)]
                fl = [pr.sem(f"fl{i}") for i in range(2)]
                x1l = [pr.sem(f"x1l{i}") for i in range(2)]

                @block.gpsimd
                def _(g):
                    g.wait_ge(ld.h, ld.v)
                    base = evV.v
                    for i in range(NT):
                        sl = i % 2
                        if i >= 2:
                            g.wait_ge(evV.h, base + (i - 1))
                        g.wait_ge(yl[sl].h, yl[sl].v)
                        yl[sl].inc(g.dma_start(out=str_t[:, sl, :],
                                               in_=y_dram[i * P:(i + 1) * P, :]), 16)
                        g.wait_ge(fl[sl].h, fl[sl].v)
                        fl[sl].inc(g.dma_start(out=sub_s[:, i, :],
                                               in_=fout_d[i * P:(i + 1) * P, :]), 16)

                @block.vector
                def _(v):
                    for i in range(NT):
                        sl = i % 2
                        v.wait_ge(yl[sl].h, 16 * (i // 2 + 1))
                        v.wait_ge(fl[sl].h, 16 * (i // 2 + 1))
                        evV.inc(v.tensor_add(sub_s[:, i, :], sub_s[:, i, :],
                                             str_t[:, sl, :]))

                ln_f = emit_ln(pr, lambda i: sub_s[:, i, :], list(range(NT)),
                               cols, 1e-5, entry=[(evV, evV.v)], apply=False)

                @block.gpsimd
                def _(g):
                    base = evV.v
                    for i in range(NT):
                        sl = i % 2
                        if i >= 2:
                            g.wait_ge(evV.h, base + (i - 1))
                        g.wait_ge(x1l[sl].h, x1l[sl].v)
                        x1l[sl].inc(g.dma_start(out=x1_t[:, sl, :],
                                                in_=x1_d[i * P:(i + 1) * P, :]), 16)

                for i in range(NT):
                    @block.scalar
                    def _(s, i=i):
                        s.wait_ge(ln_f.h, ln_f.v)
                        if i >= 2:
                            s.wait_ge(evV.h, evV.v)
                        evS.inc(s.activation(xh_t[:, i % 2, :], sub_s[:, i, :],
                                             AF.Identity,
                                             bias=cols["nmr"][:, i:i + 1],
                                             scale=cols["rstd"][:, i:i + 1]))

                    @block.vector
                    def _(v, i=i):
                        v.wait_ge(evS.h, evS.v)
                        v.wait_ge(x1l[i % 2].h, 16 * (i // 2 + 1))
                        sl = i % 4
                        v.wait_ge(dms[sl].h, dms[sl].v)
                        evV.inc(v.tensor_add(stg[:, sl, :], xh_t[:, i % 2, :],
                                             x1_t[:, i % 2, :]))

                    @block.sync
                    def _(sy, i=i):
                        sy.wait_ge(evV.h, evV.v)
                        sl = i % 4
                        sy.wait_ge(dms[sl].h, dms[sl].v)
                        dms[sl].inc(sy.dma_start(out=x2_d[i * P:(i + 1) * P, :],
                                                 in_=stg[:, sl, :]), 16)

                @block.sync
                def _(sy):
                    for nm, s_ in list(pr._sems.items()):
                        if nm.startswith("dm"):
                            sy.wait_ge(s_.h, s_.v)
    return nc



TH = T + P
NTH = TH // P


def build_l1_pre():
    nc = bass.Bass("TRN2", target_bir_lowering=False, debug=False)
    xh_d = nc.dram_tensor("xh", [TH, D], F32, kind="ExternalInput").ap()
    wT_d = {nm: nc.dram_tensor(nm, [D, D], F32, kind="ExternalInput").ap()
            for nm in ("WqT", "WkT", "WvT", "WgT")}
    cv_d = {nm: nc.dram_tensor(nm, [D, 4], F32, kind="ExternalInput").ap()
            for nm in ("conv_q", "conv_k", "conv_v")}
    ident_d = nc.dram_tensor("identf", [P, P], F32, kind="ExternalInput").ap()
    out_d = {nm: nc.dram_tensor(nm, [T, D], F32, kind="ExternalOutput").ap()
             for nm in ("q_out", "k_out", "v_out", "gate")}

    with contextlib.ExitStack() as st_:
        sb = lambda *a: st_.enter_context(nc.sbuf_tensor(*a))
        ps = lambda *a: st_.enter_context(nc.psum_tensor(*a))
        x_s = sb("x_s", [P, NTH, D], F32)
        lnT_s = sb("lnT_s", [P, KC, TH], BF16)
        w_s = sb("w_s", [P, KC, D], BF16)
        cv_s = sb("cv_s", [P, KC, 4], F32)
        cv1_s = sb("cv1_s", [P, KC], F32)
        identf_s = sb("identf_s", [P, P], F32)
        ident_s = sb("ident_s", [P, P], BF16)
        pt_s = sb("pt_s", [P, TH], F32)
        ac_s = sb("ac_s", [P, T], F32)
        acb_s = sb("acb_s", [P, T], BF16)
        tok_s = sb("tok_s", [P, NT, P], F32)
        sscol = sb("sscol", [P, 2 * NT], F32)
        stage = sb("stage", [P, 4, P], F32)
        gstage_s = sb("gstage", [P, 4, TQ], F32)
        cols = alloc_ln_cols(sb, nt=NTH)
        p_mm0 = ps("p_mm0", [P, TQ], F32)
        p_mm1 = ps("p_mm1", [P, TQ], F32)
        p_tr0 = ps("p_tr0", [P, P], F32)
        p_tr1 = ps("p_tr1", [P, P], F32)

        with nc.Block() as block:
            pr = Prog(nc, block, st_)
            ld = pr.sem("ld")
            wld = pr.sem("wld")
            evV = pr.sem("ev_vector")
            evS = pr.sem("ev_scalar")
            mm = pr.sem("mm")
            dms = [pr.sem(f"dm{i}") for i in range(4)]

            @block.gpsimd
            def _(g):
                ld.inc(g.dma_start(out=x_s[:],
                                   in_=xh_d.rearrange("(n p) d -> p n d", p=P)), 16)
                ld.inc(g.dma_start(out=identf_s[:], in_=ident_d), 16)
                ld.inc(g.dma_start(out=ident_s[:], in_=ident_d), 16)

            ln1 = emit_ln(pr, lambda i: x_s[:, i, :], list(range(NTH)), cols,
                          1e-5, entry=[(ld, ld.v)])

            tr_ev = pipe(
                pr, "tr", [(i, k) for i in range(NTH) for k in range(KC)],
                [p_tr0, p_tr1],
                lambda t, job, pm: t.transpose(
                    pm[:], x_s[:, job[0], job[1] * P:(job[1] + 1) * P],
                    identf_s[:]),
                lambda e, job, pm, sl: e.tensor_copy(
                    lnT_s[:, job[1], job[0] * P:(job[0] + 1) * P], pm[:]),
                t_entry=[(ln1, ln1.v), (ld, ld.v)], consumer_engine="vector")

            for si, (wnm, cnm, onm, do_norm) in enumerate(
                    (("WqT", "conv_q", "q_out", True),
                     ("WkT", "conv_k", "k_out", True),
                     ("WvT", "conv_v", "v_out", False))):
                @block.gpsimd
                def _(g, wnm=wnm, cnm=cnm):
                    g.wait_ge(wld.h, wld.v)
                    g.wait_ge(mm.h, mm.v)
                    wld.inc(g.dma_start(
                        out=w_s[:], in_=wT_d[wnm].rearrange("(k p) d -> p k d", p=P)), 16)
                    wld.inc(g.dma_start(
                        out=cv_s[:], in_=cv_d[cnm].rearrange("(k p) j -> p k j", p=P)), 16)

                @block.vector
                def _(v):
                    v.wait_ge(wld.h, wld.v)
                    v.wait_ge(evS.h, evS.v)
                    evV.inc(v.tensor_scalar(
                        cv1_s[:], cv_s[:, :, 3:4].rearrange("p k a -> p (k a)"),
                        1.0, None, ALU.add))

                for hp in range(KC):
                    def pj_prod(t, tc, pm, hp=hp):
                        for k in range(KC):
                            mmi = t.matmul(pm[:], w_s[:, k, hp * P:(hp + 1) * P],
                                           lnT_s[:, k, tc * P:(tc + 1) * P],
                                           start=(k == 0), stop=(k == KC - 1))
                        return mmi

                    pj_ev = pipe(pr, f"pj{si}_{hp}", list(range(NTH)),
                                 [p_tr0, p_tr1], pj_prod,
                                 lambda e, tc, pm, sl: e.copy(
                                     pt_s[:, tc * P:(tc + 1) * P], pm[:]),
                                 t_entry=[(tr_ev, tr_ev.v), (wld, wld.v),
                                          (evV, evV.v), (evS, evS.v)])

                    @block.vector
                    def _(v, hp=hp):
                        v.wait_ge(pj_ev.h, pj_ev.v)
                        evV.inc(v.tensor_scalar(ac_s[:], pt_s[:, P:TH],
                                                cv1_s[:, hp:hp + 1], None,
                                                ALU.mult))
                        for j, sh in ((2, 1), (1, 2), (0, 3)):
                            v.wait_ge(evV.h, evV.v)
                            evV.inc(v.scalar_tensor_tensor(
                                ac_s[:], pt_s[:, P - sh:TH - sh],
                                cv_s[:, hp, j:j + 1], ac_s[:], ALU.mult, ALU.add))

                    @block.scalar
                    def _(s, hp=hp):
                        s.wait_ge(evV.h, evV.v)
                        evS.inc(s.activation(ac_s[:], ac_s[:], AF.Silu))
                        s.wait_ge(evS.h, evS.v)
                        evS.inc(s.activation(acb_s[:], ac_s[:], AF.Identity))

                    tt_ev = pipe(
                        pr, f"tt{si}_{hp}", list(range(NT)), [p_tr0, p_tr1],
                        lambda t, i, pm: t.transpose(
                            pm[:], ac_s[:, i * P:(i + 1) * P], identf_s[:]),
                        lambda e, i, pm, sl: e.tensor_copy(tok_s[:, i, :], pm[:]),
                        t_entry=[(evS, evS.v)], consumer_engine="vector")

                    if do_norm:
                        for hh in range(2):
                            @block.scalar
                            def _(s, hh=hh, base_v=tt_ev.v):
                                s.wait_ge(tt_ev.h, base_v)
                                for i in range(NT):
                                    s.wait_ge(evS.h, evS.v)
                                    evS.inc(s.activation(
                                        cols["trash"][:, 0:DH],
                                        tok_s[:, i, hh * DH:(hh + 1) * DH],
                                        AF.Square,
                                        accum_out=sscol[:, hh * NT + i:hh * NT + i + 1]))

                        @block.vector
                        def _(v):
                            v.wait_ge(evS.h, evS.v)
                            evV.inc(v.tensor_scalar(sscol[:], sscol[:], 1e-12,
                                                    None, ALU.add))

                        @block.scalar
                        def _(s):
                            s.wait_ge(evV.h, evV.v)
                            evS.inc(s.activation(sscol[:], sscol[:], AF.Sqrt))

                        @block.vector
                        def _(v):
                            v.wait_ge(evS.h, evS.v)
                            evV.inc(v.reciprocal(sscol[:], sscol[:]))

                    for i in range(NT):
                        @block.scalar
                        def _(s, i=i, hp=hp, do_norm=do_norm, onm=onm):
                            s.wait_ge(evV.h, evV.v)
                            s.wait_ge(tt_ev.h, tt_ev.v)
                            sl = i % 4
                            s.wait_ge(dms[sl].h, dms[sl].v)
                            if do_norm:
                                s.activation(stage[:, sl, 0:DH],
                                             tok_s[:, i, 0:DH], AF.Identity,
                                             scale=sscol[:, i:i + 1])
                                evS.inc(s.activation(
                                    stage[:, sl, DH:P], tok_s[:, i, DH:P],
                                    AF.Identity,
                                    scale=sscol[:, NT + i:NT + i + 1]))
                            else:
                                evS.inc(s.activation(stage[:, sl, :],
                                                     tok_s[:, i, :], AF.Identity))

                        @block.sync
                        def _(sy, i=i, hp=hp, onm=onm):
                            sy.wait_ge(evS.h, evS.v)
                            sl = i % 4
                            sy.wait_ge(dms[sl].h, dms[sl].v)
                            dms[sl].inc(sy.dma_start(
                                out=out_d[onm][i * P:(i + 1) * P,
                                               hp * P:(hp + 1) * P],
                                in_=stage[:, sl, :]), 16)

            @block.gpsimd
            def _(g):
                g.wait_ge(wld.h, wld.v)
                g.wait_ge(mm.h, mm.v)
                wld.inc(g.dma_start(
                    out=w_s[:], in_=wT_d["WgT"].rearrange("(k p) d -> p k d", p=P)), 16)

            def gt_prod(t, job, pm):
                i, dq = job
                t.wait_ge(wld.h, wld.v)
                for k in range(KC):
                    mmi = t.matmul(pm[:], lnT_s[:, k, (i + 1) * P:(i + 2) * P],
                                   w_s[:, k, dq * TQ:(dq + 1) * TQ],
                                   start=(k == 0), stop=(k == KC - 1))
                return mmi

            def gt_dma(sy, job, sl):
                i, dq = job
                return sy.dma_start(
                    out=out_d["gate"][i * P:(i + 1) * P, dq * TQ:(dq + 1) * TQ],
                    in_=gstage_s[:, sl, :])

            pipe(pr, "gt", [(i, dq) for i in range(NT) for dq in range(2)],
                 [p_mm0, p_mm1], gt_prod,
                 lambda e, job, pm, sl: e.tensor_copy(gstage_s[:, sl, :], pm[:]),
                 t_entry=[(tr_ev, tr_ev.v), (evS, evS.v), (evV, evV.v)],
                 consumer_engine="vector", dma_emit=gt_dma, nstage=4,
                 stage_buf=None)

            @block.sync
            def _(sy):
                for nm, s_ in list(pr._sems.items()):
                    if nm.startswith("dm"):
                        sy.wait_ge(s_.h, s_.v)
    return nc



def build_l1_scan(NU=4, S=8192):
    NCH = S // P
    nc = bass.Bass("TRN2", target_bir_lowering=False, debug=False)
    qt_d = nc.dram_tensor("qT_u", [NU, DH, S], F32, kind="ExternalInput").ap()
    kt_d = nc.dram_tensor("kT_u", [NU, DH, S], F32, kind="ExternalInput").ap()
    ktok_d = nc.dram_tensor("ktok_u", [NU, S, DH], F32, kind="ExternalInput").ap()
    v_d = nc.dram_tensor("v_u", [NU, S, DH], F32, kind="ExternalInput").ap()
    w0_d = nc.dram_tensor("W0_u", [NU, DH, DH], F32, kind="ExternalInput").ap()
    lng_d = nc.dram_tensor("lng", [1, DH], F32, kind="ExternalInput").ap()
    lnb_d = nc.dram_tensor("lnb", [1, DH], F32, kind="ExternalInput").ap()
    mask_d = nc.dram_tensor("mask_ut", [P, P], F32, kind="ExternalInput").ap()
    out_d = nc.dram_tensor("o_u", [NU, S, DH], F32, kind="ExternalOutput").ap()

    with contextlib.ExitStack() as st_:
        sb = lambda *a: st_.enter_context(nc.sbuf_tensor(*a))
        ps = lambda *a: st_.enter_context(nc.psum_tensor(*a))
        qt_s = sb("qt_s", [DH, S], BF16)
        kt_s = sb("kt_s", [DH, S], BF16)
        ktok_s = sb("ktok_s", [P, NCH, DH], BF16)
        v_s = sb("v_s", [P, NCH, DH], BF16)
        mask_s = sb("mask_s", [P, P], BF16)
        onesf_bc = sb("onesf_bc", [1, P], F32)
        a_bf = sb("a_bf", [P, P], BF16)
        state_s = sb("state_s", [DH, DH], F32)
        state_bf = sb("state_bf", [DH, DH], BF16)
        Bg = sb("Bg", [P, DH], F32)
        Bb = sb("Bb", [P, DH], F32)
        lng_s = sb("lng_s", [1, DH], F32)
        lnb_s = sb("lnb_s", [1, DH], F32)
        xh_s = sb("xh_s", [P, DH], F32)
        stage = sb("stage", [P, 4, DH], F32)
        c_sum = sb("c_sum", [P, 1], F32)
        c_sq = sb("c_sq", [P, 1], F32)
        c_mu = sb("c_mu", [P, 1], F32)
        c_var = sb("c_var", [P, 1], F32)
        c_rstd = sb("c_rstd", [P, 1], F32)
        c_nmr = sb("c_nmr", [P, 1], F32)
        trash = sb("trash", [P, DH], BF16)
        p_a = ps("p_a", [P, P], F32)
        p_o = ps("p_o", [P, DH], F32)
        p_u = ps("p_u", [DH, DH], F32)
        p_g = ps("p_g", [P, DH], F32)

        with nc.Block() as block:
            pr = Prog(nc, block, st_)
            ld = pr.sem("ld")
            mm = pr.sem("mm")
            evV = pr.sem("ev_vector")
            evS = pr.sem("ev_scalar")
            cst = pr.sem("cst")
            dms = [pr.sem(f"dm{i}") for i in range(4)]

            @block.gpsimd
            def _(g):
                ld.inc(g.dma_start(out=mask_s[:], in_=mask_d), 16)
                ld.inc(g.dma_start(out=lng_s[:], in_=lng_d), 16)
                ld.inc(g.dma_start(out=lnb_s[:], in_=lnb_d), 16)

            @block.vector
            def _(v):
                cst.inc(v.memset(onesf_bc[:], 1.0))

            g_ev = pipe(pr, "g", [0, 1], [p_g],
                        lambda t, j, pm: t.matmul(
                            pm[:], onesf_bc[:],
                            (lng_s if j == 0 else lnb_s)[:],
                            start=True, stop=True),
                        lambda e, j, pm, sl: e.tensor_copy(
                            (Bg if j == 0 else Bb)[:], pm[:]),
                        t_entry=[(ld, ld.v), (cst, cst.v)],
                        consumer_engine="vector")

            for u in range(NU):
                @block.gpsimd
                def _(g, u=u):
                    g.wait_ge(ld.h, ld.v)
                    g.wait_ge(mm.h, mm.v)
                    ld.inc(g.dma_start(out=qt_s[:], in_=qt_d[u]), 16)
                    ld.inc(g.dma_start(out=kt_s[:], in_=kt_d[u]), 16)
                    ld.inc(g.dma_start(out=ktok_s[:],
                                       in_=ktok_d[u].rearrange("(n p) d -> p n d", p=P)), 16)
                    ld.inc(g.dma_start(out=v_s[:],
                                       in_=v_d[u].rearrange("(n p) d -> p n d", p=P)), 16)
                    g.wait_ge(evS.h, evS.v)
                    ld.inc(g.dma_start(out=state_s[:], in_=w0_d[u]), 16)

                @block.scalar
                def _(s):
                    s.wait_ge(ld.h, ld.v)
                    evS.inc(s.activation(state_bf[:], state_s[:], AF.Identity))

                for c in range(NCH):
                    tok = slice(c * P, (c + 1) * P)

                    @block.tensor
                    def _(t, u=u, c=c, tok=tok):
                        t.wait_ge(ld.h, ld.v)
                        t.wait_ge(evV.h, evV.v)
                        t.wait_ge(evS.h, evS.v)
                        mm.inc(t.matmul(p_a[:], kt_s[:, tok], qt_s[:, tok],
                                        start=True, stop=True))

                    @block.vector
                    def _(v, c=c):
                        v.wait_ge(mm.h, mm.v)
                        evV.inc(v.tensor_tensor(a_bf[:], p_a[:], mask_s[:],
                                                ALU.mult))

                    @block.tensor
                    def _(t, c=c, tok=tok):
                        t.wait_ge(evV.h, evV.v)
                        t.matmul(p_o[:], a_bf[:], v_s[:, c, :],
                                 start=True, stop=False)
                        mm.inc(t.matmul(p_o[:], qt_s[:, tok], state_bf[:],
                                        start=False, stop=True))
                        mm.inc(t.matmul(p_u[:], ktok_s[:, c, :], v_s[:, c, :],
                                        start=True, stop=True))

                    @block.vector
                    def _(v, c=c):
                        v.wait_ge(mm.h, mm.v - 1)
                        v.wait_ge(evS.h, evS.v)
                        evV.inc(v.reduce_sum(c_sum[:], p_o[:], AX.X))

                    @block.scalar
                    def _(s, c=c):
                        s.wait_ge(mm.h, mm.v - 1)
                        s.wait_ge(evV.h, evV.v)
                        evS.inc(s.activation(trash[:], p_o[:], AF.Square,
                                             accum_out=c_sq[:]))

                    @block.vector
                    def _(v, c=c):
                        v.wait_ge(evS.h, evS.v)
                        evV.inc(v.tensor_scalar_mul(c_mu[:], c_sum[:], 1.0 / DH))
                        v.wait_ge(evV.h, evV.v)
                        v.tensor_tensor(c_var[:], c_mu[:], c_mu[:], ALU.mult)
                        evV.inc(v.tensor_scalar(c_sq[:], c_sq[:], 1.0 / DH,
                                                None, ALU.mult))
                        v.wait_ge(evV.h, evV.v)
                        evV.inc(v.tensor_sub(c_var[:], c_sq[:], c_var[:]))
                        v.wait_ge(evV.h, evV.v)
                        evV.inc(v.tensor_scalar(c_var[:], c_var[:], 1e-5,
                                                None, ALU.add))

                    @block.scalar
                    def _(s, c=c):
                        s.wait_ge(evV.h, evV.v)
                        evS.inc(s.activation(c_rstd[:], c_var[:], AF.Sqrt))

                    @block.vector
                    def _(v, c=c):
                        v.wait_ge(evS.h, evS.v)
                        evV.inc(v.reciprocal(c_rstd[:], c_rstd[:]))
                        v.wait_ge(evV.h, evV.v)
                        evV.inc(v.scalar_tensor_tensor(c_nmr[:], c_mu[:], -1.0,
                                                       c_rstd[:], ALU.mult,
                                                       ALU.mult))

                    @block.scalar
                    def _(s, c=c):
                        s.wait_ge(evV.h, evV.v)
                        evS.inc(s.activation(xh_s[:], p_o[:], AF.Identity,
                                             bias=c_nmr[:], scale=c_rstd[:]))

                    @block.vector
                    def _(v, u=u, c=c):
                        v.wait_ge(evS.h, evS.v)
                        v.wait_ge(g_ev.h, g_ev.v)
                        v.wait_ge(mm.h, mm.v)
                        sl = c % 4
                        v.wait_ge(dms[sl].h, dms[sl].v)
                        evV.inc(v.tensor_mul(stage[:, sl, :], xh_s[:], Bg[:]))
                        v.wait_ge(evV.h, evV.v)
                        evV.inc(v.tensor_add(stage[:, sl, :], stage[:, sl, :],
                                             Bb[:]))
                        evV.inc(v.tensor_add(state_s[:], state_s[:], p_u[:]))

                    @block.scalar
                    def _(s, c=c):
                        s.wait_ge(evV.h, evV.v)
                        evS.inc(s.activation(state_bf[:], state_s[:], AF.Identity))

                    @block.sync
                    def _(sy, u=u, c=c):
                        sy.wait_ge(evV.h, evV.v)
                        sl = c % 4
                        sy.wait_ge(dms[sl].h, dms[sl].v)
                        dms[sl].inc(sy.dma_start(
                            out=out_d[u, c * P:(c + 1) * P, :],
                            in_=stage[:, sl, :]), 16)

            @block.sync
            def _(sy):
                for d_ in dms:
                    sy.wait_ge(d_.h, d_.v)
    return nc



def build_l1_post():
    nc = bass.Bass("TRN2", target_bir_lowering=False, debug=False)
    o_d = nc.dram_tensor("o", [T, D], F32, kind="ExternalInput").ap()
    gate_d = nc.dram_tensor("gate", [T, D], F32, kind="ExternalInput").ap()
    x2_d = nc.dram_tensor("x2", [T, D], F32, kind="ExternalInput").ap()
    woT_d = nc.dram_tensor("WoT", [D, D], F32, kind="ExternalInput").ap()
    ident_d = nc.dram_tensor("identf", [P, P], F32, kind="ExternalInput").ap()
    x3_d = nc.dram_tensor("x3", [T, D], F32, kind="ExternalOutput").ap()

    with contextlib.ExitStack() as st_:
        sb = lambda *a: st_.enter_context(nc.sbuf_tensor(*a))
        ps = lambda *a: st_.enter_context(nc.psum_tensor(*a))
        g_t = sb("g_t", [P, 2, D], F32)
        ogT_s = sb("ogT_s", [P, KC, T], BF16)
        wo_s = sb("wo_s", [P, KC, D], BF16)
        identf_s = sb("identf_s", [P, P], F32)
        res_s = sb("res_s", [P, NT, D], F32)
        cols = alloc_ln_cols(sb)
        p_mm0 = ps("p_mm0", [P, TQ], F32)
        p_mm1 = ps("p_mm1", [P, TQ], F32)
        p_tr0 = ps("p_tr0", [P, P], F32)
        p_tr1 = ps("p_tr1", [P, P], F32)
        st1 = contextlib.ExitStack()
        o_s = st1.enter_context(nc.sbuf_tensor("o_s", [P, NT, D], F32))
        st2 = contextlib.ExitStack()

        with nc.Block() as block:
            pr = Prog(nc, block, st_)
            ld = pr.sem("ld")
            gl = [pr.sem(f"gl{i}") for i in range(2)]
            x2l = [pr.sem(f"x2l{i}") for i in range(2)]
            evV = pr.sem("ev_vector")
            evS = pr.sem("ev_scalar")
            mm = pr.sem("mm")
            dms = [pr.sem(f"dm{i}") for i in range(4)]

            @block.gpsimd
            def _(g):
                ld.inc(g.dma_start(out=o_s[:],
                                   in_=o_d.rearrange("(n p) d -> p n d", p=P)), 16)
                ld.inc(g.dma_start(out=wo_s[:],
                                   in_=woT_d.rearrange("(k p) d -> p k d", p=P)), 16)
                ld.inc(g.dma_start(out=identf_s[:], in_=ident_d), 16)
                base = evV.v
                for i in range(NT):
                    sl = i % 2
                    if i >= 2:
                        g.wait_ge(evV.h, base + (i - 1))
                    g.wait_ge(gl[sl].h, gl[sl].v)
                    gl[sl].inc(g.dma_start(out=g_t[:, sl, :],
                                           in_=gate_d[i * P:(i + 1) * P, :]), 16)

            @block.vector
            def _(v):
                v.wait_ge(ld.h, ld.v)
                for i in range(NT):
                    sl = i % 2
                    v.wait_ge(gl[sl].h, 16 * (i // 2 + 1))
                    evV.inc(v.tensor_mul(o_s[:, i, :], o_s[:, i, :],
                                         g_t[:, sl, :]))

            tr_ev = pipe(
                pr, "tr", [(i, k) for i in range(NT) for k in range(KC)],
                [p_tr0, p_tr1],
                lambda t, job, pm: t.transpose(
                    pm[:], o_s[:, job[0], job[1] * P:(job[1] + 1) * P],
                    identf_s[:]),
                lambda e, job, pm, sl: e.tensor_copy(
                    ogT_s[:, job[1], job[0] * P:(job[0] + 1) * P], pm[:]),
                t_entry=[(evV, evV.v), (ld, ld.v)], consumer_engine="vector")

            emit_barrier(pr)
            st1.close()
            x2_t = st2.enter_context(nc.sbuf_tensor("x2_t", [P, 2, D], F32))
            xh_t = st2.enter_context(nc.sbuf_tensor("xh_t", [P, 2, D], F32))
            stage = st2.enter_context(nc.sbuf_tensor("stage", [P, 2, D], F32))

            def r_prod(t, job, pm):
                i, dq = job
                for k in range(KC):
                    mmi = t.matmul(pm[:], ogT_s[:, k, i * P:(i + 1) * P],
                                   wo_s[:, k, dq * TQ:(dq + 1) * TQ],
                                   start=(k == 0), stop=(k == KC - 1))
                return mmi

            r_ev = pipe(pr, "r", [(i, dq) for i in range(NT) for dq in range(2)],
                        [p_mm0, p_mm1], r_prod,
                        lambda e, job, pm, sl: e.tensor_copy(
                            res_s[:, job[0], job[1] * TQ:(job[1] + 1) * TQ],
                            pm[:]),
                        t_entry=[(tr_ev, tr_ev.v)], consumer_engine="vector")

            ln_o = emit_ln(pr, lambda i: res_s[:, i, :], list(range(NT)), cols,
                           1e-5, entry=[(r_ev, r_ev.v)], apply=False)

            @block.gpsimd
            def _(g):
                base = evV.v
                for i in range(NT):
                    sl = i % 2
                    if i >= 2:
                        g.wait_ge(evV.h, base + (i - 1))
                    g.wait_ge(x2l[sl].h, x2l[sl].v)
                    x2l[sl].inc(g.dma_start(out=x2_t[:, sl, :],
                                            in_=x2_d[i * P:(i + 1) * P, :]), 16)

            for i in range(NT):
                @block.scalar
                def _(s, i=i):
                    s.wait_ge(ln_o.h, ln_o.v)
                    if i >= 2:
                        s.wait_ge(evV.h, evV.v)
                    evS.inc(s.activation(xh_t[:, i % 2, :], res_s[:, i, :],
                                         AF.Identity,
                                         bias=cols["nmr"][:, i:i + 1],
                                         scale=cols["rstd"][:, i:i + 1]))

                @block.vector
                def _(v, i=i):
                    v.wait_ge(evS.h, evS.v)
                    v.wait_ge(x2l[i % 2].h, 16 * (i // 2 + 1))
                    sl = i % 2
                    v.wait_ge(dms[sl].h, dms[sl].v)
                    evV.inc(v.tensor_add(stage[:, sl, :], xh_t[:, i % 2, :],
                                         x2_t[:, i % 2, :]))

                @block.sync
                def _(sy, i=i):
                    sy.wait_ge(evV.h, evV.v)
                    sl = i % 2
                    sy.wait_ge(dms[sl].h, dms[sl].v)
                    dms[sl].inc(sy.dma_start(out=x3_d[i * P:(i + 1) * P, :],
                                             in_=stage[:, sl, :]), 16)

            @block.sync
            def _(sy):
                for d_ in dms:
                    sy.wait_ge(d_.h, d_.v)
        st2.close()
    return nc



def build_l1_ffn():
    nc = bass.Bass("TRN2", target_bir_lowering=False, debug=False)
    xh_d = nc.dram_tensor("xh", [TH, D], F32, kind="ExternalInput").ap()
    x3_d = nc.dram_tensor("x3", [T, D], F32, kind="ExternalInput").ap()
    wT_d = {nm: nc.dram_tensor(nm, [D, D], F32, kind="ExternalInput").ap()
            for nm in ("WrT", "WkT", "WvT")}
    mur_d = nc.dram_tensor("mu_r", [1, D], F32, kind="ExternalInput").ap()
    muk_d = nc.dram_tensor("mu_k", [1, D], F32, kind="ExternalInput").ap()
    ident_d = nc.dram_tensor("identf", [P, P], F32, kind="ExternalInput").ap()
    x4_d = nc.dram_tensor("x4", [T, D], F32, kind="ExternalOutput").ap()

    with contextlib.ExitStack() as st_:
        sb = lambda *a: st_.enter_context(nc.sbuf_tensor(*a))
        ps = lambda *a: st_.enter_context(nc.psum_tensor(*a))
        lnT_s = sb("lnT_s", [P, KC, TH], BF16)
        w_s = sb("w_s", [P, KC, D], BF16)
        mu_s = sb("mu_s", [P, KC, 2], F32)
        identf_s = sb("identf_s", [P, P], F32)
        identb_s = sb("identb_s", [P, P], BF16)
        r_s = sb("r_s", [P, NT, D], BF16)
        kT_s = sb("kT_s", [P, KC, T], BF16)
        cols = alloc_ln_cols(sb, nt=NTH)
        st1 = contextlib.ExitStack()
        x_s = st1.enter_context(nc.sbuf_tensor("x_s", [P, NTH, D], F32))
        st2 = contextlib.ExitStack()
        st3 = contextlib.ExitStack()
        p_mm0 = ps("p_mm0", [P, TQ], F32)
        p_mm1 = ps("p_mm1", [P, TQ], F32)
        p_tr0 = ps("p_tr0", [P, P], F32)
        p_tr1 = ps("p_tr1", [P, P], F32)
        p_trb = ps("p_trb", [P, P], BF16)

        with nc.Block() as block:
            pr = Prog(nc, block, st_)
            ld = pr.sem("ld")
            x3l = [pr.sem(f"x3l{i}") for i in range(2)]
            evV = pr.sem("ev_vector")
            evS = pr.sem("ev_scalar")
            mm = pr.sem("mm")
            wld = pr.sem("wld")
            dms = [pr.sem(f"dm{i}") for i in range(4)]

            @block.gpsimd
            def _(g):
                ld.inc(g.dma_start(out=x_s[:],
                                   in_=xh_d.rearrange("(n p) d -> p n d", p=P)), 16)
                ld.inc(g.dma_start(out=identf_s[:], in_=ident_d), 16)
                ld.inc(g.dma_start(out=identb_s[:], in_=ident_d), 16)
                with nc.allow_non_contiguous_dma(reason="tiny mu columns"):
                    ld.inc(g.dma_start(out=mu_s[:, :, 0:1],
                                       in_=mur_d.rearrange("a (k p) -> p k a", p=P)), 16)
                    ld.inc(g.dma_start(out=mu_s[:, :, 1:2],
                                       in_=muk_d.rearrange("a (k p) -> p k a", p=P)), 16)

            ln1 = emit_ln(pr, lambda i: x_s[:, i, :], list(range(NTH)), cols,
                          1e-5, entry=[(ld, ld.v)])

            tr_ev = pipe(
                pr, "tr", [(i, k) for i in range(NTH) for k in range(KC)],
                [p_tr0, p_tr1],
                lambda t, job, pm: t.transpose(
                    pm[:], x_s[:, job[0], job[1] * P:(job[1] + 1) * P],
                    identf_s[:]),
                lambda e, job, pm, sl: e.tensor_copy(
                    lnT_s[:, job[1], job[0] * P:(job[0] + 1) * P], pm[:]),
                t_entry=[(ln1, ln1.v), (ld, ld.v)], consumer_engine="vector")

            emit_barrier(pr)
            st1.close()
            mixT_s = st2.enter_context(nc.sbuf_tensor("mixT_s", [P, KC, T], BF16))
            kb_s = st2.enter_context(nc.sbuf_tensor("kb_s", [P, NT, D], BF16))

            for si, wnm in ((0, "WrT"), (1, "WkT")):
                @block.gpsimd
                def _(g, wnm=wnm):
                    g.wait_ge(wld.h, wld.v)
                    g.wait_ge(mm.h, mm.v)
                    wld.inc(g.dma_start(
                        out=w_s[:], in_=wT_d[wnm].rearrange("(k p) d -> p k d", p=P)), 16)

                @block.vector
                def _(v, si=si):
                    v.wait_ge(tr_ev.h, tr_ev.v)
                    v.wait_ge(mm.h, mm.v)
                    for k in range(KC):
                        v.wait_ge(evV.h, evV.v)
                        evV.inc(v.tensor_sub(mixT_s[:, k, :],
                                             lnT_s[:, k, P:TH],
                                             lnT_s[:, k, P - 1:TH - 1]))
                        v.wait_ge(evV.h, evV.v)
                        evV.inc(v.scalar_tensor_tensor(
                            mixT_s[:, k, :], mixT_s[:, k, :],
                            mu_s[:, k, si:si + 1], lnT_s[:, k, P - 1:TH - 1],
                            ALU.mult, ALU.add))

                def mmo_prod(t, job, pm, si=si):
                    i, dq = job
                    t.wait_ge(wld.h, wld.v)
                    for k in range(KC):
                        mmi = t.matmul(pm[:], mixT_s[:, k, i * P:(i + 1) * P],
                                       w_s[:, k, dq * TQ:(dq + 1) * TQ],
                                       start=(k == 0), stop=(k == KC - 1))
                    return mmi

                def mmo_cons(e, job, pm, sl, si=si):
                    i, dq = job
                    if si == 0:
                        return e.activation(r_s[:, i, dq * TQ:(dq + 1) * TQ],
                                            pm[:], AF.Sigmoid)
                    return e.activation(kb_s[:, i, dq * TQ:(dq + 1) * TQ],
                                        pm[:], AF.Relu)

                pipe(pr, f"mmo{si}", [(i, dq) for i in range(NT) for dq in range(2)],
                     [p_mm0, p_mm1], mmo_prod, mmo_cons,
                     t_entry=[(evV, evV.v)])

            @block.scalar
            def _(s):
                for i in range(NT):
                    s.wait_ge(evS.h, evS.v)
                    evS.inc(s.activation(kb_s[:, i, :], kb_s[:, i, :], AF.Square))

            kt_ev = pipe(
                pr, "kt", [(i, k) for i in range(NT) for k in range(KC)],
                [p_trb],
                lambda t, job, pm: t.transpose(
                    pm[:], kb_s[:, job[0], job[1] * P:(job[1] + 1) * P],
                    identb_s[:]),
                lambda e, job, pm, sl: e.tensor_copy(
                    kT_s[:, job[1], job[0] * P:(job[0] + 1) * P], pm[:]),
                t_entry=[(evS, evS.v)], consumer_engine="vector")

            emit_barrier(pr)
            st2.close()
            res_s = st3.enter_context(nc.sbuf_tensor("res_s", [P, NT, D], F32))
            x3_t = st3.enter_context(nc.sbuf_tensor("x3_t", [P, 2, D], F32))
            xh_t = st3.enter_context(nc.sbuf_tensor("xh_t", [P, 2, D], F32))
            stage = st3.enter_context(nc.sbuf_tensor("stage", [P, 2, D], F32))

            @block.gpsimd
            def _(g):
                g.wait_ge(wld.h, wld.v)
                g.wait_ge(mm.h, mm.v)
                wld.inc(g.dma_start(
                    out=w_s[:], in_=wT_d["WvT"].rearrange("(k p) d -> p k d", p=P)), 16)

            def v_prod(t, job, pm):
                i, dq = job
                t.wait_ge(wld.h, wld.v)
                for k in range(KC):
                    mmi = t.matmul(pm[:], kT_s[:, k, i * P:(i + 1) * P],
                                   w_s[:, k, dq * TQ:(dq + 1) * TQ],
                                   start=(k == 0), stop=(k == KC - 1))
                return mmi

            r_ev = pipe(pr, "v", [(i, dq) for i in range(NT) for dq in range(2)],
                        [p_mm0, p_mm1], v_prod,
                        lambda e, job, pm, sl: e.tensor_tensor(
                            res_s[:, job[0], job[1] * TQ:(job[1] + 1) * TQ],
                            pm[:], r_s[:, job[0], job[1] * TQ:(job[1] + 1) * TQ],
                            ALU.mult),
                        t_entry=[(kt_ev, kt_ev.v)], consumer_engine="vector")

            ln_o = emit_ln(pr, lambda i: res_s[:, i, :], list(range(NT)),
                           cols, 1e-5, entry=[(r_ev, r_ev.v)], apply=False)

            @block.gpsimd
            def _(g):
                base = evV.v
                for i in range(NT):
                    sl = i % 2
                    if i >= 2:
                        g.wait_ge(evV.h, base + (i - 1))
                    g.wait_ge(x3l[sl].h, x3l[sl].v)
                    x3l[sl].inc(g.dma_start(out=x3_t[:, sl, :],
                                            in_=x3_d[i * P:(i + 1) * P, :]), 16)

            for i in range(NT):
                @block.scalar
                def _(s, i=i):
                    s.wait_ge(ln_o.h, ln_o.v)
                    if i >= 2:
                        s.wait_ge(evV.h, evV.v)
                    evS.inc(s.activation(xh_t[:, i % 2, :], res_s[:, i, :],
                                         AF.Identity,
                                         bias=cols["nmr"][:, i:i + 1],
                                         scale=cols["rstd"][:, i:i + 1]))

                @block.vector
                def _(v, i=i):
                    v.wait_ge(evS.h, evS.v)
                    v.wait_ge(x3l[i % 2].h, 16 * (i // 2 + 1))
                    sl = i % 2
                    v.wait_ge(dms[sl].h, dms[sl].v)
                    evV.inc(v.tensor_add(stage[:, sl, :], xh_t[:, i % 2, :],
                                         x3_t[:, i % 2, :]))

                @block.sync
                def _(sy, i=i):
                    sy.wait_ge(evV.h, evV.v)
                    sl = i % 2
                    sy.wait_ge(dms[sl].h, dms[sl].v)
                    dms[sl].inc(sy.dma_start(out=x4_d[i * P:(i + 1) * P, :],
                                             in_=stage[:, sl, :]), 16)

            @block.sync
            def _(sy):
                for d_ in dms:
                    sy.wait_ge(d_.h, d_.v)
        st3.close()
    return nc



def _consts_core():
    c = fff_consts()
    c["mask_ut"] = np.triu(np.ones((P, P), np.float32))
    return c


def kernel_stageA(**ins):
    NB, S = 2, 8192
    NSB = S // T
    cores = list(range(NCORE))
    cid = [(b, sb_) for b in range(NB) for sb_ in range(NSB)]
    csts = _consts_core()
    eye = csts["identf"]

    def shard(x):
        return [np.ascontiguousarray(x[b, sb_ * T:(sb_ + 1) * T])
                for (b, sb_) in cid]

    def unshard(parts):
        out = np.empty((NB, S, D), np.float32)
        for c, (b, sb_) in enumerate(cid):
            out[b, sb_ * T:(sb_ + 1) * T] = parts[c]
        return out

    nc = build_l0_pre()
    im = [{"x": xs, "q_w": ins["L0_attn_q_w"], "k_w": ins["L0_attn_k_w"],
           "v_w": ins["L0_attn_v_w"], "q_b": ins["L0_attn_q_b"][None],
           "k_b": ins["L0_attn_k_b"][None], "v_b": ins["L0_attn_v_b"][None],
           "proj": ins["L0_attn_proj"], "ident": eye}
          for xs in shard(np.ascontiguousarray(ins["L0_attn_x"]))]
    r1 = run_bass_kernel_spmd(nc, im, core_ids=cores).results
    ln1 = [r["ln1"] for r in r1]

    qpt_g = np.empty((NB, H, M, S), np.float32)
    kpt_g = np.empty((NB, H, M, S), np.float32)
    ktok_g = np.empty((NB, H, S, M), np.float32)
    v_g = np.empty((NB, H, S, DH), np.float32)
    for c, (b, sb_) in enumerate(cid):
        qpt_g[b, :, :, sb_ * T:(sb_ + 1) * T] = r1[c]["qpt"].reshape(H, M, T)
        kpt_g[b, :, :, sb_ * T:(sb_ + 1) * T] = r1[c]["kpt"].reshape(H, M, T)
        ktok_g[b, :, sb_ * T:(sb_ + 1) * T, :] = \
            r1[c]["kptok"].reshape(T, H, M).transpose(1, 0, 2)
        v_g[b, :, sb_ * T:(sb_ + 1) * T, :] = \
            r1[c]["vtok"].reshape(T, H, DH).transpose(1, 0, 2)

    NU = NB * H // NCORE
    nc = build_l0_scan(NU=NU, S=S)
    im = []
    for c in cores:
        us = [divmod(c * NU + j, H) for j in range(NU)]
        im.append({
            "qpt_u": np.stack([qpt_g[b, h] for (b, h) in us]),
            "kpt_u": np.stack([kpt_g[b, h] for (b, h) in us]),
            "ktok_u": np.stack([ktok_g[b, h] for (b, h) in us]),
            "v_u": np.stack([v_g[b, h] for (b, h) in us]),
            "mask_ut": csts["mask_ut"],
        })
    r2 = run_bass_kernel_spmd(nc, im, core_ids=cores).results
    attn_g = np.empty((NB, H, S, DH), np.float32)
    for c in cores:
        for j in range(NU):
            b, h = divmod(c * NU + j, H)
            attn_g[b, h] = r2[c]["attn_u"].reshape(NU, S, DH)[j]

    nc = build_l0_post()
    im = []
    for c, (b, sb_) in enumerate(cid):
        at = attn_g[:, :, sb_ * T:(sb_ + 1) * T, :][b]
        attnT = np.ascontiguousarray(
            at.transpose(0, 2, 1).reshape(D, T))
        im.append({"attnT": attnT, "ln1": ln1[c],
                   "x": np.ascontiguousarray(ins["L0_attn_x"][b, sb_ * T:(sb_ + 1) * T]),
                   "out_w": ins["L0_attn_out_w"],
                   "out_b": ins["L0_attn_out_b"][None],
                   "gamma": ins["L0_attn_gamma"][None],
                   "beta": ins["L0_attn_beta"][None]})
    r3 = run_bass_kernel_spmd(nc, im, core_ids=cores).results
    x1 = [r["x1"] for r in r3]

    nc = build_l0_ffn1()
    wstage = {}
    for f in "qkv":
        wstage[f] = fff_host_weights(
            ins[f"L0_ffn_{f}_nw"], ins[f"L0_ffn_{f}_nb"],
            ins[f"L0_ffn_{f}_w1"], ins[f"L0_ffn_{f}_b1"],
            ins[f"L0_ffn_{f}_w2"], ins[f"L0_ffn_{f}_b2"])
    im = []
    for c in cores:
        d = {"x1": x1[c]}
        for f in "qkv":
            nwT, w12, b2 = wstage[f]
            d[f"{f}_nwT"] = nwT
            d[f"{f}_w12aug"] = w12
            d[f"{f}_b2"] = b2
        d.update({nm: csts[nm] for nm in fff_consts()})
        im.append(d)
    r4 = run_bass_kernel_spmd(nc, im, core_ids=cores).results
    xn = [r["xn"] for r in r4]
    k_full = [np.concatenate([r4[b * NSB + sb_]["k_out"] for sb_ in range(NSB)])
              for b in range(NB)]
    v_full = [np.concatenate([r4[b * NSB + sb_]["v_out"] for sb_ in range(NSB)])
              for b in range(NB)]

    nc = build_l0_ffn2(S=S)
    fTw = fff_host_weights(
        ins["L0_ffn_f_nw"], ins["L0_ffn_f_nb"], ins["L0_ffn_f_w1"],
        ins["L0_ffn_f_b1"], ins["L0_ffn_f_w2"], ins["L0_ffn_f_b2"])
    im = []
    for c, (b, sb_) in enumerate(cid):
        d = {"xn": xn[c], "x1": x1[c],
             "qT": np.ascontiguousarray(r4[c]["q_out"].T),
             "kT": np.ascontiguousarray(k_full[b].T),
             "vfull": v_full[b],
             "f_nwT": fTw[0], "f_w12aug": fTw[1], "f_b2": fTw[2]}
        d.update({nm: csts[nm] for nm in fff_consts()})
        im.append(d)
    r5 = run_bass_kernel_spmd(nc, im, core_ids=cores).results
    x2 = [r["x2"] for r in r5]
    x2_g = unshard(x2)

    nc = build_l1_pre()
    im = []
    for c, (b, sb_) in enumerate(cid):
        xh = np.zeros((TH, D), np.float32)
        if sb_ > 0:
            xh[125:128] = x2_g[b, sb_ * T - 3:sb_ * T]
        xh[P:] = x2_g[b, sb_ * T:(sb_ + 1) * T]
        im.append({"xh": xh,
                   "WqT": np.ascontiguousarray(ins["L1_attn_Wq"].T),
                   "WkT": np.ascontiguousarray(ins["L1_attn_Wk"].T),
                   "WvT": np.ascontiguousarray(ins["L1_attn_Wv"].T),
                   "WgT": np.ascontiguousarray(ins["L1_attn_Wg"].T),
                   "conv_q": ins["L1_attn_conv_q"],
                   "conv_k": ins["L1_attn_conv_k"],
                   "conv_v": ins["L1_attn_conv_v"],
                   "identf": eye})
    r6 = run_bass_kernel_spmd(nc, im, core_ids=cores).results

    q6 = np.empty((NB, H, S, DH), np.float32)
    k6 = np.empty((NB, H, S, DH), np.float32)
    v6 = np.empty((NB, H, S, DH), np.float32)
    for c, (b, sb_) in enumerate(cid):
        q6[b, :, sb_ * T:(sb_ + 1) * T] = r6[c]["q_out"].reshape(T, H, DH).transpose(1, 0, 2)
        k6[b, :, sb_ * T:(sb_ + 1) * T] = r6[c]["k_out"].reshape(T, H, DH).transpose(1, 0, 2)
        v6[b, :, sb_ * T:(sb_ + 1) * T] = r6[c]["v_out"].reshape(T, H, DH).transpose(1, 0, 2)

    nc = build_l1_scan(NU=NU, S=S)
    im = []
    for c in cores:
        us = [divmod(c * NU + j, H) for j in range(NU)]
        im.append({
            "qT_u": np.stack([np.ascontiguousarray(q6[b, h].T) for (b, h) in us]),
            "kT_u": np.stack([np.ascontiguousarray(k6[b, h].T) for (b, h) in us]),
            "ktok_u": np.stack([k6[b, h] for (b, h) in us]),
            "v_u": np.stack([v6[b, h] for (b, h) in us]),
            "W0_u": np.stack([ins["L1_attn_W_init"][0, h] for (b, h) in us]),
            "lng": ins["L1_attn_ln_gamma"][None],
            "lnb": ins["L1_attn_ln_beta"][None],
            "mask_ut": csts["mask_ut"],
        })
    r7 = run_bass_kernel_spmd(nc, im, core_ids=cores).results
    o_g = np.empty((NB, H, S, DH), np.float32)
    for c in cores:
        for j in range(NU):
            b, h = divmod(c * NU + j, H)
            o_g[b, h] = r7[c]["o_u"].reshape(NU, S, DH)[j]

    nc = build_l1_post()
    im = []
    for c, (b, sb_) in enumerate(cid):
        o_sh = np.ascontiguousarray(
            o_g[b, :, sb_ * T:(sb_ + 1) * T, :].transpose(1, 0, 2).reshape(T, D))
        im.append({"o": o_sh, "gate": r6[c]["gate"], "x2": x2[c],
                   "WoT": np.ascontiguousarray(ins["L1_attn_Wo"].T),
                   "identf": eye})
    r8 = run_bass_kernel_spmd(nc, im, core_ids=cores).results
    x3 = [r["x3"] for r in r8]
    x3_g = unshard(x3)

    nc = build_l1_ffn()
    im = []
    for c, (b, sb_) in enumerate(cid):
        xh = np.zeros((TH, D), np.float32)
        if sb_ > 0:
            xh[127] = x3_g[b, sb_ * T - 1]
        xh[P:] = x3_g[b, sb_ * T:(sb_ + 1) * T]
        im.append({"xh": xh, "x3": x3[c],
                   "WrT": np.ascontiguousarray(ins["L1_ffn_W_r"].T),
                   "WkT": np.ascontiguousarray(ins["L1_ffn_W_k"].T),
                   "WvT": np.ascontiguousarray(ins["L1_ffn_W_v"].T),
                   "mu_r": ins["L1_ffn_mu_r"][None],
                   "mu_k": ins["L1_ffn_mu_k"][None],
                   "identf": eye})
    r9 = run_bass_kernel_spmd(nc, im, core_ids=cores).results
    return unshard([r["x4"] for r in r9])



def build_fused(phases=("p0", "p1", "p2", "p3")):
    nc = bass.Bass("TRN2", target_bir_lowering=False, debug=False)
    S = 8192
    x_d = nc.dram_tensor("x", [T, D], F32, kind="ExternalInput").ap()
    w0 = {nm: nc.dram_tensor(nm, [D, D], F32, kind="ExternalInput").ap()
          for nm in ("q_w", "k_w", "v_w", "out_w")}
    b0 = {nm: nc.dram_tensor(nm, [1, D], F32, kind="ExternalInput").ap()
          for nm in ("q_b", "k_b", "v_b", "out_b", "gamma", "beta",
                     "mu_r", "mu_k")}
    proj_d = nc.dram_tensor("proj", [M, DH], F32, kind="ExternalInput").ap()
    wf = {}
    for f in "qkvf":
        wf[f] = dict(
            nwT=nc.dram_tensor(f"{f}_nwT", [D + 1, 255], F32, kind="ExternalInput").ap(),
            w12aug=nc.dram_tensor(f"{f}_w12aug", [4096, 2 * D + 1], F32, kind="ExternalInput").ap(),
            b2=nc.dram_tensor(f"{f}_b2", [M, D], F32, kind="ExternalInput").ap(),
        )
    w1T = {nm: nc.dram_tensor(nm, [D, D], F32, kind="ExternalInput").ap()
           for nm in ("WqT", "WkT", "WvT", "WgT", "WoT", "WrT", "WkmT", "WvmT")}
    cv_d = {nm: nc.dram_tensor(nm, [D, 4], F32, kind="ExternalInput").ap()
            for nm in ("conv_q", "conv_k", "conv_v")}
    w0u_d = nc.dram_tensor("W0h", [H * DH, DH], F32, kind="ExternalInput").ap()
    lng_d = nc.dram_tensor("lng", [1, DH], F32, kind="ExternalInput").ap()
    lnb_d = nc.dram_tensor("lnb", [1, DH], F32, kind="ExternalInput").ap()
    cst_d = {nm: nc.dram_tensor(nm, list(arr.shape),
                                I32 if arr.dtype == np.int32 else F32,
                                kind="ExternalInput").ap()
             for nm, arr in fff_consts().items()}
    maskut_d = nc.dram_tensor("mask_ut", [P, P], F32, kind="ExternalInput").ap()
    pmask_d = nc.dram_tensor("pmask", [P, 4], F32, kind="ExternalInput").ap()
    hmask_d = nc.dram_tensor("hmask", [P, 4], F32, kind="ExternalInput").ap()

    x4_d = nc.dram_tensor("x4", [T, D], F32, kind="ExternalOutput").ap()

    ln1_d = nc.dram_tensor("ln1_i", [T, D], F32).ap()
    x1_d = nc.dram_tensor("x1_i", [T, D], F32).ap()
    x2_d = nc.dram_tensor("x2_i", [T, D], F32).ap()
    x3_d = nc.dram_tensor("x3_i", [T, D], F32).ap()
    xn_d = nc.dram_tensor("xn_i", [T, D], F32).ap()
    y_d = nc.dram_tensor("y_i", [T, D], F32).ap()
    fout_d = nc.dram_tensor("fout_i", [T, D], F32).ap()
    q5_d = nc.dram_tensor("q5_i", [T, D], F32).ap()
    k5_d = nc.dram_tensor("k5_i", [T, D], F32).ap()
    v5_d = nc.dram_tensor("v5_i", [T, D], F32).ap()
    gate_d = nc.dram_tensor("gate_i", [T, D], F32).ap()
    qb5_d = nc.dram_tensor("qb5_i", [T, D], BF16).ap()
    kb5_d = nc.dram_tensor("kb5_i", [T, D], BF16).ap()
    qT5_d = nc.dram_tensor("qT5_i", [D, T], BF16).ap()
    kT5_d = nc.dram_tensor("kT5_i", [D, T], BF16).ap()
    Xs_d = {f: nc.dram_tensor(f"Xs_{f}", [NSLOT, D], BF16).ap() for f in "qkvf"}
    gl_d = {f: nc.dram_tensor(f"gl_{f}", [1, GPAD], F32).ap() for f in "qkvf"}
    OutS_d = {f: nc.dram_tensor(f"OutS_{f}", [NSLOT, D], F32).ap() for f in "qkvf"}
    qkv_i = {f: nc.dram_tensor(f"{f}fo_i", [T, D], F32).ap() for f in "qkv"}
    ag0in_d = nc.dram_tensor("ag0in", [2 * P, H * (DH + 1)], F32).ap()
    ag0out_d = nc.dram_tensor("ag0out", [8 * P, H * (DH + 1)], F32).ap()
    agk_i = nc.dram_tensor("agk_i", [T, D], BF16).ap()
    agk_o = nc.dram_tensor("agk_o", [S, D], BF16).ap()
    agv_i = nc.dram_tensor("agv_i", [T, D], BF16).ap()
    agv_o = nc.dram_tensor("agv_o", [S, D], BF16).ap()
    qb_i = nc.dram_tensor("qb_i", [T, D], BF16).ap()
    qT_i = nc.dram_tensor("qT_i", [D, T], BF16).ap()
    kT_dram = nc.dram_tensor("kT_i", [D, S], BF16).ap()
    ag1in_d = nc.dram_tensor("ag1in", [H * DH, DH], F32).ap()
    ag1out_d = nc.dram_tensor("ag1out", [4 * H * DH, DH], F32).ap()
    agh3in_d = nc.dram_tensor("agh3in", [3, D], F32).ap()
    agh3out_d = nc.dram_tensor("agh3out", [12, D], F32).ap()
    agh1in_d = nc.dram_tensor("agh1in", [1, D], F32).ap()
    agh1out_d = nc.dram_tensor("agh1out", [4, D], F32).ap()

    RG = [[0, 1, 2, 3], [4, 5, 6, 7]]

    with contextlib.ExitStack() as st_:
        sb = lambda *a: st_.enter_context(nc.sbuf_tensor(*a))
        cols = alloc_ln_cols(sb)
        pmask_s = sb("pmask_s", [P, 4], F32)
        hmask_s = sb("hmask_s", [P, 4], F32)

        with nc.Block() as block:
            pr = Prog(nc, block, st_)
            ld = pr.sem("ld")

            @block.gpsimd
            def _(g):
                ld.inc(g.dma_start(out=pmask_s[:], in_=pmask_d), 16)
                ld.inc(g.dma_start(out=hmask_s[:], in_=hmask_d), 16)

            if "p0" in phases:
                emit_l0attn_fused(pr, nc, dict(
                    x_d=x_d, w0=w0, b0=b0, proj_d=proj_d, maskut_d=maskut_d,
                    identf_d=cst_d["identf"],
                    ag_in=ag0in_d, ag_out=ag0out_d, RG=RG,
                    pmask=pmask_s, cols=cols,
                    ln1_d=ln1_d, x1_d=x1_d))
            if "p1" in phases:
                emit_l0ffn_fused(pr, nc, dict(
                    x1_d=x1_d, xn_d=xn_d, y_d=y_d, fout_d=fout_d, x2_d=x2_d,
                    wf=wf, cst_d=cst_d, Xs_d=Xs_d, gl_d=gl_d, OutS_d=OutS_d,
                    qkv_i=qkv_i, qb_i=qb_i, qT_i=qT_i, kT_i=kT_dram,
                    agk_i=agk_i, agk_o=agk_o, agv_i=agv_i, agv_o=agv_o,
                    RG=RG, cols=cols))
            if "p2" in phases:
                emit_l1attn_fused(pr, nc, dict(
                    x2_d=x2_d, x3_d=x3_d, identf_d=cst_d["identf"],
                    maskut_d=maskut_d, w1T=w1T, cv=cv_d, w0_d=w0u_d,
                    lng_d=lng_d, lnb_d=lnb_d,
                    q5=q5_d, k5=k5_d, v5=v5_d, gate_i=gate_d,
                    qb5_i=qb5_d, kb5_i=kb5_d, qT5_i=qT5_d, kT5_i=kT5_d,
                    ag1_i=ag1in_d, ag1_o=ag1out_d,
                    agh3in=agh3in_d, agh3out=agh3out_d,
                    RG=RG, pmask=pmask_s, hmask=hmask_s))
            if "p3" in phases:
                emit_l1ffn_fused(pr, nc, dict(
                    x3_d=x3_d, x4_d=x4_d, identf_d=cst_d["identf"],
                    w1T=w1T, mur_d=b0["mu_r"], muk_d=b0["mu_k"],
                    agh1in=agh1in_d, agh1out=agh1out_d,
                    RG=RG, hmask=hmask_s))
            emit_barrier(pr)
    return nc




def emit_l0attn_fused(pr, nc, a):
    block = pr.block
    ld = pr.sem("ld")
    cst = pr.sem("cst")
    evV = pr.sem("ev_vector")
    evS = pr.sem("ev_scalar")
    mm = pr.sem("mm")
    ccs = pr.sem("ccs")
    lnst = pr.sem("lnst")
    dms = [pr.sem(f"dm{i}") for i in range(4)]
    DH1 = DH + 1
    cols = a["cols"]

    with contextlib.ExitStack() as st0:
        sb0 = lambda *aa: st0.enter_context(nc.sbuf_tensor(*aa))
        identf = sb0("f0_idf", [P, P], F32)
        lnT = sb0("f0_lnT", [P, KC, T], BF16)
        proj_lo = sb0("f0_plo", [P, DH], BF16)
        proj_hi = sb0("f0_phi", [P, DH], BF16)
        projT_aug = sb0("f0_pTa", [DH1, M], BF16)
        xs_aug = sb0("f0_xsa", [DH1, T], BF16)
        sq_bf = sb0("f0_sqb", [DH, T], BF16)
        qp_s = sb0("f0_qps", [P, 2, T], BF16)
        mask_s = sb0("f0_mask", [P, P], BF16)
        ones_red = sb0("f0_or", [P, 1], BF16)
        ones_bc = sb0("f0_obc", [1, TQ], BF16)
        onesf_bc = sb0("f0_ofb", [1, P], F32)
        nln16 = sb0("f0_nl", [P, 1], F32)
        qb_row = sb0("f0_qb", [1, D], BF16)
        kb_row = sb0("f0_kb", [1, D], BF16)
        vb_row = sb0("f0_vb", [1, D], BF16)
        ob_row = sb0("f0_ob", [1, D], BF16)
        ga_row = sb0("f0_ga", [1, D], F32)
        be_row = sb0("f0_be", [1, D], F32)

        @block.gpsimd
        def _(g):
            ld.inc(g.dma_start(out=identf[:], in_=a["identf_d"]), 16)
            ld.inc(g.dma_start(out=mask_s[:], in_=a["maskut_d"]), 16)
            ld.inc(g.dma_start(out=proj_lo[:], in_=a["proj_d"][0:P, :]), 16)
            ld.inc(g.dma_start(out=proj_hi[:], in_=a["proj_d"][P:M, :]), 16)
            for row, nm in ((qb_row, "q_b"), (kb_row, "k_b"), (vb_row, "v_b"),
                            (ob_row, "out_b")):
                ld.inc(g.dma_start(out=row[:], in_=a["b0"][nm]), 16)
            ld.inc(g.dma_start(out=ga_row[:], in_=a["b0"]["gamma"]), 16)
            ld.inc(g.dma_start(out=be_row[:], in_=a["b0"]["beta"]), 16)

        @block.vector
        def _(v):
            v.memset(ones_red[:], 1.0)
            v.memset(ones_bc[:], 1.0)
            v.memset(onesf_bc[:], 1.0)
            v.memset(nln16[:], -LN16)
            cst.inc(v.memset(projT_aug[DH:DH1, :], 1.0))

        def feat(src, base, p_corr):
            @block.scalar
            def _(s, src=src, base=base):
                s.wait_ge(mm.h, mm.v)
                s.wait_ge(evS.h, evS.v)
                s.activation(xs_aug[0:DH, :], src[base:base + DH, :],
                             AF.Identity, scale=0.125)
                evS.inc(s.activation(sq_bf[:], src[base:base + DH, :],
                                     AF.Square, scale=0.125))
            return pipe(
                pr, "f0corr", list(range(T // TQ)), [p_corr],
                lambda t, tq, pm: t.matmul(
                    pm[:], ones_red[0:DH, :], sq_bf[:, tq * TQ:(tq + 1) * TQ],
                    start=True, stop=True),
                lambda e, tq, pm, sl: e.activation(
                    xs_aug[DH:DH1, tq * TQ:(tq + 1) * TQ], pm[:],
                    AF.Identity, scale=-0.5),
                t_entry=[(evS, evS.v)])

        def featmap(dst, corr_ev, p0, p1):
            return pipe(
                pr, "f0map", [(mf, tq) for mf in range(2)
                              for tq in range(T // TQ)],
                [p0, p1],
                lambda t, job, pm: t.matmul(
                    pm[:], projT_aug[:, job[0] * P:(job[0] + 1) * P],
                    xs_aug[:, job[1] * TQ:(job[1] + 1) * TQ],
                    start=True, stop=True),
                lambda e, job, pm, sl: e.activation(
                    dst[:, job[0], job[1] * TQ:(job[1] + 1) * TQ], pm[:],
                    AF.Exp, bias=nln16[:]),
                t_entry=[(corr_ev, corr_ev.v), (cst, cst.v)])

        with contextlib.ExitStack() as stN:
            sbN = lambda *aa: stN.enter_context(nc.sbuf_tensor(*aa))
            num_s = sbN("f0_num", [P, NT, D], F32)
            den_s = sbN("f0_den", [P, NT, H], F32)

            with contextlib.ExitStack() as stA:
                sbA = lambda *aa: stA.enter_context(nc.sbuf_tensor(*aa))
                psA = lambda *aa: stA.enter_context(nc.psum_tensor(*aa))
                x_s = sbA("f0_x", [P, NT, D], F32)
                identb = sbA("f0_idb", [P, P], BF16)
                p_tr0 = psA("f0_ptr0", [P, P], F32)
                p_tr1 = psA("f0_ptr1", [P, P], F32)
                p_trb = psA("f0_ptrb", [P, P], BF16)

                @block.gpsimd
                def _(g):
                    ld.inc(g.dma_start(out=x_s[:],
                                       in_=a["x_d"].rearrange("(n p) d -> p n d", p=P)), 16)
                    ld.inc(g.dma_start(out=identb[:], in_=a["identf_d"]), 16)

                lnd = emit_ln(pr, lambda i: x_s[:, i, :], list(range(NT)),
                              cols, 1e-5, entry=[(ld, ld.v)])

                @block.sync
                def _(sy):
                    sy.wait_ge(lnd.h, lnd.v)
                    sy.wait_ge(dms[1].h, dms[1].v)
                    dms[1].inc(sy.dma_start(
                        out=a["ln1_d"].rearrange("(n p) d -> p n d", p=P),
                        in_=x_s[:]), 16)

                pipe(pr, "f0tr", [(i, k) for i in range(NT) for k in range(KC)],
                     [p_tr0, p_tr1],
                     lambda t, job, pm: t.transpose(
                         pm[:], x_s[:, job[0], job[1] * P:(job[1] + 1) * P],
                         identf[:]),
                     lambda e, job, pm, sl: e.tensor_copy(
                         lnT[:, job[1], job[0] * P:(job[0] + 1) * P], pm[:]),
                     t_entry=[(lnd, lnd.v)], consumer_engine="vector")
                pipe(pr, "f0ptr", [0, 1], [p_trb],
                     lambda t, j, pm: t.transpose(
                         pm[0:DH, :], (proj_lo if j == 0 else proj_hi)[:],
                         identb[:]),
                     lambda e, j, pm, sl: e.copy(
                         projT_aug[0:DH, j * P:(j + 1) * P], pm[0:DH, :]),
                     t_entry=[(ld, ld.v)])
                emit_barrier(pr)

            with contextlib.ExitStack() as stH:
                sbH = lambda *aa: stH.enter_context(nc.sbuf_tensor(*aa))
                psH = lambda *aa: stH.enter_context(nc.psum_tensor(*aa))
                qwp = sbH("f0_qw", [P, KC, P], BF16)
                kwp = sbH("f0_kw", [P, KC, P], BF16)
                vwp = sbH("f0_vw", [P, KC, P], BF16)
                qt_f = sbH("f0_qt", [P, T], BF16)
                kt_f = sbH("f0_kt", [P, T], BF16)
                kp_s = sbH("f0_kps", [P, 2, T], BF16)
                ktok_s = sbH("f0_ktok", [P, NT, M], BF16)
                vtok_s = sbH("f0_vtok", [P, NT, DH], BF16)
                a_bf = sbH("f0_abf", [P, P], BF16)
                st_s = sbH("f0_sts", [P, 2, DH1], F32)
                stb_s = sbH("f0_stb", [P, 2, DH1], BF16)
                p_mm0 = psH("f0_pm0", [P, TQ], F32)
                p_mm1 = psH("f0_pm1", [P, TQ], F32)
                p_corr = psH("f0_pc", [1, TQ], F32)
                p_a = psH("f0_pa", [P, P], F32)
                p_o = psH("f0_po", [P, DH1], F32)
                p_u0 = psH("f0_pu0", [P, DH1], F32)
                p_u1 = psH("f0_pu1", [P, DH1], F32)

                for hp in range(KC):
                    @block.gpsimd
                    def _(g, hp=hp):
                        g.wait_ge(mm.h, mm.v)
                        for w_s, nm in ((qwp, "q_w"), (kwp, "k_w"),
                                        (vwp, "v_w")):
                            ld.inc(g.dma_start(
                                out=w_s[:],
                                in_=a["w0"][nm].rearrange("(k p) d -> p k d", p=P)
                                [:, :, hp * P:(hp + 1) * P]), 16)

                    def qk_prod(t, job, pm, hp=hp):
                        side, tq = job
                        w_s = qwp if side == 0 else kwp
                        bb = qb_row if side == 0 else kb_row
                        for k in range(KC):
                            t.matmul(pm[:], w_s[:, k, :],
                                     lnT[:, k, tq * TQ:(tq + 1) * TQ],
                                     start=(k == 0), stop=False)
                        return t.matmul(pm[:], bb[:, hp * P:(hp + 1) * P],
                                        ones_bc[:, 0:TQ], start=False, stop=True)

                    pipe(pr, f"f0qk{hp}",
                         [(sd, tq) for sd in range(2) for tq in range(T // TQ)],
                         [p_mm0, p_mm1], qk_prod,
                         lambda e, job, pm, sl: e.copy(
                             (qt_f if job[0] == 0 else kt_f)
                             [:, job[1] * TQ:(job[1] + 1) * TQ], pm[:]),
                         t_entry=[(ld, ld.v), (cst, cst.v)])

                    for hh in range(2):
                        head = 2 * hp + hh
                        base = DH * hh
                        corr_ev = feat(kt_f, base, p_corr)
                        featmap(kp_s, corr_ev, p_mm0, p_mm1)
                        pipe(pr, f"f0kt{head}", list(range(NT)),
                             [p_mm0, p_mm1],
                             lambda t, i, pm: t.matmul(
                                 pm[:, 0:M], xs_aug[:, i * P:(i + 1) * P],
                                 projT_aug[:], start=True, stop=True),
                             lambda e, i, pm, sl: e.activation(
                                 ktok_s[:, i, :], pm[:, 0:M], AF.Exp,
                                 bias=nln16[:]),
                             t_entry=[(corr_ev, corr_ev.v)])

                        def v_prod(t, i, pm, hh=hh):
                            for k in range(KC):
                                t.matmul(pm[:, 0:DH],
                                         lnT[:, k, i * P:(i + 1) * P],
                                         vwp[:, k, hh * DH:(hh + 1) * DH],
                                         start=(k == 0), stop=False)
                            return t.matmul(pm[:, 0:DH], ones_bc[:, 0:P],
                                            vb_row[:, head * DH:(head + 1) * DH],
                                            start=False, stop=True)

                        pipe(pr, f"f0v{head}", list(range(NT)),
                             [p_mm0, p_mm1], v_prod,
                             lambda e, i, pm, sl: e.copy(vtok_s[:, i, :],
                                                         pm[:, 0:DH]),
                             t_entry=[(ld, ld.v)])
                        corr2_ev = feat(qt_f, base, p_corr)
                        featmap(qp_s, corr2_ev, p_mm0, p_mm1)

                        @block.vector
                        def _(v):
                            v.wait_ge(evS.h, evS.v)
                            v.wait_ge(dms[0].h, dms[0].v)
                            v.memset(st_s[:], 0.0)
                            evV.inc(v.memset(stb_s[:], 0.0))

                        for c in range(NT):
                            tok = slice(c * P, (c + 1) * P)

                            @block.tensor
                            def _(t, tok=tok):
                                t.wait_ge(evV.h, evV.v)
                                t.wait_ge(evS.h, evS.v)
                                t.matmul(p_a[:], kp_s[:, 0, tok],
                                         qp_s[:, 0, tok], start=True, stop=False)
                                mm.inc(t.matmul(p_a[:], kp_s[:, 1, tok],
                                                qp_s[:, 1, tok],
                                                start=False, stop=True))

                            @block.vector
                            def _(v):
                                v.wait_ge(mm.h, mm.v)
                                evV.inc(v.tensor_tensor(a_bf[:], p_a[:],
                                                        mask_s[:], ALU.mult))

                            @block.tensor
                            def _(t, c=c, tok=tok):
                                t.wait_ge(evV.h, evV.v)
                                t.wait_ge(evS.h, evS.v)
                                t.matmul(p_o[:, 0:DH], a_bf[:], vtok_s[:, c, :],
                                         start=True, stop=False)
                                t.matmul(p_o[:, DH:DH1], a_bf[:], ones_red[:],
                                         start=False, stop=False)
                                t.matmul(p_o[:], qp_s[:, 0, tok], stb_s[:, 0, :],
                                         start=False, stop=False)
                                mm.inc(t.matmul(p_o[:], qp_s[:, 1, tok],
                                                stb_s[:, 1, :],
                                                start=False, stop=True))
                                t.matmul(p_u0[:, 0:DH], ktok_s[:, c, 0:P],
                                         vtok_s[:, c, :], start=True, stop=False)
                                t.matmul(p_u0[:, DH:DH1], ktok_s[:, c, 0:P],
                                         ones_red[:], start=False, stop=True)
                                t.matmul(p_u1[:, 0:DH], ktok_s[:, c, P:M],
                                         vtok_s[:, c, :], start=True, stop=False)
                                mm.inc(t.matmul(p_u1[:, DH:DH1],
                                                ktok_s[:, c, P:M], ones_red[:],
                                                start=False, stop=True))

                            @block.scalar
                            def _(s, c=c, head=head):
                                s.wait_ge(mm.h, mm.v - 1)
                                s.activation(num_s[:, c, head * DH:(head + 1) * DH],
                                             p_o[:, 0:DH], AF.Identity)
                                evS.inc(s.activation(den_s[:, c, head:head + 1],
                                                     p_o[:, DH:DH1], AF.Identity))

                            @block.vector
                            def _(v):
                                v.wait_ge(mm.h, mm.v)
                                v.tensor_add(st_s[:, 0, :], st_s[:, 0, :], p_u0[:])
                                evV.inc(v.tensor_add(st_s[:, 1, :], st_s[:, 1, :],
                                                     p_u1[:]))

                            @block.scalar
                            def _(s):
                                s.wait_ge(evV.h, evV.v)
                                s.activation(stb_s[:, 0, :], st_s[:, 0, :],
                                             AF.Identity)
                                evS.inc(s.activation(stb_s[:, 1, :], st_s[:, 1, :],
                                                     AF.Identity))

                        @block.sync
                        def _(sy, head=head):
                            sy.wait_ge(evV.h, evV.v)
                            sy.wait_ge(dms[0].h, dms[0].v)
                            dms[0].inc(sy.dma_start(
                                out=a["ag_in"].rearrange("(c p) m -> p c m", p=P)
                                [:, :, head * DH1:(head + 1) * DH1],
                                in_=st_s[:]), 16)

                emit_barrier(pr)

            @block.gpsimd
            def _(g):
                g.wait_ge(dms[0].h, dms[0].v)
                ccs.inc(g.collective_compute(
                    "AllGather", ALU.bypass, replica_groups=a["RG"],
                    ins=[a["ag_in"].opt()], outs=[a["ag_out"].opt()]), 1)

            with contextlib.ExitStack() as stH2:
                sb2 = lambda *aa: stH2.enter_context(nc.sbuf_tensor(*aa))
                ps2 = lambda *aa: stH2.enter_context(nc.psum_tensor(*aa))
                qw2 = sb2("f0_qw2", [P, KC, P], BF16)
                qt2 = sb2("f0_qt2", [P, T], BF16)
                agt4 = sb2("f0_agt", [P, 8, H * DH1], F32)
                G_s = sb2("f0_G", [P, 2, H * DH1], F32)
                G_b = sb2("f0_Gbb", [P, 2, H * DH1], BF16)
                ccol = sb2("f0_cc", [P, NT], F32)
                rc16 = sb2("f0_rc16", [P, NT], F32)
                p_mm0 = ps2("f0_2m0", [P, TQ], F32)
                p_mm1 = ps2("f0_2m1", [P, TQ], F32)
                p_corr = ps2("f0_2pc", [1, TQ], F32)
                p_o0 = ps2("f0_2po0", [P, DH1], F32)
                p_o1 = ps2("f0_2po1", [P, DH1], F32)

                @block.gpsimd
                def _(g):
                    g.wait_ge(ccs.h, ccs.v)
                    ld.inc(g.dma_start(
                        out=agt4[:],
                        in_=a["ag_out"].rearrange("(j c p) m -> p (j c) m",
                                                  p=P, c=2)), 16)

                @block.vector
                def _(v):
                    v.wait_ge(ld.h, ld.v)
                    evV.inc(v.memset(G_s[:], 0.0))
                    for j in range(4):
                        for c in range(2):
                            v.wait_ge(evV.h, evV.v)
                            evV.inc(v.scalar_tensor_tensor(
                                G_s[:, c, :], agt4[:, 2 * j + c, :],
                                a["pmask"][:, j:j + 1], G_s[:, c, :],
                                ALU.mult, ALU.add))

                @block.scalar
                def _(s):
                    s.wait_ge(evV.h, evV.v)
                    s.activation(G_b[:, 0, :], G_s[:, 0, :], AF.Identity)
                    evS.inc(s.activation(G_b[:, 1, :], G_s[:, 1, :], AF.Identity))

                for hp in range(KC):
                    @block.gpsimd
                    def _(g, hp=hp):
                        g.wait_ge(mm.h, mm.v)
                        g.wait_ge(ld.h, ld.v)
                        ld.inc(g.dma_start(
                            out=qw2[:],
                            in_=a["w0"]["q_w"].rearrange("(k p) d -> p k d", p=P)
                            [:, :, hp * P:(hp + 1) * P]), 16)

                    def q2_prod(t, tq, pm, hp=hp):
                        for k in range(KC):
                            t.matmul(pm[:], qw2[:, k, :],
                                     lnT[:, k, tq * TQ:(tq + 1) * TQ],
                                     start=(k == 0), stop=False)
                        return t.matmul(pm[:], qb_row[:, hp * P:(hp + 1) * P],
                                        ones_bc[:, 0:TQ], start=False, stop=True)

                    pipe(pr, f"f0q2{hp}", list(range(T // TQ)),
                         [p_mm0, p_mm1], q2_prod,
                         lambda e, tq, pm, sl: e.copy(
                             qt2[:, tq * TQ:(tq + 1) * TQ], pm[:]),
                         t_entry=[(ld, ld.v), (evS, evS.v), (evV, evV.v)])

                    for hh in range(2):
                        head = 2 * hp + hh
                        base = DH * hh
                        corr_ev = feat(qt2, base, p_corr)
                        featmap(qp_s, corr_ev, p_mm0, p_mm1)

                    for hh in range(2):
                        head = 2 * hp + hh
                        base = DH * hh
                        corr_ev = feat(qt2, base, p_corr)
                        featmap(qp_s, corr_ev, p_mm0, p_mm1)

                        def c_prod(t, i, pm, head=head):
                            t.matmul(pm[:], qp_s[:, 0, i * P:(i + 1) * P],
                                     G_b[:, 0, head * DH1:(head + 1) * DH1],
                                     start=True, stop=False)
                            return t.matmul(
                                pm[:], qp_s[:, 1, i * P:(i + 1) * P],
                                G_b[:, 1, head * DH1:(head + 1) * DH1],
                                start=False, stop=True)

                        def c_cons(e, i, pm, sl, head=head):
                            e.tensor_add(
                                num_s[:, i, head * DH:(head + 1) * DH],
                                num_s[:, i, head * DH:(head + 1) * DH],
                                pm[:, 0:DH])
                            return e.tensor_tensor(
                                ccol[:, i:i + 1], pm[:, DH:DH1],
                                den_s[:, i, head:head + 1], ALU.add)

                        cp_ev = pipe(pr, f"f0cp{head}", list(range(NT)),
                                     [p_o0, p_o1], c_prod, c_cons,
                                     t_entry=[(evS, evS.v), (evV, evV.v)],
                                     c_entry=[(evV, evV.v)],
                                     consumer_engine="vector")

                        @block.vector
                        def _(v, cv=cp_ev.v):
                            v.wait_ge(cp_ev.h, cv)
                            evV.inc(v.reciprocal(rc16[:], ccol[:]))

                        for i in range(NT):
                            @block.scalar
                            def _(s, i=i, head=head):
                                s.wait_ge(evV.h, evV.v)
                                evS.inc(s.activation(
                                    num_s[:, i, head * DH:(head + 1) * DH],
                                    num_s[:, i, head * DH:(head + 1) * DH],
                                    AF.Identity, scale=rc16[:, i:i + 1]))

                emit_barrier(pr)

            with contextlib.ExitStack() as stT:
                p_tr0 = stT.enter_context(nc.psum_tensor("f0_t0", [P, P], F32))
                p_tr1 = stT.enter_context(nc.psum_tensor("f0_t1", [P, P], F32))
                pipe(pr, "f0trc",
                     [(i, k) for i in range(NT) for k in range(KC)],
                     [p_tr0, p_tr1],
                     lambda t, job, pm: t.transpose(
                         pm[:], num_s[:, job[0], job[1] * P:(job[1] + 1) * P],
                         identf[:]),
                     lambda e, job, pm, sl: e.tensor_copy(
                         lnT[:, job[1], job[0] * P:(job[0] + 1) * P], pm[:]),
                     t_entry=[(evS, evS.v)], consumer_engine="vector")
                emit_barrier(pr)

        with contextlib.ExitStack() as stE:
            sbE = lambda *aa: stE.enter_context(nc.sbuf_tensor(*aa))
            psE = lambda *aa: stE.enter_context(nc.psum_tensor(*aa))
            ow_s = sbE("f0_ow", [P, KC, D], BF16)
            Bg = sbE("f0_Bg", [P, D], F32)
            Bb = sbE("f0_Bb", [P, D], F32)
            res_s = sbE("f0_res", [P, NT, D], F32)
            ln1_t = sbE("f0_l1t", [P, 2, D], F32)
            x_t = sbE("f0_xt", [P, 2, D], F32)
            xh_t = sbE("f0_xh", [P, 2, D], F32)
            stage = sbE("f0_stg", [P, 4, D], F32)
            p_mm0 = psE("f0_em0", [P, TQ], F32)
            p_mm1 = psE("f0_em1", [P, TQ], F32)
            p_g = psE("f0_eg", [P, TQ], F32)
            glds = [pr.sem("gl0"), pr.sem("gl1")]
            xlds = [pr.sem("xl0"), pr.sem("xl1")]

            @block.gpsimd
            def _(g):
                ld.inc(g.dma_start(
                    out=ow_s[:],
                    in_=a["w0"]["out_w"].rearrange("(k p) d -> p k d", p=P)), 16)

            g_ev = pipe(pr, "f0g", [(w, dq) for w in range(2) for dq in range(2)],
                        [p_g],
                        lambda t, j, pm: t.matmul(
                            pm[:], onesf_bc[:, 0:P],
                            (ga_row if j[0] == 0 else be_row)
                            [:, j[1] * TQ:(j[1] + 1) * TQ],
                            start=True, stop=True),
                        lambda e, j, pm, sl: e.tensor_copy(
                            (Bg if j[0] == 0 else Bb)
                            [:, j[1] * TQ:(j[1] + 1) * TQ], pm[:]),
                        t_entry=[(ld, ld.v), (cst, cst.v)],
                        consumer_engine="vector")

            rbase = evV.v
            gld_base = [g_.v for g_ in glds]

            @block.gpsimd
            def _(g):
                for i in range(NT):
                    sl = i % 2
                    if i >= 2:
                        g.wait_ge(evV.h, rbase + 2 * (i - 1))
                    g.wait_ge(glds[sl].h, glds[sl].v)
                    glds[sl].inc(g.dma_start(
                        out=ln1_t[:, sl, :],
                        in_=a["ln1_d"][i * P:(i + 1) * P, :]), 16)

            def r_prod(t, job, pm):
                i, dq = job
                for k in range(KC):
                    t.matmul(pm[:], lnT[:, k, i * P:(i + 1) * P],
                             ow_s[:, k, dq * TQ:(dq + 1) * TQ],
                             start=(k == 0), stop=False)
                return t.matmul(pm[:], ones_bc[:, 0:P],
                                ob_row[:, dq * TQ:(dq + 1) * TQ],
                                start=False, stop=True)

            def r_cons(e, job, pm, sl):
                i, dq = job
                e.wait_ge(glds[i % 2].h, gld_base[i % 2] + 16 * (i // 2 + 1))
                return e.scalar_tensor_tensor(
                    res_s[:, i, dq * TQ:(dq + 1) * TQ], pm[:], 1.0,
                    ln1_t[:, i % 2, dq * TQ:(dq + 1) * TQ], ALU.mult, ALU.add)

            r_ev = pipe(pr, "f0r", [(i, dq) for i in range(NT) for dq in range(2)],
                        [p_mm0, p_mm1], r_prod, r_cons,
                        t_entry=[(ld, ld.v), (cst, cst.v)],
                        consumer_engine="vector")

            ln_i = emit_ln(pr, lambda i: res_s[:, i, :], list(range(NT)), cols,
                           1e-6, entry=[(r_ev, r_ev.v), (g_ev, g_ev.v)],
                           apply=False)

            for i in range(NT):
                @block.scalar
                def _(s, i=i):
                    s.wait_ge(ln_i.h, ln_i.v)
                    if i >= 2:
                        s.wait_ge(evV.h, evV.v)
                    evS.inc(s.activation(xh_t[:, i % 2, :], res_s[:, i, :],
                                         AF.Identity, bias=cols["nmr"][:, i:i + 1],
                                         scale=cols["rstd"][:, i:i + 1]))

                @block.vector
                def _(v, i=i):
                    v.wait_ge(evS.h, evS.v)
                    evV.inc(v.tensor_mul(res_s[:, i, :], xh_t[:, i % 2, :], Bg[:]))
                    v.wait_ge(evV.h, evV.v)
                    evV.inc(v.tensor_add(res_s[:, i, :], res_s[:, i, :], Bb[:]))

            ln_o = emit_ln(pr, lambda i: res_s[:, i, :], list(range(NT)), cols,
                           1e-5, entry=[(evV, evV.v), (evS, evS.v)], apply=False)

            xbaseV = evV.v
            xld_base = [x_.v for x_ in xlds]

            @block.gpsimd
            def _(g):
                for i in range(NT):
                    sl = i % 2
                    if i >= 2:
                        g.wait_ge(evV.h, xbaseV + (i - 1))
                    g.wait_ge(xlds[sl].h, xlds[sl].v)
                    xlds[sl].inc(g.dma_start(out=x_t[:, sl, :],
                                             in_=a["x_d"][i * P:(i + 1) * P, :]), 16)

            for i in range(NT):
                @block.scalar
                def _(s, i=i):
                    s.wait_ge(ln_o.h, ln_o.v)
                    if i >= 2:
                        s.wait_ge(evV.h, evV.v)
                    evS.inc(s.activation(xh_t[:, i % 2, :], res_s[:, i, :],
                                         AF.Identity, bias=cols["nmr"][:, i:i + 1],
                                         scale=cols["rstd"][:, i:i + 1]))

                @block.vector
                def _(v, i=i):
                    v.wait_ge(evS.h, evS.v)
                    v.wait_ge(xlds[i % 2].h, xld_base[i % 2] + 16 * (i // 2 + 1))
                    sl = i % 4
                    v.wait_ge(dms[sl].h, dms[sl].v)
                    evV.inc(v.tensor_add(stage[:, sl, :], xh_t[:, i % 2, :],
                                         x_t[:, i % 2, :]))

                @block.sync
                def _(sy, i=i):
                    sy.wait_ge(evV.h, evV.v)
                    sl = i % 4
                    sy.wait_ge(dms[sl].h, dms[sl].v)
                    dms[sl].inc(sy.dma_start(out=a["x1_d"][i * P:(i + 1) * P, :],
                                             in_=stage[:, sl, :]), 16)

            emit_barrier(pr)


def emit_l0ffn_fused(pr, nc, a):
    block = pr.block
    ld = pr.sem("ld")
    cst = pr.sem("cst")
    evV = pr.sem("ev_vector")
    evS = pr.sem("ev_scalar")
    mm = pr.sem("mm")
    ccs = pr.sem("ccs")
    S = 8192
    NKC = S // TQ
    cols = a["cols"]
    cst_d = a["cst_d"]
    wd = a["wf"]
    Xs_d, gl_d, OutS_d = a["Xs_d"], a["gl_d"], a["OutS_d"]

    with contextlib.ExitStack() as stP:
        sbP = lambda *aa: stP.enter_context(nc.sbuf_tensor(*aa))
        bf_p = alloc_fff_persist(sbP, 3)

        with contextlib.ExitStack() as ph1:
            sb1 = lambda *aa: ph1.enter_context(nc.sbuf_tensor(*aa))
            ps1 = lambda *aa: ph1.enter_context(nc.psum_tensor(*aa))
            bf = dict(alloc_fff_route_bufs(sb1, ps1), **bf_p)
            x_s = sb1("f1_xs", [P, NT, D], F32)
            xnT_s = sb1("f1_xnT", [P, KC, T], BF16)
            xnT_lo = sb1("f1_xnTl", [P, KC, T], BF16)
            nwT_s = sb1("f1_nwT", [P, KC, 255], BF16)
            nwT_lo = sb1("f1_nwTl", [P, KC, 255], BF16)
            nb_f = sb1("f1_nbf", [1, 255], F32)
            nb_row = sb1("f1_nbr", [1, 255], BF16)
            nb_lo = sb1("f1_nbl", [1, 255], BF16)
            p_tr0 = ps1("f1_tr0", [P, P], F32)
            p_tr1 = ps1("f1_tr1", [P, P], F32)

            @block.gpsimd
            def _(g):
                g.wait_ge(ld.h, ld.v)
                ld.inc(g.dma_start(out=x_s[:],
                                   in_=a["x1_d"].rearrange("(n p) d -> p n d", p=P)), 16)
                for nm in ("triu_strict", "g16col", "iota256col", "identf"):
                    tgt = {"triu_strict": bf["triu"], "g16col": bf["g16"],
                           "iota256col": bf["iota2"], "identf": bf["identf"]}[nm]
                    ld.inc(g.dma_start(out=tgt[:], in_=cst_d[nm]), 16)
                ld.inc(g.dma_start(out=bf["identb"][:], in_=cst_d["identf"]), 16)

            @block.vector
            def _(v):
                v.memset(bf["onesf"][:], 1.0)
                v.memset(bf["onesfbc"][:], 1.0)
                cst.inc(v.memset(bf["onesbc1"][:], 1.0))
                v.wait_ge(ld.h, ld.v)
                for c2 in range(2):
                    cst.inc(v.tensor_copy(bf["rhs2"][:, c2, 1:2],
                                          bf["iota2"][:, c2:c2 + 1]))

            ln_o = emit_ln(pr, lambda i: x_s[:, i, :], list(range(NT)),
                           cols, 1e-5, entry=[(ld, ld.v)])
            ln_s = emit_ln(pr, lambda i: x_s[:, i, :], list(range(NT)),
                           cols, 1e-5, entry=[(ln_o, ln_o.v)], sd_mode=True)

            @block.sync
            def _(sy):
                sy.wait_ge(ln_s.h, ln_s.v)
                pr.sem("dmx").inc(sy.dma_start(
                    out=a["xn_d"].rearrange("(n p) d -> p n d", p=P),
                    in_=x_s[:]), 16)

            tr_mm = pr.sem("trmm")
            tr_ev = pr.sem("trev")
            trjobs = [(i, k) for i in range(NT) for k in range(KC)]
            trm_base, tre_base = tr_mm.v, tr_ev.v

            @block.tensor
            def _(t):
                t.wait_ge(ln_s.h, ln_s.v)
                t.wait_ge(ld.h, ld.v)
                for j, (i, k) in enumerate(trjobs):
                    if j >= 2:
                        t.wait_ge(tr_ev.h, tre_base + 2 * (j - 1))
                    tr_mm.inc(t.transpose(
                        (p_tr0 if j % 2 == 0 else p_tr1)[:],
                        x_s[:, i, k * P:(k + 1) * P], bf["identf"][:]))

            @block.vector
            def _(v):
                for j, (i, k) in enumerate(trjobs):
                    pm = (p_tr0 if j % 2 == 0 else p_tr1)
                    v.wait_ge(tr_mm.h, trm_base + j + 1)
                    tr_ev.inc(v.tensor_copy(
                        xnT_s[:, k, i * P:(i + 1) * P], pm[:]))
                    v.wait_ge(tr_ev.h, tr_ev.v)
                    tr_ev.inc(v.scalar_tensor_tensor(
                        xnT_lo[:, k, i * P:(i + 1) * P], pm[:], 1.0,
                        xnT_s[:, k, i * P:(i + 1) * P],
                        ALU.mult, ALU.subtract))

            for fi, f in enumerate("qkv"):
                nwT_f = bf["lg"][:, 0:KC, 0:255]

                @block.gpsimd
                def _(g, f=f, nwT_f=nwT_f):
                    g.wait_ge(ld.h, ld.v)
                    g.wait_ge(mm.h, mm.v)
                    g.wait_ge(evV.h, evV.v)
                    ld.inc(g.dma_start(
                        out=nwT_f,
                        in_=wd[f]["nwT"][0:D, :].rearrange("(k p) n -> p k n", p=P)), 16)
                    ld.inc(g.dma_start(out=nb_f[:],
                                       in_=wd[f]["nwT"][D:D + 1, :]), 16)

                @block.vector
                def _(v, nwT_f=nwT_f):
                    v.wait_ge(ld.h, ld.v)
                    evV.inc(v.tensor_copy(nwT_s[:], nwT_f))
                    evV.inc(v.tensor_copy(nb_row[:], nb_f[:]))
                    v.wait_ge(evV.h, evV.v)
                    evV.inc(v.scalar_tensor_tensor(
                        nwT_lo[:], nwT_f, 1.0, nwT_s[:],
                        ALU.mult, ALU.subtract))
                    evV.inc(v.scalar_tensor_tensor(
                        nb_lo[:], nb_f[:], 1.0, nb_row[:],
                        ALU.mult, ALU.subtract))

                emit_fff_route(pr, bf, fi, lambda i: x_s[:, i, :],
                               xnT_s, xnT_lo, nwT_s, nwT_lo, nb_row, nb_lo,
                               Xs_d[f], gl_d[f],
                               entry=[(tr_ev, tr_ev.v), (ld, ld.v),
                                      (cst, cst.v), (evS, evS.v),
                                      (evV, evV.v)])
            emit_barrier(pr)

        with contextlib.ExitStack() as ph2:
            sb2 = lambda *aa: ph2.enter_context(nc.sbuf_tensor(*aa))
            ps2 = lambda *aa: ph2.enter_context(nc.psum_tensor(*aa))
            cb = alloc_fff_comp_bufs(sb2, ps2, 23)
            bf = dict(bf_p)
            bf["act_fn"] = AF.Relu

            @block.gpsimd
            def _(g):
                g.wait_ge(ld.h, ld.v)
                ld.inc(g.dma_start(out=cb["posmap"][:], in_=cst_d["posmap"]), 16)
                ld.inc(g.dma_start(out=cb["jcol"][:], in_=cst_d["jcol"]), 16)
                ld.inc(g.dma_start(out=cb["bd"][:], in_=cst_d["blockdiag"]), 16)

            for fi, f in ((1, "k"), (2, "v"), (0, "q")):
                bf["OutS_d"] = OutS_d[f]
                emit_fff_compute(pr, bf, cb, fi, wd[f]["w12aug"],
                                 wd[f]["b2"], Xs_d[f],
                                 gl_d[f], a["qkv_i"][f], half=23)
            emit_barrier(pr)

    with contextlib.ExitStack() as stG:
        sbG = lambda *aa: stG.enter_context(nc.sbuf_tensor(*aa))
        tb = sbG("f1_tb", [P, 2, D], BF16)
        ktb = sbG("f1_ktb", [P, S], BF16)
        gld = [pr.sem("gl0"), pr.sem("gl1")]
        dms = [pr.sem(f"dm{i}") for i in range(4)]
        ccs_base = ccs.v

        for src, dst in ((a["qkv_i"]["k"], a["agk_i"]),
                         (a["qkv_i"]["v"], a["agv_i"]),
                         (a["qkv_i"]["q"], a["qb_i"])):
            for i in range(NT):
                sl = i % 2

                @block.gpsimd
                def _(g, src=src, i=i, sl=sl):
                    g.wait_ge(dms[sl].h, dms[sl].v)
                    g.wait_ge(gld[sl].h, gld[sl].v)
                    gld[sl].inc(g.dma_start(out=tb[:, sl, :],
                                            in_=src[i * P:(i + 1) * P, :]), 16)

                @block.sync
                def _(sy, dst=dst, i=i, sl=sl):
                    sy.wait_ge(gld[sl].h, gld[sl].v)
                    sy.wait_ge(dms[sl].h, dms[sl].v)
                    dms[sl].inc(sy.dma_start(out=dst[i * P:(i + 1) * P, :],
                                             in_=tb[:, sl, :]), 16)

        @block.gpsimd
        def _(g):
            for d_ in dms:
                g.wait_ge(d_.h, d_.v)
            ccs.inc(g.collective_compute(
                "AllGather", ALU.bypass, replica_groups=a["RG"],
                ins=[a["agk_i"].opt()], outs=[a["agk_o"].opt()]), 1)
            ccs.inc(g.collective_compute(
                "AllGather", ALU.bypass, replica_groups=a["RG"],
                ins=[a["agv_i"].opt()], outs=[a["agv_o"].opt()]), 1)

        @block.sync
        def _(sy):
            sy.wait_ge(dms[0].h, dms[0].v)
            sy.wait_ge(dms[1].h, dms[1].v)
            for dcb in range(KC):
                sy.wait_ge(dms[2].h, dms[2].v)
                dms[2].inc(sy.dma_start(
                    out=ktb[:, 0:T],
                    in_=a["qb_i"][:, dcb * P:(dcb + 1) * P],
                    transpose=True), 16)
                sy.wait_ge(dms[2].h, dms[2].v)
                dms[2].inc(sy.dma_start(
                    out=a["qT_i"][dcb * P:(dcb + 1) * P, :],
                    in_=ktb[:, 0:T]), 16)
            sy.wait_ge(ccs.h, ccs_base + 1)
            for dcb in range(KC):
                sy.wait_ge(dms[2].h, dms[2].v)
                dms[2].inc(sy.dma_start(
                    out=ktb[:],
                    in_=a["agk_o"][:, dcb * P:(dcb + 1) * P],
                    transpose=True), 16)
                sy.wait_ge(dms[2].h, dms[2].v)
                dms[2].inc(sy.dma_start(
                    out=a["kT_i"][dcb * P:(dcb + 1) * P, :],
                    in_=ktb[:]), 16)
        emit_barrier(pr)

    with contextlib.ExitStack() as phA:
        sbA = lambda *aa: phA.enter_context(nc.sbuf_tensor(*aa))
        psA = lambda *aa: phA.enter_context(nc.psum_tensor(*aa))
        qTa = sbA("f1_qTa", [P, KC, T], BF16)
        qaug = sbA("f1_qaug", [1, T], BF16)
        onesb = sbA("f1_onesb", [1, P], BF16)
        ones_red = sbA("f1_or", [P, 1], F16)
        kt_s = sbA("f1_kt", [P, 2, KC, TQ], BF16)
        vt_s = sbA("f1_vt", [P, 2, 4, D], F16)
        ST = sbA("f1_ST", [P, 2, 4, T], F16)
        y_s = sbA("f1_ysum", [P, NT, D], F32)
        den_s = sbA("f1_dens", [P, NT], F32)
        xn_t = sbA("f1_xnt", [P, 2, D], F32)
        yst = sbA("f1_yst", [P, 2, D], F32)
        rcol = sbA("f1_rc", [P, 2], F32)
        p_s0 = psA("f1_ps0", [P, TQ], F32)
        p_s1 = psA("f1_ps1", [P, TQ], F32)
        p_av = [[psA(f"f1_pav{i}{j}", [P, TQ], F32) for j in range(2)]
                for i in range(2)]
        p_d = [psA(f"f1_pd{i}", [P, 1], F32) for i in range(2)]
        dmy = [pr.sem(f"dmy{i}") for i in range(2)]
        kld = [pr.sem(f"kld{i}") for i in range(2)]
        vld = [pr.sem(f"vld{i}") for i in range(2)]

        @block.gpsimd
        def _(g):
            g.wait_ge(ld.h, ld.v)
            ld.inc(g.dma_start(
                out=qTa[:],
                in_=a["qT_i"].rearrange("(k p) t -> p k t", p=P)), 16)

        @block.vector
        def _(v):
            v.memset(qaug[:], -128.0)
            v.memset(onesb[:], 1.0)
            cst.inc(v.memset(ones_red[:], 1.0))

        for ktc in range(NKC):
            sl = ktc % 2

            @block.gpsimd
            def _(g, ktc=ktc, sl=sl):
                g.wait_ge(kld[sl].h, kld[sl].v)
                if ktc >= 2:
                    g.wait_ge(mm.h, mm.v)
                kld[sl].inc(g.dma_start(
                    out=kt_s[:, sl, :, :],
                    in_=a["kT_i"][:, ktc * TQ:(ktc + 1) * TQ]
                    .rearrange("(k p) t -> p k t", p=P)), 16)
                g.wait_ge(vld[sl].h, vld[sl].v)
                vld[sl].inc(g.dma_start(
                    out=vt_s[:, sl, :, :],
                    in_=a["agv_o"][ktc * TQ:(ktc + 1) * TQ, :]
                    .rearrange("(a p) d -> p a d", p=P)), 16)

            def st_prod(t, job, pm, sl=sl):
                kc, qf = job
                t.wait_ge(kld[sl].h, kld[sl].v)
                t.wait_ge(ld.h, ld.v)
                t.wait_ge(cst.h, cst.v)
                for k in range(KC):
                    t.matmul(pm[:], kt_s[:, sl, k, kc * P:(kc + 1) * P],
                             qTa[:, k, qf * TQ:(qf + 1) * TQ],
                             start=(k == 0), stop=False)
                return t.matmul(pm[:], onesb[:, 0:P],
                                qaug[:, qf * TQ:(qf + 1) * TQ],
                                start=False, stop=True)

            st_ev = pipe(pr, f"f1st{ktc}",
                         [(kc, qf) for kc in range(4) for qf in range(4)],
                         [p_s0, p_s1], st_prod,
                         lambda e, job, pm, s_, sl=sl: e.activation(
                             ST[:, sl, job[0], job[1] * TQ:(job[1] + 1) * TQ],
                             pm[:], AF.Exp, scale=1.0 / 32.0),
                         c_entry=[(mm, mm.v)])

            for qc in range(NT):
                pj = qc % 2

                @block.tensor
                def _(t, sl=sl, qc=qc, pj=pj, stv=st_ev.v, evv=evV.v):
                    t.wait_ge(st_ev.h, stv)
                    t.wait_ge(vld[sl].h, vld[sl].v)
                    t.wait_ge(evV.h, max(0, evv - 1))
                    for kc in range(4):
                        t.matmul(p_d[pj][:], ST[:, sl, kc, qc * P:(qc + 1) * P],
                                 ones_red[:], start=(kc == 0), stop=(kc == 3))
                    last = None
                    for half in range(2):
                        for kc in range(4):
                            last = t.matmul(
                                p_av[pj][half][:],
                                ST[:, sl, kc, qc * P:(qc + 1) * P],
                                vt_s[:, sl, kc, half * TQ:(half + 1) * TQ],
                                start=(kc == 0), stop=(kc == 3))
                    mm.inc(last)

                @block.vector
                def _(v, ktc=ktc, qc=qc, pj=pj):
                    v.wait_ge(mm.h, mm.v)
                    if ktc == 0:
                        v.tensor_copy(den_s[:, qc:qc + 1], p_d[pj][:])
                        v.tensor_copy(y_s[:, qc, 0:TQ], p_av[pj][0][:])
                        evV.inc(v.tensor_copy(y_s[:, qc, TQ:D],
                                              p_av[pj][1][:]))
                    else:
                        v.tensor_add(den_s[:, qc:qc + 1], den_s[:, qc:qc + 1],
                                     p_d[pj][:])
                        v.tensor_add(y_s[:, qc, 0:TQ], y_s[:, qc, 0:TQ],
                                     p_av[pj][0][:])
                        evV.inc(v.tensor_add(y_s[:, qc, TQ:D],
                                             y_s[:, qc, TQ:D],
                                             p_av[pj][1][:]))

        for i in range(NT):
            sl2 = i % 2

            @block.gpsimd
            def _(g, i=i, sl2=sl2):
                g.wait_ge(dmy[sl2].h, dmy[sl2].v)
                g.wait_ge(evV.h, evV.v)
                pr.sem("xnld").inc(g.dma_start(
                    out=xn_t[:, sl2, :],
                    in_=a["xn_d"][i * P:(i + 1) * P, :]), 16)

            @block.vector
            def _(v, i=i, sl2=sl2):
                v.wait_ge(mm.h, mm.v)
                v.wait_ge(evV.h, evV.v)
                v.wait_ge(evS.h, evS.v)
                evV.inc(v.reciprocal(rcol[:, sl2:sl2 + 1],
                                     den_s[:, i:i + 1]))

            @block.scalar
            def _(s, i=i, sl2=sl2):
                s.wait_ge(evV.h, evV.v)
                s.wait_ge(dmy[sl2].h, dmy[sl2].v)
                evS.inc(s.activation(yst[:, sl2, :], y_s[:, i, :],
                                     AF.Identity, scale=rcol[:, sl2:sl2 + 1]))

            @block.vector
            def _(v, i=i, sl2=sl2):
                v.wait_ge(evS.h, evS.v)
                v.wait_ge(pr.sem("xnld").h, pr.sem("xnld").v)
                evV.inc(v.tensor_add(yst[:, sl2, :], yst[:, sl2, :],
                                     xn_t[:, sl2, :]))

            @block.sync
            def _(sy, i=i, sl2=sl2):
                sy.wait_ge(evV.h, evV.v)
                dmy[sl2].inc(sy.dma_start(
                    out=a["y_d"][i * P:(i + 1) * P, :],
                    in_=yst[:, sl2, :]), 16)

        emit_barrier(pr)

    with contextlib.ExitStack() as stF:
        sbF = lambda nm, *aa: stF.enter_context(nc.sbuf_tensor("B" + nm, *aa))
        bf_p = alloc_fff_persist(sbF, 1)

        with contextlib.ExitStack() as phB:
            sbB = lambda nm, *aa: phB.enter_context(nc.sbuf_tensor("B" + nm, *aa))
            psB = lambda nm, *aa: phB.enter_context(nc.psum_tensor("B" + nm, *aa))
            bf = dict(alloc_fff_route_bufs(sbB, psB), **bf_p)
            y_s = sbB("f1_ys", [P, NT, D], F32)
            yT_s = sbB("f1_yT", [P, KC, T], BF16)
            yT_lo = sbB("f1_yTl", [P, KC, T], BF16)
            nwT_s = sbB("f1_fnwT", [P, KC, 255], BF16)
            nwT_lo = sbB("f1_fnwTl", [P, KC, 255], BF16)
            nb_f = sbB("f1_fnbf", [1, 255], F32)
            nb_row = sbB("f1_fnbr", [1, 255], BF16)
            nb_lo = sbB("f1_fnbl", [1, 255], BF16)
            p_tr0 = psB("f1_btr0", [P, P], F32)
            p_tr1 = psB("f1_btr1", [P, P], F32)
            nwT_f = bf["lg"][:, 0:KC, 0:255]

            @block.gpsimd
            def _(g, nwT_f=nwT_f):
                g.wait_ge(ld.h, ld.v)
                ld.inc(g.dma_start(out=y_s[:],
                                   in_=a["y_d"].rearrange("(n p) d -> p n d", p=P)), 16)
                for nm in ("triu_strict", "g16col", "iota256col", "identf"):
                    tgt = {"triu_strict": bf["triu"], "g16col": bf["g16"],
                           "iota256col": bf["iota2"], "identf": bf["identf"]}[nm]
                    ld.inc(g.dma_start(out=tgt[:], in_=cst_d[nm]), 16)
                ld.inc(g.dma_start(out=bf["identb"][:], in_=cst_d["identf"]), 16)
                ld.inc(g.dma_start(out=nwT_f,
                                   in_=wd["f"]["nwT"][0:D, :].rearrange("(k p) n -> p k n", p=P)), 16)
                ld.inc(g.dma_start(out=nb_f[:], in_=wd["f"]["nwT"][D:D + 1, :]), 16)

            @block.vector
            def _(v, nwT_f=nwT_f):
                v.memset(bf["onesf"][:], 1.0)
                v.memset(bf["onesfbc"][:], 1.0)
                cst.inc(v.memset(bf["onesbc1"][:], 1.0))
                v.wait_ge(ld.h, ld.v)
                for c2 in range(2):
                    cst.inc(v.tensor_copy(bf["rhs2"][:, c2, 1:2],
                                          bf["iota2"][:, c2:c2 + 1]))
                evV.inc(v.tensor_copy(nwT_s[:], nwT_f))
                evV.inc(v.tensor_copy(nb_row[:], nb_f[:]))
                v.wait_ge(evV.h, evV.v)
                evV.inc(v.scalar_tensor_tensor(nwT_lo[:], nwT_f, 1.0,
                                               nwT_s[:], ALU.mult,
                                               ALU.subtract))
                evV.inc(v.scalar_tensor_tensor(nb_lo[:], nb_f[:], 1.0,
                                               nb_row[:], ALU.mult,
                                               ALU.subtract))

            tr_mm = pr.sem("trmm")
            tr_ev = pr.sem("trev")
            trjobs = [(i, k) for i in range(NT) for k in range(KC)]
            trm_base, tre_base = tr_mm.v, tr_ev.v

            @block.tensor
            def _(t):
                t.wait_ge(ld.h, ld.v)
                for j, (i, k) in enumerate(trjobs):
                    if j >= 2:
                        t.wait_ge(tr_ev.h, tre_base + 2 * (j - 1))
                    tr_mm.inc(t.transpose(
                        (p_tr0 if j % 2 == 0 else p_tr1)[:],
                        y_s[:, i, k * P:(k + 1) * P], bf["identf"][:]))

            @block.vector
            def _(v):
                for j, (i, k) in enumerate(trjobs):
                    pm = (p_tr0 if j % 2 == 0 else p_tr1)
                    v.wait_ge(tr_mm.h, trm_base + j + 1)
                    tr_ev.inc(v.tensor_copy(
                        yT_s[:, k, i * P:(i + 1) * P], pm[:]))
                    v.wait_ge(tr_ev.h, tr_ev.v)
                    tr_ev.inc(v.scalar_tensor_tensor(
                        yT_lo[:, k, i * P:(i + 1) * P], pm[:], 1.0,
                        yT_s[:, k, i * P:(i + 1) * P],
                        ALU.mult, ALU.subtract))

            emit_fff_route(pr, bf, 0, lambda i: y_s[:, i, :],
                           yT_s, yT_lo, nwT_s, nwT_lo, nb_row, nb_lo,
                           Xs_d["f"], gl_d["f"],
                           entry=[(tr_ev, tr_ev.v), (ld, ld.v),
                                  (cst, cst.v), (evV, evV.v)])
            emit_barrier(pr)

        with contextlib.ExitStack() as phC:
            sbC = lambda nm, *aa: phC.enter_context(nc.sbuf_tensor("B" + nm, *aa))
            psC = lambda nm, *aa: phC.enter_context(nc.psum_tensor("B" + nm, *aa))
            cb = alloc_fff_comp_bufs(sbC, psC, 23)
            bf = dict(bf_p)
            bf["act_fn"] = AF.Gelu
            bf["OutS_d"] = OutS_d["f"]

            @block.gpsimd
            def _(g):
                g.wait_ge(ld.h, ld.v)
                ld.inc(g.dma_start(out=cb["posmap"][:], in_=cst_d["posmap"]), 16)
                ld.inc(g.dma_start(out=cb["jcol"][:], in_=cst_d["jcol"]), 16)
                ld.inc(g.dma_start(out=cb["bd"][:], in_=cst_d["blockdiag"]), 16)

            emit_fff_compute(pr, bf, cb, 0, wd["f"]["w12aug"], wd["f"]["b2"],
                             Xs_d["f"], gl_d["f"], a["fout_d"], half=23)
            emit_barrier(pr)

    with contextlib.ExitStack() as phD:
        sbD = lambda *aa: phD.enter_context(nc.sbuf_tensor(*aa))
        sub_s = sbD("f1_sub", [P, NT, D], F32)
        str_t = sbD("f1_str", [P, 2, D], F32)
        x1_t = sbD("f1_x1t", [P, 2, D], F32)
        xh_t = sbD("f1_xh", [P, 2, D], F32)
        stg = sbD("f1_stg", [P, 4, D], F32)
        dms = [pr.sem(f"dmD{i}") for i in range(4)]
        yl = [pr.sem(f"yl{i}") for i in range(2)]
        fl = [pr.sem(f"fl{i}") for i in range(2)]
        x1l = [pr.sem(f"x1l{i}") for i in range(2)]
        yl_base = [s_.v for s_ in yl]
        fl_base = [s_.v for s_ in fl]
        x1l_base = [s_.v for s_ in x1l]

        @block.gpsimd
        def _(g):
            g.wait_ge(ld.h, ld.v)
            base = evV.v
            for i in range(NT):
                sl = i % 2
                if i >= 2:
                    g.wait_ge(evV.h, base + (i - 1))
                g.wait_ge(yl[sl].h, yl[sl].v)
                yl[sl].inc(g.dma_start(out=str_t[:, sl, :],
                                       in_=a["y_d"][i * P:(i + 1) * P, :]), 16)
                g.wait_ge(fl[sl].h, fl[sl].v)
                fl[sl].inc(g.dma_start(out=sub_s[:, i, :],
                                       in_=a["fout_d"][i * P:(i + 1) * P, :]), 16)

        @block.vector
        def _(v):
            for i in range(NT):
                sl = i % 2
                v.wait_ge(yl[sl].h, yl_base[sl] + 16 * (i // 2 + 1))
                v.wait_ge(fl[sl].h, fl_base[sl] + 16 * (i // 2 + 1))
                evV.inc(v.tensor_add(sub_s[:, i, :], sub_s[:, i, :],
                                     str_t[:, sl, :]))

        ln_f = emit_ln(pr, lambda i: sub_s[:, i, :], list(range(NT)),
                       cols, 1e-5, entry=[(evV, evV.v)], apply=False)

        @block.gpsimd
        def _(g):
            base = evV.v
            for i in range(NT):
                sl = i % 2
                if i >= 2:
                    g.wait_ge(evV.h, base + (i - 1))
                g.wait_ge(x1l[sl].h, x1l[sl].v)
                x1l[sl].inc(g.dma_start(out=x1_t[:, sl, :],
                                        in_=a["x1_d"][i * P:(i + 1) * P, :]), 16)

        for i in range(NT):
            @block.scalar
            def _(s, i=i):
                s.wait_ge(ln_f.h, ln_f.v)
                if i >= 2:
                    s.wait_ge(evV.h, evV.v)
                evS.inc(s.activation(xh_t[:, i % 2, :], sub_s[:, i, :],
                                     AF.Identity,
                                     bias=cols["nmr"][:, i:i + 1],
                                     scale=cols["rstd"][:, i:i + 1]))

            @block.vector
            def _(v, i=i):
                v.wait_ge(evS.h, evS.v)
                v.wait_ge(x1l[i % 2].h, x1l_base[i % 2] + 16 * (i // 2 + 1))
                sl = i % 4
                v.wait_ge(dms[sl].h, dms[sl].v)
                evV.inc(v.tensor_add(stg[:, sl, :], xh_t[:, i % 2, :],
                                     x1_t[:, i % 2, :]))

            @block.sync
            def _(sy, i=i):
                sy.wait_ge(evV.h, evV.v)
                sl = i % 4
                sy.wait_ge(dms[sl].h, dms[sl].v)
                dms[sl].inc(sy.dma_start(out=a["x2_d"][i * P:(i + 1) * P, :],
                                         in_=stg[:, sl, :]), 16)

        emit_barrier(pr)


def emit_halo(pr, nc, a, src_d, agin, agout, nrows, hs_buf, x_s, tag):
    block = pr.block
    ld = pr.sem("ld")
    cst = pr.sem("cst")
    ccs = pr.sem("ccs")
    evV = pr.sem("ev_vector")
    hdm = pr.sem("hdm")

    @block.sync
    def _(sy):
        sy.wait_ge(hdm.h, hdm.v)
        hdm.inc(sy.dma_start(out=agin, in_=src_d[T - nrows:T, :]), 16)

    @block.gpsimd
    def _(g):
        g.wait_ge(hdm.h, hdm.v)
        ccs.inc(g.collective_compute(
            "AllGather", ALU.bypass, replica_groups=a["RG"],
            ins=[agin.opt()], outs=[agout.opt()]), 1)
        g.wait_ge(ccs.h, ccs.v)
        ld.inc(g.dma_start(
            out=hs_buf[0:nrows, 0:4, :],
            in_=agout.rearrange("(j r) d -> r j d", r=nrows)), 16)

    @block.vector
    def _(v):
        v.wait_ge(ld.h, ld.v)
        v.wait_ge(cst.h, cst.v)
        evV.inc(v.memset(hs_buf[0:nrows, 4, :], 0.0))
        for j in range(4):
            v.wait_ge(evV.h, evV.v)
            evV.inc(v.scalar_tensor_tensor(
                hs_buf[0:nrows, 4, :], hs_buf[0:nrows, j, :],
                a["hmask"][0:nrows, j:j + 1], hs_buf[0:nrows, 4, :],
                ALU.mult, ALU.add))

    @block.gpsimd
    def _(g):
        g.wait_ge(evV.h, evV.v)
        g.wait_ge(ld.h, ld.v)
        ld.inc(g.dma_start(out=x_s[P - nrows:P, 0, :],
                           in_=hs_buf[0:nrows, 4, :]), 16)
    return ld


def emit_l1attn_fused(pr, nc, a):
    block = pr.block
    ld = pr.sem("ld")
    cst = pr.sem("cst")
    evV = pr.sem("ev_vector")
    evS = pr.sem("ev_scalar")
    mm = pr.sem("mm")
    ccs = pr.sem("ccs")
    wld = pr.sem("wld")
    dms = [pr.sem(f"dm{i}") for i in range(4)]
    NCH = NT

    with contextlib.ExitStack() as st0:
        sb0 = lambda *aa: st0.enter_context(nc.sbuf_tensor(*aa))
        identf = sb0("h2_idf", [P, P], F32)
        cols = alloc_ln_cols(sb0, nt=NTH, tag="h2")

        @block.gpsimd
        def _(g):
            g.wait_ge(ld.h, ld.v)
            ld.inc(g.dma_start(out=identf[:], in_=a["identf_d"]), 16)

        with contextlib.ExitStack() as st1:
            sb1 = lambda *aa: st1.enter_context(nc.sbuf_tensor(*aa))
            ps1 = lambda *aa: st1.enter_context(nc.psum_tensor(*aa))
            x_s = sb1("h2_x", [P, NTH, D], F32)
            lnT_s = sb1("h2_lnT", [P, KC, TH], BF16)
            w_s = sb1("h2_w", [P, KC, D], BF16)
            cv_s = sb1("h2_cv", [P, KC, 4], F32)
            cv1_s = sb1("h2_cv1", [P, KC], F32)
            pt_s = sb1("h2_pt", [P, TH], F32)
            ac_s = sb1("h2_ac", [P, T], F32)
            acb_s = sb1("h2_acb", [P, T], BF16)
            tok_s = sb1("h2_tok", [P, NT, P], F32)
            sscol = sb1("h2_ss", [P, 2 * NT], F32)
            stage = sb1("h2_stg", [P, 4, P], F32)
            gstage_s = sb1("h2_gstg", [P, 4, TQ], F32)
            hs_buf = sb1("h2_hs", [3, 5, D], F32)
            p_mm0 = ps1("h2_pm0", [P, TQ], F32)
            p_mm1 = ps1("h2_pm1", [P, TQ], F32)
            p_tr0 = ps1("h2_pt0", [P, P], F32)
            p_tr1 = ps1("h2_pt1", [P, P], F32)

            @block.gpsimd
            def _(g):
                g.wait_ge(ld.h, ld.v)
                ld.inc(g.dma_start(
                    out=x_s[:, 1:NTH, :],
                    in_=a["x2_d"].rearrange("(n p) d -> p n d", p=P)), 16)

            @block.vector
            def _(v):
                cst.inc(v.memset(x_s[:, 0, :], 0.0))

            hdm = emit_halo(pr, nc, a, a["x2_d"], a["agh3in"], a["agh3out"],
                            3, hs_buf, x_s, "h2")

            ln1 = emit_ln(pr, lambda i: x_s[:, i, :], list(range(NTH)), cols,
                          1e-5, entry=[(ld, ld.v), (cst, cst.v), (hdm, hdm.v)])

            tr_ev = pipe(
                pr, "h2tr", [(i, k) for i in range(NTH) for k in range(KC)],
                [p_tr0, p_tr1],
                lambda t, job, pm: t.transpose(
                    pm[:], x_s[:, job[0], job[1] * P:(job[1] + 1) * P],
                    identf[:]),
                lambda e, job, pm, sl: e.tensor_copy(
                    lnT_s[:, job[1], job[0] * P:(job[0] + 1) * P], pm[:]),
                t_entry=[(ln1, ln1.v), (ld, ld.v)], consumer_engine="vector")

            for si, (wnm, cnm, onm, do_norm) in enumerate(
                    (("WqT", "conv_q", "q5", True),
                     ("WkT", "conv_k", "k5", True),
                     ("WvT", "conv_v", "v5", False))):
                @block.gpsimd
                def _(g, wnm=wnm, cnm=cnm):
                    g.wait_ge(wld.h, wld.v)
                    g.wait_ge(mm.h, mm.v)
                    wld.inc(g.dma_start(
                        out=w_s[:],
                        in_=a["w1T"][wnm].rearrange("(k p) d -> p k d", p=P)), 16)
                    wld.inc(g.dma_start(
                        out=cv_s[:],
                        in_=a["cv"][cnm].rearrange("(k p) j -> p k j", p=P)), 16)

                @block.vector
                def _(v):
                    v.wait_ge(wld.h, wld.v)
                    v.wait_ge(evS.h, evS.v)
                    evV.inc(v.tensor_scalar(
                        cv1_s[:], cv_s[:, :, 3:4].rearrange("p k a -> p (k a)"),
                        1.0, None, ALU.add))

                for hp in range(KC):
                    def pj_prod(t, tc, pm, hp=hp):
                        for k in range(KC):
                            mmi = t.matmul(pm[:], w_s[:, k, hp * P:(hp + 1) * P],
                                           lnT_s[:, k, tc * P:(tc + 1) * P],
                                           start=(k == 0), stop=(k == KC - 1))
                        return mmi

                    pj_ev = pipe(pr, f"h2pj{si}_{hp}", list(range(NTH)),
                                 [p_tr0, p_tr1], pj_prod,
                                 lambda e, tc, pm, sl: e.copy(
                                     pt_s[:, tc * P:(tc + 1) * P], pm[:]),
                                 t_entry=[(tr_ev, tr_ev.v), (wld, wld.v),
                                          (evV, evV.v), (evS, evS.v)])

                    @block.vector
                    def _(v, hp=hp, v0=pj_ev.v):
                        v.wait_ge(pj_ev.h, v0)
                        evV.inc(v.tensor_scalar(ac_s[:], pt_s[:, P:TH],
                                                cv1_s[:, hp:hp + 1], None,
                                                ALU.mult))
                        for j, sh in ((2, 1), (1, 2), (0, 3)):
                            v.wait_ge(evV.h, evV.v)
                            evV.inc(v.scalar_tensor_tensor(
                                ac_s[:], pt_s[:, P - sh:TH - sh],
                                cv_s[:, hp, j:j + 1], ac_s[:], ALU.mult, ALU.add))

                    @block.scalar
                    def _(s, hp=hp):
                        s.wait_ge(evV.h, evV.v)
                        evS.inc(s.activation(ac_s[:], ac_s[:], AF.Silu))
                        s.wait_ge(evS.h, evS.v)
                        evS.inc(s.activation(acb_s[:], ac_s[:], AF.Identity))

                    tt_ev = pipe(
                        pr, f"h2tt{si}_{hp}", list(range(NT)), [p_tr0, p_tr1],
                        lambda t, i, pm: t.transpose(
                            pm[:], ac_s[:, i * P:(i + 1) * P], identf[:]),
                        lambda e, i, pm, sl: e.tensor_copy(tok_s[:, i, :], pm[:]),
                        t_entry=[(evS, evS.v)], consumer_engine="vector")

                    if do_norm:
                        for hh in range(2):
                            @block.scalar
                            def _(s, hh=hh, base_v=tt_ev.v):
                                s.wait_ge(tt_ev.h, base_v)
                                for i in range(NT):
                                    s.wait_ge(evS.h, evS.v)
                                    evS.inc(s.activation(
                                        cols["trash"][:, 0:DH],
                                        tok_s[:, i, hh * DH:(hh + 1) * DH],
                                        AF.Square,
                                        accum_out=sscol[:, hh * NT + i:hh * NT + i + 1]))

                        @block.vector
                        def _(v):
                            v.wait_ge(evS.h, evS.v)
                            evV.inc(v.tensor_scalar(sscol[:], sscol[:], 1e-12,
                                                    None, ALU.add))

                        @block.scalar
                        def _(s):
                            s.wait_ge(evV.h, evV.v)
                            evS.inc(s.activation(sscol[:], sscol[:], AF.Sqrt))

                        @block.vector
                        def _(v):
                            v.wait_ge(evS.h, evS.v)
                            evV.inc(v.reciprocal(sscol[:], sscol[:]))

                    for i in range(NT):
                        @block.scalar
                        def _(s, i=i, do_norm=do_norm, v0=tt_ev.v):
                            s.wait_ge(evV.h, evV.v)
                            s.wait_ge(tt_ev.h, v0)
                            sl = i % 4
                            s.wait_ge(dms[sl].h, dms[sl].v)
                            if do_norm:
                                s.activation(stage[:, sl, 0:DH],
                                             tok_s[:, i, 0:DH], AF.Identity,
                                             scale=sscol[:, i:i + 1])
                                evS.inc(s.activation(
                                    stage[:, sl, DH:P], tok_s[:, i, DH:P],
                                    AF.Identity,
                                    scale=sscol[:, NT + i:NT + i + 1]))
                            else:
                                evS.inc(s.activation(stage[:, sl, :],
                                                     tok_s[:, i, :], AF.Identity))

                        @block.sync
                        def _(sy, i=i, hp=hp, onm=onm):
                            sy.wait_ge(evS.h, evS.v)
                            sl = i % 4
                            sy.wait_ge(dms[sl].h, dms[sl].v)
                            dms[sl].inc(sy.dma_start(
                                out=a[onm][i * P:(i + 1) * P,
                                           hp * P:(hp + 1) * P],
                                in_=stage[:, sl, :]), 16)

            @block.gpsimd
            def _(g):
                g.wait_ge(wld.h, wld.v)
                g.wait_ge(mm.h, mm.v)
                wld.inc(g.dma_start(
                    out=w_s[:],
                    in_=a["w1T"]["WgT"].rearrange("(k p) d -> p k d", p=P)), 16)

            def gt_prod(t, job, pm):
                i, dq = job
                t.wait_ge(wld.h, wld.v)
                for k in range(KC):
                    mmi = t.matmul(pm[:], lnT_s[:, k, (i + 1) * P:(i + 2) * P],
                                   w_s[:, k, dq * TQ:(dq + 1) * TQ],
                                   start=(k == 0), stop=(k == KC - 1))
                return mmi

            pipe(pr, "h2gt", [(i, dq) for i in range(NT) for dq in range(2)],
                 [p_mm0, p_mm1], gt_prod,
                 lambda e, job, pm, sl: e.tensor_copy(gstage_s[:, sl, :], pm[:]),
                 t_entry=[(tr_ev, tr_ev.v), (evS, evS.v), (evV, evV.v)],
                 consumer_engine="vector",
                 dma_emit=lambda sy, job, sl: sy.dma_start(
                     out=a["gate_i"][job[0] * P:(job[0] + 1) * P,
                                     job[1] * TQ:(job[1] + 1) * TQ],
                     in_=gstage_s[:, sl, :]),
                 nstage=4, stage_buf=None)
            emit_barrier(pr)

        with contextlib.ExitStack() as stG:
            sbG = lambda *aa: stG.enter_context(nc.sbuf_tensor(*aa))
            tb = sbG("h2_tb", [P, 2, D], BF16)
            ttb = sbG("h2_ttb", [P, T], BF16)
            gld = [pr.sem("gl0"), pr.sem("gl1")]

            for src, dst in ((a["q5"], a["qb5_i"]), (a["k5"], a["kb5_i"])):
                for i in range(NT):
                    sl = i % 2

                    @block.gpsimd
                    def _(g, src=src, i=i, sl=sl):
                        g.wait_ge(dms[sl].h, dms[sl].v)
                        g.wait_ge(gld[sl].h, gld[sl].v)
                        gld[sl].inc(g.dma_start(out=tb[:, sl, :],
                                                in_=src[i * P:(i + 1) * P, :]), 16)

                    @block.sync
                    def _(sy, dst=dst, i=i, sl=sl):
                        sy.wait_ge(gld[sl].h, gld[sl].v)
                        sy.wait_ge(dms[sl].h, dms[sl].v)
                        dms[sl].inc(sy.dma_start(out=dst[i * P:(i + 1) * P, :],
                                                 in_=tb[:, sl, :]), 16)

            @block.sync
            def _(sy):
                sy.wait_ge(dms[0].h, dms[0].v)
                sy.wait_ge(dms[1].h, dms[1].v)
                for src, dst in ((a["qb5_i"], a["qT5_i"]),
                                 (a["kb5_i"], a["kT5_i"])):
                    for dcb in range(KC):
                        sy.wait_ge(dms[2].h, dms[2].v)
                        dms[2].inc(sy.dma_start(
                            out=ttb[:],
                            in_=src[:, dcb * P:(dcb + 1) * P],
                            transpose=True), 16)
                        sy.wait_ge(dms[2].h, dms[2].v)
                        dms[2].inc(sy.dma_start(
                            out=dst[dcb * P:(dcb + 1) * P, :],
                            in_=ttb[:]), 16)
            emit_barrier(pr)

        stOG = contextlib.ExitStack()
        ogT_s = stOG.enter_context(nc.sbuf_tensor("h2_ogT", [P, KC, T], BF16))
        with contextlib.ExitStack() as stN:
            sbN = lambda *aa: stN.enter_context(nc.sbuf_tensor(*aa))
            num5 = sbN("h2_num", [P, NT, D], F32)

            with contextlib.ExitStack() as stS:
                sbS = lambda *aa: stS.enter_context(nc.sbuf_tensor(*aa))
                psS = lambda *aa: stS.enter_context(nc.psum_tensor(*aa))
                qt_s = sbS("h2_qts", [DH, T], BF16)
                kt_s = sbS("h2_kts", [DH, T], BF16)
                ktok_s = sbS("h2_ktok", [P, NCH, DH], BF16)
                v_s = sbS("h2_vs", [P, NCH, DH], BF16)
                mask_s = sbS("h2_mask", [P, P], BF16)
                a_bf = sbS("h2_abf", [P, P], BF16)
                state_s = sbS("h2_st", [DH, DH], F32)
                state_bf = sbS("h2_stb", [DH, DH], BF16)
                w0_s = sbS("h2_w0", [DH, DH], F32)
                p_a = psS("h2_pa", [P, P], F32)
                p_o = psS("h2_po", [P, DH], F32)
                p_u = psS("h2_pu", [DH, DH], F32)

                @block.gpsimd
                def _(g):
                    g.wait_ge(ld.h, ld.v)
                    ld.inc(g.dma_start(out=mask_s[:], in_=a["maskut_d"]), 16)

                for h in range(H):
                    @block.gpsimd
                    def _(g, h=h):
                        g.wait_ge(ld.h, ld.v)
                        g.wait_ge(mm.h, mm.v)
                        g.wait_ge(evS.h, evS.v)
                        g.wait_ge(dms[0].h, dms[0].v)
                        ld.inc(g.dma_start(out=qt_s[:],
                                           in_=a["qT5_i"][h * DH:(h + 1) * DH, :]), 16)
                        ld.inc(g.dma_start(out=kt_s[:],
                                           in_=a["kT5_i"][h * DH:(h + 1) * DH, :]), 16)
                        ld.inc(g.dma_start(
                            out=ktok_s[:],
                            in_=a["k5"].rearrange("(n p) d -> p n d", p=P)
                            [:, :, h * DH:(h + 1) * DH]), 16)
                        ld.inc(g.dma_start(
                            out=v_s[:],
                            in_=a["v5"].rearrange("(n p) d -> p n d", p=P)
                            [:, :, h * DH:(h + 1) * DH]), 16)
                        ld.inc(g.dma_start(out=state_s[:],
                                           in_=a["w0_d"][h * DH:(h + 1) * DH, :]), 16)
                        ld.inc(g.dma_start(out=w0_s[:],
                                           in_=a["w0_d"][h * DH:(h + 1) * DH, :]), 16)

                    @block.scalar
                    def _(s):
                        s.wait_ge(ld.h, ld.v)
                        evS.inc(s.activation(state_bf[:], state_s[:], AF.Identity))

                    for c in range(NCH):
                        tok = slice(c * P, (c + 1) * P)

                        @block.tensor
                        def _(t, tok=tok):
                            t.wait_ge(ld.h, ld.v)
                            t.wait_ge(evV.h, evV.v)
                            t.wait_ge(evS.h, evS.v)
                            mm.inc(t.matmul(p_a[:], kt_s[:, tok], qt_s[:, tok],
                                            start=True, stop=True))

                        @block.vector
                        def _(v):
                            v.wait_ge(mm.h, mm.v)
                            evV.inc(v.tensor_tensor(a_bf[:], p_a[:], mask_s[:],
                                                    ALU.mult))

                        @block.tensor
                        def _(t, c=c, tok=tok):
                            t.wait_ge(evV.h, evV.v)
                            t.wait_ge(evS.h, evS.v)
                            t.matmul(p_o[:], a_bf[:], v_s[:, c, :],
                                     start=True, stop=False)
                            mm.inc(t.matmul(p_o[:], qt_s[:, tok], state_bf[:],
                                            start=False, stop=True))
                            mm.inc(t.matmul(p_u[:], ktok_s[:, c, :], v_s[:, c, :],
                                            start=True, stop=True))

                        @block.scalar
                        def _(s, c=c, h=h):
                            s.wait_ge(mm.h, mm.v - 1)
                            evS.inc(s.activation(
                                num5[:, c, h * DH:(h + 1) * DH], p_o[:],
                                AF.Identity))

                        @block.vector
                        def _(v):
                            v.wait_ge(mm.h, mm.v)
                            evV.inc(v.tensor_add(state_s[:], state_s[:], p_u[:]))

                        @block.scalar
                        def _(s):
                            s.wait_ge(evV.h, evV.v)
                            evS.inc(s.activation(state_bf[:], state_s[:],
                                                 AF.Identity))

                    @block.vector
                    def _(v):
                        v.wait_ge(evS.h, evS.v)
                        evV.inc(v.tensor_sub(state_s[:], state_s[:], w0_s[:]))

                    @block.sync
                    def _(sy, h=h):
                        sy.wait_ge(evV.h, evV.v)
                        sy.wait_ge(dms[0].h, dms[0].v)
                        dms[0].inc(sy.dma_start(
                            out=a["ag1_i"][h * DH:(h + 1) * DH, :],
                            in_=state_s[:]), 16)

                emit_barrier(pr)

            @block.gpsimd
            def _(g):
                g.wait_ge(dms[0].h, dms[0].v)
                ccs.inc(g.collective_compute(
                    "AllGather", ALU.bypass, replica_groups=a["RG"],
                    ins=[a["ag1_i"].opt()], outs=[a["ag1_o"].opt()]), 1)

            with contextlib.ExitStack() as stC:
                sbC = lambda *aa: stC.enter_context(nc.sbuf_tensor(*aa))
                psC = lambda *aa: stC.enter_context(nc.psum_tensor(*aa))
                qt_s = sbC("h2_qt2", [DH, T], BF16)
                agt = sbC("h2_agt", [DH, 4 * H, DH], F32)
                G_s = sbC("h2_G", [DH, H, DH], F32)
                G_b = sbC("h2_Gb", [DH, H, DH], BF16)
                Bg = sbC("h2_Bg", [P, DH], F32)
                Bb = sbC("h2_Bb", [P, DH], F32)
                lng_s = sbC("h2_lng", [1, DH], F32)
                lnb_s = sbC("h2_lnb", [1, DH], F32)
                onesf_bc = sbC("h2_ofb", [1, P], F32)
                xh_s = sbC("h2_xhs", [P, 2, DH], F32)
                sq3 = sbC("h2_sq3", [P, NT, DH], F32)
                c_sum = sbC("h2_cs", [P, NT], F32)
                c_sq = sbC("h2_cq", [P, NT], F32)
                c_mu = sbC("h2_cm", [P, NT], F32)
                c_var = sbC("h2_cv2", [P, NT], F32)
                c_rstd = sbC("h2_cr", [P, NT], F32)
                c_nmr = sbC("h2_cn", [P, NT], F32)
                p_o0 = psC("h2_2po0", [P, DH], F32)
                p_o1 = psC("h2_2po1", [P, DH], F32)
                p_g = psC("h2_2pg", [P, DH], F32)

                @block.gpsimd
                def _(g):
                    g.wait_ge(ld.h, ld.v)
                    ld.inc(g.dma_start(out=lng_s[:], in_=a["lng_d"]), 16)
                    g.wait_ge(ld.h, ld.v)
                    ld.inc(g.dma_start(out=lnb_s[:], in_=a["lnb_d"]), 16)
                    g.wait_ge(ccs.h, ccs.v)
                    ld.inc(g.dma_start(
                        out=agt[:],
                        in_=a["ag1_o"].rearrange("(j h p) d -> p (j h) d",
                                                 p=DH, h=H)), 16)

                @block.vector
                def _(v):
                    cst.inc(v.memset(onesf_bc[:], 1.0))
                    v.wait_ge(ld.h, ld.v)
                    evV.inc(v.memset(G_s[:], 0.0))
                    for j in range(4):
                        v.wait_ge(evV.h, evV.v)
                        evV.inc(v.scalar_tensor_tensor(
                            G_s[:], agt[:, j * H:(j + 1) * H, :],
                            a["pmask"][0:DH, j:j + 1], G_s[:],
                            ALU.mult, ALU.add))

                @block.scalar
                def _(s):
                    s.wait_ge(evV.h, evV.v)
                    evS.inc(s.activation(G_b[:], G_s[:], AF.Identity))

                g_ev = pipe(pr, "h2g", [0, 1], [p_g],
                            lambda t, j, pm: t.matmul(
                                pm[:], onesf_bc[:],
                                (lng_s if j == 0 else lnb_s)[:],
                                start=True, stop=True),
                            lambda e, j, pm, sl: e.tensor_copy(
                                (Bg if j == 0 else Bb)[:], pm[:]),
                            t_entry=[(ld, ld.v), (cst, cst.v)],
                            consumer_engine="vector")

                for h in range(H):
                    @block.gpsimd
                    def _(g, h=h):
                        g.wait_ge(ld.h, ld.v)
                        g.wait_ge(mm.h, mm.v)
                        ld.inc(g.dma_start(out=qt_s[:],
                                           in_=a["qT5_i"][h * DH:(h + 1) * DH, :]), 16)

                    corr_ev = pipe(
                        pr, f"h2c{h}", list(range(NT)), [p_o0, p_o1],
                        lambda t, i, pm, h=h: t.matmul(
                            pm[:], qt_s[:, i * P:(i + 1) * P], G_b[:, h, :],
                            start=True, stop=True),
                        lambda e, i, pm, sl, h=h: e.tensor_add(
                            num5[:, i, h * DH:(h + 1) * DH],
                            num5[:, i, h * DH:(h + 1) * DH], pm[:]),
                        t_entry=[(ld, ld.v), (evS, evS.v), (evV, evV.v)],
                        consumer_engine="vector")

                    v3 = num5[:, :, h * DH:(h + 1) * DH]

                    @block.vector
                    def _(v, v3=v3, cv=corr_ev.v):
                        v.wait_ge(corr_ev.h, cv)
                        v.wait_ge(evS.h, evS.v)
                        evV.inc(v.reduce_sum(c_sum[:], v3, AX.X))
                        v.wait_ge(evV.h, evV.v)
                        evV.inc(v.tensor_tensor(sq3[:], v3, v3, ALU.mult))
                        v.wait_ge(evV.h, evV.v)
                        evV.inc(v.reduce_sum(c_sq[:], sq3[:], AX.X))
                        v.wait_ge(evV.h, evV.v)
                        evV.inc(v.tensor_scalar_mul(c_mu[:], c_sum[:], 1.0 / DH))
                        v.wait_ge(evV.h, evV.v)
                        v.tensor_tensor(c_var[:], c_mu[:], c_mu[:], ALU.mult)
                        evV.inc(v.tensor_scalar(c_sq[:], c_sq[:], 1.0 / DH,
                                                None, ALU.mult))
                        v.wait_ge(evV.h, evV.v)
                        evV.inc(v.tensor_sub(c_var[:], c_sq[:], c_var[:]))
                        v.wait_ge(evV.h, evV.v)
                        evV.inc(v.tensor_scalar(c_var[:], c_var[:], 1e-5,
                                                None, ALU.add))

                    @block.scalar
                    def _(s):
                        s.wait_ge(evV.h, evV.v)
                        evS.inc(s.activation(c_rstd[:], c_var[:], AF.Sqrt))

                    @block.vector
                    def _(v):
                        v.wait_ge(evS.h, evS.v)
                        evV.inc(v.reciprocal(c_rstd[:], c_rstd[:]))
                        v.wait_ge(evV.h, evV.v)
                        evV.inc(v.scalar_tensor_tensor(
                            c_nmr[:], c_mu[:], -1.0, c_rstd[:],
                            ALU.mult, ALU.mult))

                    for i in range(NT):
                        sl_ = num5[:, i, h * DH:(h + 1) * DH]
                        xsl = xh_s[:, i % 2, :]

                        @block.scalar
                        def _(s, sl_=sl_, xsl=xsl, i=i):
                            s.wait_ge(evV.h, evV.v)
                            evS.inc(s.activation(xsl, sl_, AF.Identity,
                                                 bias=c_nmr[:, i:i + 1],
                                                 scale=c_rstd[:, i:i + 1]))

                        @block.vector
                        def _(v, sl_=sl_, xsl=xsl, gv=g_ev.v):
                            v.wait_ge(evS.h, evS.v)
                            v.wait_ge(g_ev.h, gv)
                            evV.inc(v.tensor_mul(sl_, xsl, Bg[:]))
                            v.wait_ge(evV.h, evV.v)
                            evV.inc(v.tensor_add(sl_, sl_, Bb[:]))

                emit_barrier(pr)

            with contextlib.ExitStack() as stM:
                sbM = lambda *aa: stM.enter_context(nc.sbuf_tensor(*aa))
                psM = lambda *aa: stM.enter_context(nc.psum_tensor(*aa))
                g_t = sbM("h2_gt", [P, 2, D], F32)
                p_tr0 = psM("h2_8t0", [P, P], F32)
                p_tr1 = psM("h2_8t1", [P, P], F32)
                gl = [pr.sem("gl0"), pr.sem("gl1")]
                gl_base = [s_.v for s_ in gl]

                @block.gpsimd
                def _(g):
                    base = evV.v
                    for i in range(NT):
                        sl = i % 2
                        if i >= 2:
                            g.wait_ge(evV.h, base + (i - 1))
                        g.wait_ge(gl[sl].h, gl[sl].v)
                        gl[sl].inc(g.dma_start(
                            out=g_t[:, sl, :],
                            in_=a["gate_i"][i * P:(i + 1) * P, :]), 16)

                @block.vector
                def _(v):
                    for i in range(NT):
                        sl = i % 2
                        v.wait_ge(gl[sl].h, gl_base[sl] + 16 * (i // 2 + 1))
                        evV.inc(v.tensor_mul(num5[:, i, :], num5[:, i, :],
                                             g_t[:, sl, :]))

                pipe(pr, "h2ogt",
                     [(i, k) for i in range(NT) for k in range(KC)],
                     [p_tr0, p_tr1],
                     lambda t, job, pm: t.transpose(
                         pm[:], num5[:, job[0], job[1] * P:(job[1] + 1) * P],
                         identf[:]),
                     lambda e, job, pm, sl: e.tensor_copy(
                         ogT_s[:, job[1], job[0] * P:(job[0] + 1) * P], pm[:]),
                     t_entry=[(evV, evV.v), (ld, ld.v)],
                     consumer_engine="vector")
                emit_barrier(pr)

        with contextlib.ExitStack() as st8:
            sb8 = lambda *aa: st8.enter_context(nc.sbuf_tensor(*aa))
            ps8 = lambda *aa: st8.enter_context(nc.psum_tensor(*aa))
            wo_s = sb8("h2_wo", [P, KC, D], BF16)
            res_s = sb8("h2_res", [P, NT, D], F32)
            x2_t = sb8("h2_x2t", [P, 2, D], F32)
            xh_t = sb8("h2_xht", [P, 2, D], F32)
            stg8 = sb8("h2_stg8", [P, 2, D], F32)
            p_mm0 = ps8("h2_8m0", [P, TQ], F32)
            p_mm1 = ps8("h2_8m1", [P, TQ], F32)
            x2l = [pr.sem("xl0"), pr.sem("xl1")]
            x2l_base = [s_.v for s_ in x2l]

            @block.gpsimd
            def _(g):
                g.wait_ge(ld.h, ld.v)
                ld.inc(g.dma_start(
                    out=wo_s[:],
                    in_=a["w1T"]["WoT"].rearrange("(k p) d -> p k d", p=P)), 16)

            def r_prod(t, job, pm):
                i, dq = job
                t.wait_ge(ld.h, ld.v)
                for k in range(KC):
                    mmi = t.matmul(pm[:], ogT_s[:, k, i * P:(i + 1) * P],
                                   wo_s[:, k, dq * TQ:(dq + 1) * TQ],
                                   start=(k == 0), stop=(k == KC - 1))
                return mmi

            r_ev = pipe(pr, "h2r",
                        [(i, dq) for i in range(NT) for dq in range(2)],
                        [p_mm0, p_mm1], r_prod,
                        lambda e, job, pm, sl: e.tensor_copy(
                            res_s[:, job[0], job[1] * TQ:(job[1] + 1) * TQ],
                            pm[:]),
                        t_entry=[(mm, mm.v)], consumer_engine="vector")

            ln_o = emit_ln(pr, lambda i: res_s[:, i, :], list(range(NT)),
                           cols, 1e-5, entry=[(r_ev, r_ev.v)], apply=False)

            @block.gpsimd
            def _(g):
                base = evV.v
                for i in range(NT):
                    sl = i % 2
                    if i >= 2:
                        g.wait_ge(evV.h, base + (i - 1))
                    g.wait_ge(x2l[sl].h, x2l[sl].v)
                    x2l[sl].inc(g.dma_start(
                        out=x2_t[:, sl, :],
                        in_=a["x2_d"][i * P:(i + 1) * P, :]), 16)

            for i in range(NT):
                @block.scalar
                def _(s, i=i):
                    s.wait_ge(ln_o.h, ln_o.v)
                    if i >= 2:
                        s.wait_ge(evV.h, evV.v)
                    evS.inc(s.activation(xh_t[:, i % 2, :], res_s[:, i, :],
                                         AF.Identity,
                                         bias=cols["nmr"][:, i:i + 1],
                                         scale=cols["rstd"][:, i:i + 1]))

                @block.vector
                def _(v, i=i):
                    v.wait_ge(evS.h, evS.v)
                    v.wait_ge(x2l[i % 2].h, x2l_base[i % 2] + 16 * (i // 2 + 1))
                    sl = i % 2
                    v.wait_ge(dms[sl].h, dms[sl].v)
                    evV.inc(v.tensor_add(stg8[:, sl, :], xh_t[:, i % 2, :],
                                         x2_t[:, i % 2, :]))

                @block.sync
                def _(sy, i=i):
                    sy.wait_ge(evV.h, evV.v)
                    sl = i % 2
                    sy.wait_ge(dms[sl].h, dms[sl].v)
                    dms[sl].inc(sy.dma_start(
                        out=a["x3_d"][i * P:(i + 1) * P, :],
                        in_=stg8[:, sl, :]), 16)

            emit_barrier(pr)
        stOG.close()


def emit_l1ffn_fused(pr, nc, a):
    block = pr.block
    ld = pr.sem("ld")
    cst = pr.sem("cst")
    evV = pr.sem("ev_vector")
    evS = pr.sem("ev_scalar")
    mm = pr.sem("mm")
    wld = pr.sem("wld")
    dms = [pr.sem(f"dm{i}") for i in range(4)]

    with contextlib.ExitStack() as st0:
        sb0 = lambda *aa: st0.enter_context(nc.sbuf_tensor(*aa))
        ps0 = lambda *aa: st0.enter_context(nc.psum_tensor(*aa))
        identf = sb0("h3_idf", [P, P], F32)
        identb = sb0("h3_idb", [P, P], BF16)
        lnT_s = sb0("h3_lnT", [P, KC, TH], BF16)
        w_s = sb0("h3_w", [P, KC, D], BF16)
        mu_s = sb0("h3_mu", [P, KC, 2], F32)
        r_s = sb0("h3_r", [P, NT, D], BF16)
        kT_s = sb0("h3_kT", [P, KC, T], BF16)
        cols = alloc_ln_cols(sb0, nt=NTH, tag="h3")
        p_mm0 = ps0("h3_pm0", [P, TQ], F32)
        p_mm1 = ps0("h3_pm1", [P, TQ], F32)
        st1 = contextlib.ExitStack()
        x_s = st1.enter_context(nc.sbuf_tensor("h3_x", [P, NTH, D], F32))
        hs_buf = st1.enter_context(nc.sbuf_tensor("h3_hs", [1, 5, D], F32))
        p_tr0 = st1.enter_context(nc.psum_tensor("h3_pt0", [P, P], F32))
        p_tr1 = st1.enter_context(nc.psum_tensor("h3_pt1", [P, P], F32))

        @block.gpsimd
        def _(g):
            g.wait_ge(ld.h, ld.v)
            ld.inc(g.dma_start(out=x_s[:, 1:NTH, :],
                               in_=a["x3_d"].rearrange("(n p) d -> p n d", p=P)), 16)
            ld.inc(g.dma_start(out=identf[:], in_=a["identf_d"]), 16)
            ld.inc(g.dma_start(out=identb[:], in_=a["identf_d"]), 16)
            with nc.allow_non_contiguous_dma(reason="tiny mu columns"):
                ld.inc(g.dma_start(out=mu_s[:, :, 0:1],
                                   in_=a["mur_d"].rearrange("a (k p) -> p k a", p=P)), 16)
                ld.inc(g.dma_start(out=mu_s[:, :, 1:2],
                                   in_=a["muk_d"].rearrange("a (k p) -> p k a", p=P)), 16)

        @block.vector
        def _(v):
            cst.inc(v.memset(x_s[:, 0, :], 0.0))

        hdm = emit_halo(pr, nc, a, a["x3_d"], a["agh1in"], a["agh1out"],
                        1, hs_buf, x_s, "h3")

        ln1 = emit_ln(pr, lambda i: x_s[:, i, :], list(range(NTH)), cols,
                      1e-5, entry=[(ld, ld.v), (cst, cst.v), (hdm, hdm.v)])

        tr_ev = pipe(
            pr, "h3tr", [(i, k) for i in range(NTH) for k in range(KC)],
            [p_tr0, p_tr1],
            lambda t, job, pm: t.transpose(
                pm[:], x_s[:, job[0], job[1] * P:(job[1] + 1) * P],
                identf[:]),
            lambda e, job, pm, sl: e.tensor_copy(
                lnT_s[:, job[1], job[0] * P:(job[0] + 1) * P], pm[:]),
            t_entry=[(ln1, ln1.v), (ld, ld.v)], consumer_engine="vector")

        emit_barrier(pr)
        st1.close()
        st2 = contextlib.ExitStack()
        mixT_s = st2.enter_context(nc.sbuf_tensor("h3_mixT", [P, KC, T], BF16))
        kb_s = st2.enter_context(nc.sbuf_tensor("h3_kb", [P, NT, D], BF16))
        p_trb = st2.enter_context(nc.psum_tensor("h3_ptb", [P, P], BF16))

        for si, wnm in ((0, "WrT"), (1, "WkmT")):
            @block.gpsimd
            def _(g, wnm=wnm):
                g.wait_ge(wld.h, wld.v)
                g.wait_ge(mm.h, mm.v)
                wld.inc(g.dma_start(
                    out=w_s[:],
                    in_=a["w1T"][wnm].rearrange("(k p) d -> p k d", p=P)), 16)

            @block.vector
            def _(v, si=si, trv=tr_ev.v):
                v.wait_ge(tr_ev.h, trv)
                v.wait_ge(mm.h, mm.v)
                for k in range(KC):
                    v.wait_ge(evV.h, evV.v)
                    evV.inc(v.tensor_sub(mixT_s[:, k, :],
                                         lnT_s[:, k, P:TH],
                                         lnT_s[:, k, P - 1:TH - 1]))
                    v.wait_ge(evV.h, evV.v)
                    evV.inc(v.scalar_tensor_tensor(
                        mixT_s[:, k, :], mixT_s[:, k, :],
                        mu_s[:, k, si:si + 1], lnT_s[:, k, P - 1:TH - 1],
                        ALU.mult, ALU.add))

            def mmo_prod(t, job, pm):
                i, dq = job
                t.wait_ge(wld.h, wld.v)
                for k in range(KC):
                    mmi = t.matmul(pm[:], mixT_s[:, k, i * P:(i + 1) * P],
                                   w_s[:, k, dq * TQ:(dq + 1) * TQ],
                                   start=(k == 0), stop=(k == KC - 1))
                return mmi

            def mmo_cons(e, job, pm, sl, si=si):
                i, dq = job
                if si == 0:
                    return e.activation(r_s[:, i, dq * TQ:(dq + 1) * TQ],
                                        pm[:], AF.Sigmoid)
                return e.activation(kb_s[:, i, dq * TQ:(dq + 1) * TQ],
                                    pm[:], AF.Relu)

            pipe(pr, f"h3mmo{si}",
                 [(i, dq) for i in range(NT) for dq in range(2)],
                 [p_mm0, p_mm1], mmo_prod, mmo_cons,
                 t_entry=[(evV, evV.v)])

        @block.scalar
        def _(s):
            for i in range(NT):
                s.wait_ge(evS.h, evS.v)
                evS.inc(s.activation(kb_s[:, i, :], kb_s[:, i, :], AF.Square))

        kt_ev = pipe(
            pr, "h3kt", [(i, k) for i in range(NT) for k in range(KC)],
            [p_trb],
            lambda t, job, pm: t.transpose(
                pm[:], kb_s[:, job[0], job[1] * P:(job[1] + 1) * P],
                identb[:]),
            lambda e, job, pm, sl: e.tensor_copy(
                kT_s[:, job[1], job[0] * P:(job[0] + 1) * P], pm[:]),
            t_entry=[(evS, evS.v)], consumer_engine="vector")

        emit_barrier(pr)
        st2.close()
        st3 = contextlib.ExitStack()
        res_s = st3.enter_context(nc.sbuf_tensor("h3_res", [P, NT, D], F32))
        x3_t = st3.enter_context(nc.sbuf_tensor("h3_x3t", [P, 2, D], F32))
        xh_t = st3.enter_context(nc.sbuf_tensor("h3_xht", [P, 2, D], F32))
        stage = st3.enter_context(nc.sbuf_tensor("h3_stg", [P, 2, D], F32))
        x3l = [pr.sem("xl0"), pr.sem("xl1")]
        x3l_base = [s_.v for s_ in x3l]

        @block.gpsimd
        def _(g):
            g.wait_ge(wld.h, wld.v)
            g.wait_ge(mm.h, mm.v)
            wld.inc(g.dma_start(
                out=w_s[:],
                in_=a["w1T"]["WvmT"].rearrange("(k p) d -> p k d", p=P)), 16)

        def v_prod(t, job, pm):
            i, dq = job
            t.wait_ge(wld.h, wld.v)
            for k in range(KC):
                mmi = t.matmul(pm[:], kT_s[:, k, i * P:(i + 1) * P],
                               w_s[:, k, dq * TQ:(dq + 1) * TQ],
                               start=(k == 0), stop=(k == KC - 1))
            return mmi

        r_ev = pipe(pr, "h3v", [(i, dq) for i in range(NT) for dq in range(2)],
                    [p_mm0, p_mm1], v_prod,
                    lambda e, job, pm, sl: e.tensor_tensor(
                        res_s[:, job[0], job[1] * TQ:(job[1] + 1) * TQ],
                        pm[:], r_s[:, job[0], job[1] * TQ:(job[1] + 1) * TQ],
                        ALU.mult),
                    t_entry=[(kt_ev, kt_ev.v)], consumer_engine="vector")

        ln_o = emit_ln(pr, lambda i: res_s[:, i, :], list(range(NT)),
                       cols, 1e-5, entry=[(r_ev, r_ev.v)], apply=False)

        @block.gpsimd
        def _(g):
            base = evV.v
            for i in range(NT):
                sl = i % 2
                if i >= 2:
                    g.wait_ge(evV.h, base + (i - 1))
                g.wait_ge(x3l[sl].h, x3l[sl].v)
                x3l[sl].inc(g.dma_start(out=x3_t[:, sl, :],
                                        in_=a["x3_d"][i * P:(i + 1) * P, :]), 16)

        for i in range(NT):
            @block.scalar
            def _(s, i=i):
                s.wait_ge(ln_o.h, ln_o.v)
                if i >= 2:
                    s.wait_ge(evV.h, evV.v)
                evS.inc(s.activation(xh_t[:, i % 2, :], res_s[:, i, :],
                                     AF.Identity,
                                     bias=cols["nmr"][:, i:i + 1],
                                     scale=cols["rstd"][:, i:i + 1]))

            @block.vector
            def _(v, i=i):
                v.wait_ge(evS.h, evS.v)
                v.wait_ge(x3l[i % 2].h, x3l_base[i % 2] + 16 * (i // 2 + 1))
                sl = i % 2
                v.wait_ge(dms[sl].h, dms[sl].v)
                evV.inc(v.tensor_add(stage[:, sl, :], xh_t[:, i % 2, :],
                                     x3_t[:, i % 2, :]))

            @block.sync
            def _(sy, i=i):
                sy.wait_ge(evV.h, evV.v)
                sl = i % 2
                sy.wait_ge(dms[sl].h, dms[sl].v)
                dms[sl].inc(sy.dma_start(out=a["x4_d"][i * P:(i + 1) * P, :],
                                         in_=stage[:, sl, :]), 16)

        emit_barrier(pr)
        st3.close()



_NC_CACHE = {}


def kernel(**ins):
    NB, S = 2, 8192
    NSB = S // T
    cores = list(range(NCORE))
    cid = [(b, sb_) for b in range(NB) for sb_ in range(NSB)]
    csts = _consts_core()

    if "fused" not in _NC_CACHE:
        _NC_CACHE["fused"] = build_fused()
    nc = _NC_CACHE["fused"]

    wstage = {f: fff_host_weights(
        ins[f"L0_ffn_{f}_nw"], ins[f"L0_ffn_{f}_nb"], ins[f"L0_ffn_{f}_w1"],
        ins[f"L0_ffn_{f}_b1"], ins[f"L0_ffn_{f}_w2"], ins[f"L0_ffn_{f}_b2"])
        for f in "qkvf"}

    shared = {
        "q_w": ins["L0_attn_q_w"], "k_w": ins["L0_attn_k_w"],
        "v_w": ins["L0_attn_v_w"], "out_w": ins["L0_attn_out_w"],
        "q_b": ins["L0_attn_q_b"][None], "k_b": ins["L0_attn_k_b"][None],
        "v_b": ins["L0_attn_v_b"][None], "out_b": ins["L0_attn_out_b"][None],
        "gamma": ins["L0_attn_gamma"][None], "beta": ins["L0_attn_beta"][None],
        "mu_r": ins["L1_ffn_mu_r"][None], "mu_k": ins["L1_ffn_mu_k"][None],
        "proj": ins["L0_attn_proj"],
        "WqT": np.ascontiguousarray(ins["L1_attn_Wq"].T),
        "WkT": np.ascontiguousarray(ins["L1_attn_Wk"].T),
        "WvT": np.ascontiguousarray(ins["L1_attn_Wv"].T),
        "WgT": np.ascontiguousarray(ins["L1_attn_Wg"].T),
        "WoT": np.ascontiguousarray(ins["L1_attn_Wo"].T),
        "WrT": np.ascontiguousarray(ins["L1_ffn_W_r"].T),
        "WkmT": np.ascontiguousarray(ins["L1_ffn_W_k"].T),
        "WvmT": np.ascontiguousarray(ins["L1_ffn_W_v"].T),
        "conv_q": ins["L1_attn_conv_q"], "conv_k": ins["L1_attn_conv_k"],
        "conv_v": ins["L1_attn_conv_v"],
        "W0h": np.ascontiguousarray(
            ins["L1_attn_W_init"][0].reshape(H * DH, DH)),
        "lng": ins["L1_attn_ln_gamma"][None],
        "lnb": ins["L1_attn_ln_beta"][None],
        "mask_ut": csts["mask_ut"],
    }
    for nm in fff_consts():
        shared[nm] = csts[nm]
    for f in "qkvf":
        nwT, w12, b2 = wstage[f]
        shared[f"{f}_nwT"] = nwT
        shared[f"{f}_w12aug"] = w12
        shared[f"{f}_b2"] = b2

    x_full = np.ascontiguousarray(ins["L0_attn_x"])
    im = []
    for c, (b, sb_) in enumerate(cid):
        d = dict(shared)
        d["x"] = np.ascontiguousarray(x_full[b, sb_ * T:(sb_ + 1) * T])
        pm = np.zeros((P, 4), np.float32)
        pm[:, :sb_] = 1.0
        d["pmask"] = pm
        hm = np.zeros((P, 4), np.float32)
        if sb_ > 0:
            hm[:, sb_ - 1] = 1.0
        d["hmask"] = hm
        im.append(d)

    res = run_bass_kernel_spmd(nc, im, core_ids=cores).results
    out = np.empty((NB, S, D), np.float32)
    for c, (b, sb_) in enumerate(cid):
        out[b, sb_ * T:(sb_ + 1) * T] = res[c]["x4"]
    return out
